# version 26
# baseline (speedup 1.0000x reference)
import numpy as np
import ml_dtypes
import concourse.bass as bass
import concourse.mybir as mybir
from concourse.bass_utils import run_bass_kernel_spmd

F32 = mybir.dt.float32
F16 = mybir.dt.float16
BF16 = mybir.dt.bfloat16
U8 = mybir.dt.uint8
I32 = mybir.dt.int32
AF = mybir.ActivationFunctionType
ALU = mybir.AluOpType
AX = mybir.AxisListType

NCORES = 8
BATCH, SEQ, D = 32, 2048, 1024
NH, HD = 16, 64
NSEQ = BATCH // NCORES
TPC = NSEQ * SEQ
NT = SEQ // 128
NCHK = 4
CH = 512
NCH_B = TPC // CH
LIMIT_BCHUNKS = None
LIMIT_ASEQS = None
NE = 8
EPS = 1e-5

ENGINES = ("sync", "tensor", "vector", "scalar", "gpsimd")
SEMKEY = {"sync": "sy", "tensor": "te", "vector": "ve", "scalar": "sc",
          "gpsimd": "gp"}
INORDER_SKIP = ("tensor",)


class Tok:
    __slots__ = ("sem", "v", "eng", "seq")

    def __init__(self, sem, v, eng, seq):
        self.sem, self.v, self.eng, self.seq = sem, v, eng, seq


class Buf:
    __slots__ = ("name", "w", "r")

    def __init__(self, name):
        self.name = name
        self.w = None
        self.r = []


class Prog:
    def __init__(self, nc, sems):
        self.nc = nc
        self.sems = sems
        self.count = {n: 0 for n in sems}
        self.q = {e: [] for e in ENGINES}
        self.nops = {e: 0 for e in ENGINES}
        self.gseq = 0
        self.last_real = {e: None for e in ENGINES}

    def fence(self):
        toks = []
        for e in ENGINES:
            lr = self.last_real[e]
            if lr is not None:
                toks.append(lr)
        for s in self.sems:
            if s.startswith("d") and self.count[s]:
                toks.append(Tok(s, self.count[s], None, self.gseq))
        return toks

    def op(self, eng, fn, reads=(), writes=(), waits=(), inc=True, dma=False):
        self.gseq += 1
        seq = self.gseq
        deps = list(waits)
        for b in reads:
            if b.w is not None:
                deps.append(b.w)
        for b in writes:
            if b.w is not None:
                deps.append(b.w)
            deps.extend(b.r)
        best_real = {}
        for t in deps:
            if t is not None and t.sem is not None and not t.sem.startswith("d") \
                    and t.eng is not None:
                if t.eng not in best_real or best_real[t.eng].seq < t.seq:
                    best_real[t.eng] = t
        wd = {}
        for t in deps:
            if t is None:
                continue
            if t.sem is None:
                br = best_real.get(t.eng)
                if br is not None and br.seq >= t.seq:
                    continue
                if t.eng == eng and eng in INORDER_SKIP and not dma:
                    continue
                sub = self.last_real[t.eng]
                if sub is None or sub.seq < t.seq:
                    raise RuntimeError(
                        f"unresolvable dep on {t.eng} (add inc=True there)")
                t = sub
            if (t.eng == eng and eng in INORDER_SKIP and not dma
                    and not t.sem.startswith("d")):
                continue
            if t.v > self.count[t.sem]:
                raise RuntimeError(f"forward wait {t.sem}={t.v}")
            wd[t.sem] = max(wd.get(t.sem, 0), t.v)
        if dma:
            semn = dma
            self.count[semn] += 16
            tok = Tok(semn, self.count[semn], eng, seq)
        elif inc:
            semn = SEMKEY[eng]
            self.count[semn] += 1
            tok = Tok(semn, self.count[semn], eng, seq)
        else:
            semn, tok = None, Tok(None, 0, eng, seq)
        if semn is not None:
            self.last_real[eng] = tok
        self.q[eng].append(
            (wd, fn, (self.sems[semn], 16 if dma else 1) if semn else None))
        self.nops[eng] += 1
        for b in reads:
            b.r.append(tok)
        for b in writes:
            b.w = tok
            b.r = []
        return tok

    def dma(self, eng, sem, out, in_, reads=(), writes=(), waits=(), **kw):
        return self.op(eng, lambda e: e.dma_start(out=out, in_=in_, **kw),
                       reads=reads, writes=writes, waits=waits, dma=sem)

    def emit(self, block):
        def section(engname):
            def f(eng):
                last = {}
                for wd, fn, incv in self.q[engname]:
                    for s, v in sorted(wd.items()):
                        if last.get(s, -1) < v:
                            eng.wait_ge(self.sems[s], v)
                            last[s] = v
                    inst = fn(eng)
                    if incv is not None:
                        inst.then_inc(incv[0], incv[1])
            return f
        block.sync(section("sync"))
        block.tensor(section("tensor"))
        block.vector(section("vector"))
        block.scalar(section("scalar"))
        block.gpsimd(section("gpsimd"))


class Arena:
    def __init__(self, t, nbytes):
        self.t = t
        self.nbytes = nbytes
        self.off = 0
        self.hi = 0

    def alloc(self, shape, dt, parts=128):
        esz = {F32: 4, F16: 2, BF16: 2, U8: 1, I32: 4}[dt]
        n = int(np.prod(shape)) * esz
        off = (self.off + 63) & ~63
        self.off = off + n
        self.hi = max(self.hi, self.off)
        assert self.off <= self.nbytes, f"arena overflow {self.off}"
        ap = self.t[0:parts, off:off + n].bitcast(dt)
        if len(shape) > 1:
            pat = " ".join(f"a{i}" for i in range(len(shape)))
            ap = ap.rearrange(f"p ({pat}) -> p {pat}",
                              **{f"a{i}": shape[i] for i in range(len(shape))})
        return ap

    def mark(self):
        return self.off

    def reset(self, off):
        self.off = off


def ds(i, n=128):
    return slice(i * n, (i + 1) * n)


def build_nc(mode="full"):
    nc = bass.Bass("TRN2", target_bir_lowering=False)

    x_d = nc.dram_tensor("x", [NSEQ, SEQ, D], F16, kind="ExternalInput").ap()
    att_w = {n: nc.dram_tensor(n, [D, D], F16, kind="ExternalInput").ap()
             for n in ("qwp", "kwp", "vw", "ow")}
    f32in = {}
    for n, sh in [("lin0_w", [D, D]), ("lin0_b", [128, 8]),
                  ("s1w1", [D, D]), ("s1b1", [128, 8]), ("s1w2", [D, D]),
                  ("s1b2", [128, 8]),
                  ("gateT", [D, NE]),
                  ("ew1", [NE, D, D]), ("ew3", [NE, D, D]), ("ew2", [NE, D, D]),
                  ("shw1", [D, 2 * D]), ("shw3", [D, 2 * D]), ("shw2", [2 * D, D]),
                  ("lin1_w", [D, D]), ("lin1_b", [128, 8]),
                  ("s2w1", [D, D]), ("s2b1", [128, 8]), ("s2w2", [D, D]),
                  ("s2b2", [128, 8])]:
        f32in[n] = nc.dram_tensor(n, sh, F32, kind="ExternalInput").ap()
    consts_d = {}
    for n, sh, dt in [("cosq", [128, NT, 32], F16), ("sinq", [128, NT, 32], F16),
                      ("cosk", [128, NT, 32], F16), ("sink", [128, NT, 32], F16),
                      ("tri", [128, 128], F16), ("dmm", [128, 896], F16),
                      ("idb", [128, 128], BF16),
                      ("idh", [128, 128], F16),
                      ("id32", [128, 128], F32), ("hsel", [128, NE, NH], BF16)]:
        consts_d[n] = nc.dram_tensor(n, sh, dt, kind="ExternalInput").ap()

    x1s_kind = {"full": "Internal", "A": "ExternalOutput", "B": "ExternalInput"}[mode]
    x1s = nc.dram_tensor("x1s", [TPC, D], F32, kind=x1s_kind).ap()
    out_d = nc.dram_tensor("out", [TPC, D], F32, kind="ExternalOutput").ap()

    wb = {}
    for n, sh in [("b_lin0", [D, D]), ("b_s1w1", [D, D]), ("b_s1w2", [D, D]),
                  ("b_gateT", [D, NE]),
                  ("b_ew1", [NE, D, D]), ("b_ew3", [NE, D, D]), ("b_ew2", [NE, D, D]),
                  ("b_shw1", [D, 2 * D]), ("b_shw3", [D, 2 * D]), ("b_shw2", [2 * D, D]),
                  ("b_lin1", [D, D]), ("b_s2w1", [D, D]), ("b_s2w2", [D, D])]:
        wb[n] = nc.dram_tensor(n, sh, BF16, kind="Internal").ap()
    ccs = nc.dram_tensor("ccs", [NE, CH], BF16, kind="Internal").ap()

    ctxs = []

    def enter(cm):
        ctxs.append(cm)
        return cm.__enter__()

    ARENA_BYTES = 198 * 1024
    arena_t = enter(nc.sbuf_tensor("arena", [128, ARENA_BYTES], U8))
    ar = Arena(arena_t, ARENA_BYTES)

    sbc = {}
    for n in ("cosq", "sinq", "cosk", "sink"):
        sbc[n] = enter(nc.sbuf_tensor(n + "_s", [128, NT, 32], F16))
    sbc["tri"] = enter(nc.sbuf_tensor("tri_s", [128, 128], F16))
    sbc["dmm"] = ar.alloc([896], F16)
    sbc["idb"] = enter(nc.sbuf_tensor("idb_s", [128, 128], BF16))
    sbc["idh"] = enter(nc.sbuf_tensor("idh_s", [128, 128], F16))
    sbc["id32"] = enter(nc.sbuf_tensor("id32_s", [128, 128], F32))
    sbc["hsel"] = enter(nc.sbuf_tensor("hsel_s", [128, NE, NH], BF16))
    biases = {n: enter(nc.sbuf_tensor(n + "_s", [128, 8], F32))
              for n in ("lin0_b", "s1b1", "s1b2", "lin1_b", "s2b1", "s2b2")}
    ggT = enter(nc.sbuf_tensor("ggT", [128, 8, NE], BF16))
    neg1 = enter(nc.sbuf_tensor("neg1", [128, 4 * NE], F32))
    cEPS = enter(nc.sbuf_tensor("cEPS", [128, 1], F32))
    cTINY = enter(nc.sbuf_tensor("cTINY", [128, 1], F32))
    ones128 = enter(nc.sbuf_tensor("ones128", [1, 128], F32))

    pS1 = enter(nc.psum_tensor("pS1", [128, 2048], F32))
    pS2 = enter(nc.psum_tensor("pS2", [128, 1024], F32))
    pAVX = enter(nc.psum_tensor("pAVX", [128, 512], F32))
    pTR = enter(nc.psum_tensor("pTR", [128, 1024], F16))
    pTRbf = pTR[:].bitcast(BF16)

    semnames = ("dC", "dGG", "dP0i", "dP0o", "dP1i", "dP1o", "dP2i", "dP2o",
                "dWU0", "dWU1", "dWU2", "dWA", "dWB",
                "dXI0", "dXI1", "dXR", "dXR1", "dX1S", "dX1S2", "dRDS", "dDEN",
                "dRD2",
                "dRU0", "dRU1", "dBB", "dCC", "dXT0", "dXT1", "dOUT",
                "sy", "te", "ve", "sc", "gp")
    smap = {n: enter(nc.semaphore(n)) for n in semnames}
    P = Prog(nc, smap)

    BD = {}

    def gbuf(n):
        if n not in BD:
            BD[n] = Buf(n)
        return BD[n]

    _prevc = []
    for n in ("cosq", "sinq", "cosk", "sink", "tri", "dmm", "idb",
              "idh", "id32", "hsel"):
        dst = sbc[n] if n == "dmm" else sbc[n][:]
        t = P.dma("sync", "dC", dst, consts_d[n],
                  writes=[gbuf("c_" + n)], waits=tuple(_prevc))
        _prevc = [t]
    for n in biases:
        t = P.dma("sync", "dC", biases[n][:], f32in[n],
                  writes=[gbuf("c_" + n)], waits=tuple(_prevc))
        _prevc = [t]
    P.op("vector", lambda e: e.memset(neg1[:], -1.0), writes=[gbuf("c_neg1")])
    P.op("vector", lambda e: e.memset(cEPS[:], EPS), writes=[gbuf("c_eps")])
    P.op("vector", lambda e: e.memset(cTINY[:], 1e-35), writes=[gbuf("c_tiny")])
    P.op("vector", lambda e: e.memset(ones128[:], 1.0), writes=[gbuf("c_ones")])
    cb = {n: gbuf("c_" + n) for n in
          ("cosq", "sinq", "cosk", "sink", "tri", "dmm", "idb",
           "idh", "id32", "hsel", "neg1",
           "lin0_b", "s1b1", "s1b2", "lin1_b", "s2b1", "s2b2")}
    cb["eps"] = gbuf("c_eps")
    cb["tiny"] = gbuf("c_tiny")
    cb["ones"] = gbuf("c_ones")
    bpinit = gbuf("psum_init")
    P.op("vector", lambda e: e.memset(pS1[:], 0.0), writes=[bpinit])
    P.op("vector", lambda e: e.memset(pS2[:], 0.0), reads=[bpinit])
    P.op("vector", lambda e: e.memset(pAVX[:], 0.0), writes=[bpinit])

    units0 = []
    if mode != "A":
        def conv_mat(src, dst, name):
            R, C = src.shape
            for r0 in range(0, R, 128):
                for c0 in range(0, C, 512):
                    cw = min(512, C - c0)
                    units0.append((src[r0:r0 + 128, c0:c0 + cw],
                                   dst[r0:r0 + 128, c0:c0 + cw], name, cw))

        conv_mat(f32in["lin0_w"], wb["b_lin0"], "b_lin0")
        conv_mat(f32in["s1w1"], wb["b_s1w1"], "b_s1w1")
        conv_mat(f32in["s1w2"], wb["b_s1w2"], "b_s1w2")
        conv_mat(f32in["gateT"], wb["b_gateT"], "b_gateT")
        for ei in range(NE):
            conv_mat(f32in["ew1"][ei], wb["b_ew1"][ei], "b_ew1")
            conv_mat(f32in["ew3"][ei], wb["b_ew3"][ei], "b_ew3")
            conv_mat(f32in["ew2"][ei], wb["b_ew2"][ei], "b_ew2")
        conv_mat(f32in["shw1"], wb["b_shw1"], "b_shw1")
        conv_mat(f32in["shw3"], wb["b_shw3"], "b_shw3")
        conv_mat(f32in["shw2"], wb["b_shw2"], "b_shw2")
        conv_mat(f32in["lin1_w"], wb["b_lin1"], "b_lin1")
        conv_mat(f32in["s2w1"], wb["b_s2w1"], "b_s2w1")
        conv_mat(f32in["s2w2"], wb["b_s2w2"], "b_s2w2")

    w0in = [ar.alloc([512], F32) for _ in range(3)]
    w0out = [ar.alloc([512], BF16) for _ in range(3)]
    b_w0in = [Buf(f"w0in{i}") for i in range(3)]
    b_w0out = [Buf(f"w0out{i}") for i in range(3)]
    u0 = [0]

    def drip_phase0(k):
        for _ in range(k):
            i = u0[0]
            if i >= len(units0):
                return
            u0[0] += 1
            srcap, dstap, dname, C = units0[i]
            p = i % 3
            iv = w0in[p][:, 0:C]
            ov = w0out[p][:, 0:C]
            P.dma("sync", ("dP0i", "dP1i", "dP2i")[p], iv, srcap,
                  writes=[b_w0in[p]])
            P.op("gpsimd",
                 (lambda a, bq: lambda e: e.tensor_copy(bq, a))(iv, ov),
                 reads=[b_w0in[p]], writes=[b_w0out[p]])
            P.dma("sync", ("dP0o", "dP1o", "dP2o")[p], dstap, ov,
                  reads=[b_w0out[p]], writes=[gbuf(dname)])

    markA = ar.mark()
    if mode != "B":
        xT = ar.alloc([8, 2048], F16)
        KrT = ar.alloc([8, 2048], F16)
        Vaug = ar.alloc([NT, NH * 65], F16)
        QrT = ar.alloc([8, 512], F16)
        PTb = ar.alloc([NT, 512], F16)
        mergedT = ar.alloc([8, 512], F16)

        watt = ar.alloc([8, 1024], F16)
        wattB = ar.alloc([8, 1024], F16)
        xin = [ar.alloc([1024], F16)]
        qkrope = [ar.alloc([1024], F16) for _ in range(2)]
        rtmpA = [ar.alloc([512], F16) for _ in range(2)]
        rtmpB = [ar.alloc([512], F16) for _ in range(2)]
        xres = ar.alloc([2, 1024], F16)
        zb = [ar.alloc([1024], F32), None, None, None]
        sqQT = [ar.alloc([8, 128], F16)]
        rdenS = ar.alloc([1024], F16, parts=1)
        ones64 = ar.alloc([64], F16, parts=1)
        kn2 = ar.alloc([1], F32, parts=16)
        qn2 = ar.alloc([1], F32, parts=16)
        qk2 = ar.alloc([1], F32, parts=16)
        btmp = ar.alloc([1], F32, parts=16)
        bnegrow = ar.alloc([16], F32, parts=1)
        negb = ar.alloc([16], F32)
        mx16 = ar.alloc([1], F32, parts=16)
        lnrow = [[ar.alloc([1], F32) for _ in range(4)] for _ in range(4)]

        bxT, bKrT, bVaug, bQrT, bPT, bmg = (Buf(n) for n in
            ("xT", "KrT", "Vaug", "QrT", "PT", "mergedT"))

        bwatt = Buf("watt")
        bwattB = Buf("wattB")
        bxin = [Buf("xin0")]
        bqkrope = [Buf("qkrope0"), Buf("qkrope1")]
        brtmpA = [Buf("rtmpA0"), Buf("rtmpA1")]
        brtmpB = [Buf("rtmpB0"), Buf("rtmpB1")]
        bxres = [Buf("xres0"), Buf("xres1")]
        bzb = [Buf(f"zb{i}") for i in range(4)]
        bsqQT = [Buf("sqQT0")]
        brdS = [Buf("rdS0"), Buf("rdS1")]
        bones64 = Buf("ones64")
        bkn2, bqn2, bqk2, bbtmp = Buf("kn2"), Buf("qn2"), Buf("qk2"), Buf("btmp")
        bbnegrow, bnegb, bmx16 = Buf("bnegrow"), Buf("negb"), Buf("mx16")
        blnrow = [[Buf(f"lnrow{q}{i}") for i in range(4)] for q in range(4)]
        bqA, bqB, bqC = Buf("AqA"), Buf("AqB"), Buf("AqC")
        bpAVXb, bpTRb = Buf("ApAVX"), Buf("ApTR")
        for _bq in (bqA, bqB, bqC, bpAVXb):
            _bq.w = gbuf("psum_init").w

        pTR3 = pTR[:].rearrange("p (a b) -> p a b", b=128)
        pTRf32 = pTR[:].bitcast(F32)
        P.op("vector", lambda e: e.memset(ones64[:], 1.0), writes=[bones64])
        zb[1] = PTb[:, 2:6, :].rearrange("p a b -> p (a b)").bitcast(F32)
        zb[2] = PTb[:, 6:10, :].rearrange("p a b -> p (a b)").bitcast(F32)
        zb[3] = PTb[:, 10:14, :].rearrange("p a b -> p (a b)").bitcast(F32)

        def load_watt(name, dst=None, dbuf=None, sem="dWA"):
            return P.dma("scalar", sem, (dst if dst is not None else watt)[:, :, :],
                         att_w[name].rearrange("(k p) f -> p k f", p=128),
                         writes=[dbuf if dbuf is not None else bwatt])

        def rope6(dst, psrc, ct, st, bpsrc, bdst, btabs, par):
            E = psrc.rearrange("p (h j) -> p h j", j=64)[:, :, 0:32]
            O = psrc.rearrange("p (h j) -> p h j", j=64)[:, :, 32:64]
            dE = dst.rearrange("p (h j) -> p h j", j=64)[:, :, 0:32]
            dO = dst.rearrange("p (h j) -> p h j", j=64)[:, :, 32:64]
            ctb = ct.unsqueeze(1).to_broadcast((128, 16, 32))
            stb = st.unsqueeze(1).to_broadcast((128, 16, 32))
            tA = rtmpA[par][:].rearrange("p (h j) -> p h j", j=32)
            tB = rtmpB[par][:].rearrange("p (h j) -> p h j", j=32)
            brA, brB = brtmpA[par], brtmpB[par]
            P.op("vector", lambda e: e.tensor_tensor(tA, E, ctb, op=ALU.mult),
                 reads=[bpsrc] + btabs, writes=[brA])
            P.op("vector", lambda e: e.tensor_tensor(tB, O, stb, op=ALU.mult),
                 reads=[bpsrc], writes=[brB])
            P.op("vector", lambda e: e.tensor_tensor(dE, tA, tB, op=ALU.subtract),
                 reads=[brA, brB], writes=[bdst])
            P.op("vector", lambda e: e.tensor_tensor(tA, E, stb, op=ALU.mult),
                 reads=[bpsrc], writes=[brA])
            P.op("vector", lambda e: e.tensor_tensor(tB, O, ctb, op=ALU.mult),
                 reads=[bpsrc], writes=[brB])
            P.op("vector", lambda e: e.tensor_tensor(dO, tA, tB, op=ALU.add),
                 reads=[brA, brB], writes=[bdst])

        def transp8_evac(srcap, bsrc, dstap, bdst):
            for j in range(8):
                P.op("tensor",
                     (lambda jj: lambda e: e.transpose(
                         pTR[:, ds(jj)], srcap[:, ds(jj)], sbc["idh"][:]))(j),
                     reads=[bsrc, cb["idh"]], writes=([bpTRb] if j == 0 else []),
                     inc=(j == 7))
            P.op("scalar", lambda e: e.copy(dstap, pTR3),
                 reads=[bpTRb], writes=[bdst])

        genPP = None

        def gen_proj(t, w_ap, bw, pr):
            dst, dbuf = genPP[pr]
            for n in range(2):
                for k in range(8):
                    P.op("tensor",
                         (lambda nn, kk, tt, dd: lambda e: e.matmul(
                             dd[:, ds(nn, 512)], xT[:, kk, ds(tt)],
                             w_ap[:, kk, ds(nn, 512)],
                             start=(kk == 0), stop=(kk == 7)))(n, k, t, dst),
                         reads=[bxT, bw],
                         writes=([dbuf] if (n == 0 and k == 0) else []),
                         inc=(n == 1 and k == 7))

        def norm_tile(srcT, bsrcT, t, dstv, bdstv, first):
            P.op("scalar", (lambda tt: lambda e: e.activation(
                sqQT[0][:, 0:8, :], srcT[:, :, ds(tt)], AF.Square))(t),
                reads=[bsrcT], writes=[bsqQT[0]])
            _t = None
            for pt in range(8):
                _t = P.op("tensor", (lambda p_: lambda e: e.matmul(
                    pAVX[0:16, 0:128], sbc["hsel"][:, p_, :],
                    sqQT[0][:, p_, :], start=(p_ == 0), stop=(p_ == 7)))(pt),
                    reads=[bsqQT[0], cb["hsel"]],
                    writes=([bpAVXb] if pt == 0 else []), inc=(pt == 7))
            bpAVXb.w = _t
            P.op("vector", lambda e: e.reduce_max(mx16[:], pAVX[0:16, 0:128],
                                                  axis=AX.X),
                 reads=[bpAVXb], writes=[bmx16])
            if first:
                P.op("vector", (lambda dv: lambda e: e.tensor_copy(
                    dv[:], mx16[:]))(dstv),
                    reads=[bmx16], writes=[bdstv])
            else:
                P.op("vector", (lambda dv: lambda e: e.tensor_tensor(
                    dv[:], dv[:], mx16[:], op=ALU.max))(dstv),
                    reads=[bmx16, bdstv], writes=[bdstv])

        genPP = [(pS2[:], bqA), (pS1[:, 0:1024], bqB)]
        SCB = [(pS2[:, 0:1024], bqA), (pS1[:, 0:1024], bqB),
               (pS1[:, 1024:2048], bqC)]

        deferred_ln = []

        def flush_ln():
            for f in deferred_ln:
                f()
            deferred_ln.clear()

        for b in range(LIMIT_ASEQS or NSEQ):
            load_watt("kwp")
            load_watt("ow", dst=wattB, dbuf=bwattB, sem="dWB")
            for t in range(NT + 3):
                if 2 <= t < NT + 2:
                    tp0 = t - 2
                    transp8_evac(qkrope[tp0 % 2][:], bqkrope[tp0 % 2],
                                 KrT[:, :, ds(tp0)], bKrT)
                if t >= 3:
                    tn = t - 3
                    norm_tile(KrT, bKrT, tn, kn2, bkn2, tn == 0)
                if t < NT:
                    p = t % 2
                    P.dma("scalar", "dXI0", xin[0][:],
                          x_d[b, ds(t), :], writes=[bxin[0]])
                    transp8_evac(xin[0][:], bxin[0], xT[:, :, ds(t)], bxT)
                    prg = t % 2
                    gen_proj(t, watt, bwatt, prg)
                    rope6(qkrope[t % 2][:], genPP[prg][0], sbc["cosk"][:, t, :],
                          sbc["sink"][:, t, :], genPP[prg][1], bqkrope[t % 2],
                          [cb["cosk"], cb["sink"]], t % 2)
                drip_phase0(6)
            load_watt("vw")
            P.op("vector", lambda e: e.memset(
                Vaug[:, :, :].rearrange("p t (h c) -> p t h c", c=65)
                [:, :, :, 64:65].squeeze(3), 1.0), writes=[bVaug])
            for t in range(NT):
                prg = t % 2
                gen_proj(t, watt, bwatt, prg)
                P.op("vector", (lambda tt, dd: lambda e: e.tensor_copy(
                    Vaug[:, tt, :].rearrange("p (h c) -> p h c", c=65)[:, :, 0:64],
                    dd.rearrange("p (h c) -> p h c", c=64)))(t, genPP[prg][0]),
                    reads=[genPP[prg][1]], writes=[bVaug])
                drip_phase0(6)

            load_watt("qwp")
            for c in range(NCHK):
                for ts4 in range(7):
                    if 2 <= ts4 < 6:
                        tq = ts4 - 2
                        transp8_evac(qkrope[tq % 2][:], bqkrope[tq % 2],
                                     QrT[:, :, ds(tq)], bQrT)
                    if ts4 >= 3:
                        tq = ts4 - 3
                        norm_tile(QrT, bQrT, tq, qn2, bqn2, tq == 0)
                    if ts4 < 4:
                        t = c * 4 + ts4
                        p = ts4 % 2
                        prg = ts4 % 2
                        gen_proj(t, watt, bwatt, prg)
                        rope6(qkrope[ts4 % 2][:], genPP[prg][0],
                              sbc["cosq"][:, t, :],
                              sbc["sinq"][:, t, :], genPP[prg][1],
                              bqkrope[ts4 % 2],
                              [cb["cosq"], cb["sinq"]], ts4 % 2)
                drip_phase0(2)
                P.op("vector", lambda e: e.tensor_tensor(
                    qk2[:], qn2[:], kn2[:], op=ALU.mult),
                    reads=[bqn2, bkn2], writes=[bqk2])
                P.op("scalar", lambda e: e.activation(
                    btmp[:], qk2[:], AF.Ln, bias=cTINY[0:16, :]),
                    reads=[bqk2, cb["tiny"]], writes=[bbtmp])
                P.op("scalar", lambda e: e.activation(
                    btmp[:], btmp[:], AF.Exp, scale=0.5),
                    reads=[bbtmp], writes=[bbtmp])
                P.op("vector", lambda e: e.tensor_scalar(
                    btmp[:], btmp[:], -1.0, None, op0=ALU.mult),
                    reads=[bbtmp], writes=[bbtmp])
                P.op("tensor", lambda e: e.transpose(
                    pS2[0:1, 0:16], btmp[:], sbc["id32"][0:16, 0:16]),
                    reads=[bbtmp, cb["id32"]], writes=[bqA])
                P.op("vector", lambda e: e.tensor_copy(bnegrow[:], pS2[0:1, 0:16]),
                     reads=[bqA], writes=[bbnegrow])
                P.op("tensor", lambda e: e.matmul(
                    pAVX[:, 0:16], ones128[:], bnegrow[:], start=True, stop=True),
                    reads=[bbnegrow, cb["ones"]], writes=[bpAVXb])
                P.op("vector", lambda e: e.tensor_copy(negb[:], pAVX[:, 0:16]),
                     reads=[bpAVXb], writes=[bnegb])

                flush_ln()
                ktiles = 4 * (c + 1)
                groups = [(gi % 3, 2 * gi, 2) for gi in range(ktiles // 2)]

                def emit_fix(hh):
                    uu, rr = hh // 2, (hh % 2) * 64
                    off = (hh % 2) * 512
                    P.op("tensor", (lambda of: lambda e: e.matmul(
                        pTRf32[0:64, 0:512], ones64[0:1, 0:64],
                        rdenS[0:1, of:of + 512], start=True, stop=True))(off),
                        reads=[brdS[hh % 2], bones64], writes=[bpTRb])
                    P.op("vector", (lambda uu_, rr_: lambda e: e.tensor_tensor(
                        mergedT[rr_:rr_ + 64, uu_, :],
                        mergedT[rr_:rr_ + 64, uu_, :],
                        pTRf32[0:64, 0:512], op=ALU.mult))(uu, rr),
                        reads=[bpTRb], writes=[bmg])

                av_last = None
                avtok = {}
                for h in range(NH):
                    u, r0 = h // 2, (h % 2) * 64
                    gtoks = {}

                    def emit_scores_exp(gidx):
                        gsel, gk0, gsz = groups[gidx]
                        pbuf, pB = SCB[gsel]
                        for i in range(gsz):
                            kj = gk0 + i
                            r = kj - 4 * c
                            off = 128 * r if r >= 0 else 0
                            P.op("tensor",
                                 (lambda pb, ii, kk, of, uu, rr: lambda e: e.matmul(
                                     pb[:, ii * 512 + of:(ii + 1) * 512],
                                     KrT[rr:rr + 64, uu, ds(kk)],
                                     QrT[rr:rr + 64, uu, of:512],
                                     start=True, stop=True))(pbuf, i, kj, off, u, r0),
                                 reads=[bKrT, bQrT],
                                 writes=([pB] if i == 0 else []),
                                 inc=(i == gsz - 1))
                        zjs = sorted({(t - 2) // 4 + 1
                                      for t in range(gk0, gk0 + gsz)
                                      if 2 <= t <= 13})
                        avw = [t for t in (avtok.get(gk0 + gsz - 1),)
                               if t is not None]
                        texp = P.op("scalar",
                                    (lambda pb, kk0, sz_, hh: lambda e: e.activation(
                                        PTb[:, kk0:kk0 + sz_, :], pb[:, 0:sz_ * 512],
                                        AF.Exp, bias=negb[:, hh:hh + 1]))(
                                            pbuf, gk0, gsz, h),
                                    reads=[pB, bnegb], waits=avw,
                                    writes=[bPT] + [bzb[j] for j in zjs])
                        r0g = gk0 - 4 * c
                        if r0g >= 0:
                            tm = None
                            for ii in range(2):
                                rr_ = r0g + ii
                                tm = P.op("vector",
                                          (lambda kk, st: lambda e:
                                           e.tensor_tensor(
                                              PTb[:, kk, :], PTb[:, kk, :],
                                              sbc["dmm"][:, st:st + 512],
                                              op=ALU.mult))(
                                                  gk0 + ii, 384 - 128 * rr_),
                                          reads=[cb["dmm"]], writes=[bPT])
                                gtoks[gk0 + ii] = tm
                        else:
                            gtoks[gk0] = texp
                            gtoks[gk0 + 1] = texp

                    def emit_av_group(gidx):
                        nonlocal av_last
                        _, gk0, gsz = groups[gidx]
                        for kj in range(gk0, gk0 + gsz):
                            av_last = P.op(
                                "tensor",
                                (lambda kk, hh, kt: lambda e: e.matmul(
                                    pAVX[0:65, :],
                                    Vaug[:, kk, hh * 65:(hh + 1) * 65],
                                    PTb[:, kk, :], start=(kk == 0),
                                    stop=(kk == kt - 1)))(kj, h, ktiles),
                                reads=[bVaug], waits=[gtoks[kj]],
                                writes=([bpAVXb] if kj == 0 else []),
                                inc=True)
                            avtok[kj] = av_last

                    ng = len(groups)
                    for gi in range(ng):
                        emit_scores_exp(gi)
                    if h > 0:
                        emit_fix(h - 1)
                    for gi in range(ng):
                        emit_av_group(gi)
                    bpAVXb.w = av_last
                    drip_phase0(1)
                    P.op("vector", (lambda uu, rr: lambda e: e.tensor_copy(
                        mergedT[rr:rr + 64, uu, :], pAVX[0:64, :]))(u, r0),
                        reads=[bpAVXb], writes=[bmg])
                    def _recip(of):
                        def g(e):
                            with nc.allow_low_precision(reason="f16 recip den"):
                                return e.reciprocal(rdenS[0:1, of:of + 512],
                                                    pAVX[64:65, 0:512])
                        return g
                    P.op("vector", _recip((h % 2) * 512),
                         reads=[bpAVXb], writes=[brdS[h % 2]])
                emit_fix(NH - 1)
                for ts4 in range(4):
                    pz = ts4
                    half = ts4 % 2
                    P.dma("scalar", "dXR" if ts4 % 2 == 0 else "dXR1",
                          xres[:, ts4 % 2, :],
                          x_d[b, c * 512 + ts4 * 128:c * 512 + (ts4 + 1) * 128, :],
                          writes=[bxres[ts4 % 2]])
                    for n in range(2):
                        for k in range(8):
                            P.op("tensor",
                                 (lambda hh, nn, kk, tt: lambda e: e.matmul(
                                     pS1[:, hh * 1024 + nn * 512:
                                         hh * 1024 + (nn + 1) * 512],
                                     mergedT[:, kk, ds(tt)],
                                     wattB[:, kk, ds(nn, 512)],
                                     start=(kk == 0), stop=(kk == 7)))(half, n, k, ts4),
                                 reads=[bmg, bwattB],
                                 writes=([bqB if half == 0 else bqC]
                                         if (n == 0 and k == 0) else []),
                                 inc=(n == 1 and k == 7))
                    zv, bzv = zb[pz], bzb[pz]
                    alias = [bPT] if pz > 0 else []
                    lr, blr = lnrow[pz], blnrow[pz]
                    P.op("vector", (lambda hh, zz, pzz: lambda e: e.tensor_tensor(
                        zz[:], pS1[:, hh * 1024:(hh + 1) * 1024],
                        xres[:, pzz, :], op=ALU.add))(half, zv, ts4 % 2),
                        reads=[bqB if half == 0 else bqC, bxres[ts4 % 2]],
                        writes=[bzv] + alias)
                    def _ln_tail(zv=zv, bzv=bzv, lr=lr, blr=blr, alias=alias,
                                 pz=pz, row0=b * SEQ + c * 512 + ts4 * 128):
                        P.op("vector", (lambda zz, l0: lambda e: e.reduce_sum(
                            l0[:], zz[:], axis=AX.X))(zv, lr[0]),
                            reads=[bzv], writes=[blr[0]])
                        P.op("vector", (lambda l0, l1: lambda e: e.tensor_scalar(
                            l1[:], l0[:], 1.0 / D, None, op0=ALU.mult))(
                                lr[0], lr[1]),
                            reads=[blr[0]], writes=[blr[1]])
                        P.op("vector", (lambda zz, l1: lambda e: e.tensor_scalar(
                            zz[:], zz[:], l1[:], None,
                            op0=ALU.subtract))(zv, lr[1]),
                            reads=[blr[1]], writes=[bzv])
                        P.op("scalar", (lambda zz, l2: lambda e: e.activation(
                            PTb[:, 14:16, :].rearrange("p a b -> p (a b)"), zz[:],
                            AF.Square, accum_out=l2[:]))(zv, lr[2]),
                            reads=[bzv], writes=[bPT, blr[2]])
                        P.op("scalar", (lambda l2, l3: lambda e: e.activation(
                            l3[:], l2[:], AF.Ln, scale=1.0 / D,
                            bias=cEPS[:]))(lr[2], lr[3]),
                            reads=[blr[2], cb["eps"]], writes=[blr[3]])
                        P.op("scalar", (lambda l3: lambda e: e.activation(
                            l3[:], l3[:], AF.Exp, scale=-0.5))(lr[3]),
                            reads=[blr[3]], writes=[blr[3]])
                        P.op("vector", (lambda zz, l3: lambda e: e.tensor_scalar(
                            zz[:], zz[:], l3[:], None, op0=ALU.mult))(zv, lr[3]),
                            reads=[blr[3]], writes=[bzv] + alias)
                        P.dma("scalar", "dX1S" if pz % 2 == 0 else "dX1S2",
                              x1s[row0:row0 + 128, :], zv[:], reads=[bzv],
                              writes=[gbuf("x1s0" if pz % 2 == 0 else "x1s1")])
                    deferred_ln.append(_ln_tail)
                drip_phase0(6)

    if mode != "B":
        flush_ln()
    if mode != "A":
        drip_phase0(len(units0))
        P.dma("sync", "dGG", ggT[:], wb["b_gateT"].rearrange("(k p) f -> p k f", p=128),
              reads=[gbuf("b_gateT")], writes=[gbuf("ggT")])
        fenceA = P.fence()
        ar.reset(markA)
        x1tm = [ar.alloc([4, 1024], F32) for _ in range(2)]
        x1T = ar.alloc([8, 512], BF16)
        h0T = ar.alloc([8, 512], BF16)
        h1T = ar.alloc([8, 512], BF16)
        actT = [ar.alloc([8, 512], BF16) for _ in range(2)]
        spT = ar.alloc([16, 512], BF16)
        routed = ar.alloc([8, 512], F32)
        c_bc = ar.alloc([NE, 512], BF16)
        cT = ar.alloc([512], BF16, parts=8)
        gexp = ar.alloc([4 * NE], F32)
        workg = [ar.alloc([4 * NE], F32) for _ in range(2)]
        selg = ar.alloc([4 * NE], F32)
        gtmp = ar.alloc([4 * NE], F32)
        mxg = ar.alloc([4], F32)
        t4g = ar.alloc([4], F32)
        accg = ar.alloc([4], F32)
        rivg = ar.alloc([4], F32)
        combine = ar.alloc([4, NE], BF16)
        zsqB = ar.alloc([1024], BF16)
        lnB4 = [[ar.alloc([1], F32) for _ in range(4)] for _ in range(4)]
        wsl = [ar.alloc([8, 1024], BF16) for _ in range(3)]

        bx1tm = [Buf(f"x1tm{i}") for i in range(2)]
        bx1b = None
        bx1T, bh0, bh1, bsp, brt = (Buf(n) for n in
            ("x1T", "h0T", "h1T", "spT", "routed"))
        bact = [Buf(f"actT{i}") for i in range(2)]
        bcbc, bcT, bgexp, bselg, bmxg, bt4g, baccg, brivg, bcomb = (
            Buf(n) for n in ("c_bc", "cT", "gexp", "selg", "mxg", "t4g",
                             "accg", "rivg", "comb"))
        bgtmp = Buf("gtmp")
        bworkg = [Buf(f"workg{i}") for i in range(2)]
        bzsqB = Buf("zsqB")
        blnB4 = [[Buf(f"lnB{t}{i}") for i in range(4)] for t in range(4)]
        bwsl = [Buf(f"wsl{i}") for i in range(3)]
        bPP = [[Buf("Bp0a"), Buf("Bp0b")], [Buf("Bp1a"), Buf("Bp1b")]]
        bpT1 = Buf("BpT1")
        bpAVXb2 = Buf("BpAVX")
        bpTRb2 = Buf("BpTR")
        for bb in ([bx1tm[0], bx1tm[1], bx1T, bh0, bh1, bsp, brt,
                    bcbc, bcT, bgexp, bselg, bmxg, bt4g, baccg, brivg, bcomb,
                    bzsqB, bPP[0][0], bPP[0][1], bPP[1][0], bPP[1][1], bpT1,
                    bpAVXb2, bpTRb2] + bact + bworkg + bwsl
                   + [b for bs in blnB4 for b in bs]):
            bb.r = list(fenceA)
        fwaits = fenceA

        deferB = []

        def flushB(k):
            if deferB:
                deferB[0][k]()
                if k == 3:
                    deferB.pop(0)

        NU = 36

        def unit_src(name, sub):
            t = wb[name]
            if name in ("b_ew1", "b_ew3", "b_ew2"):
                return t[sub].rearrange("(k p) f -> p k f", p=128)
            if name in ("b_shw1", "b_shw3"):
                return t[:, sub * 1024:(sub + 1) * 1024].rearrange(
                    "(k p) f -> p k f", p=128)
            if name == "b_shw2":
                return t[sub * 1024:(sub + 1) * 1024, :].rearrange(
                    "(k p) f -> p k f", p=128)
            return t.rearrange("(k p) f -> p k f", p=128)

        pTR3b = pTR[:].rearrange("p (a b) -> p a b", b=128)
        gv = lambda ap: ap.rearrange("p (a b) -> p a b", b=NE)
        PP = [pS2[:], pS1[:, 0:1024]]

        def emit_x1_load(cc):
            pcc = cc % 2
            P.dma("gpsimd", "dXT0" if pcc == 0 else "dXT1", x1tm[pcc][:, :, :],
                  x1s[cc * 512:(cc + 1) * 512, :].rearrange(
                      "(t p) f -> p t f", p=128),
                  reads=[gbuf("x1s0"), gbuf("x1s1")], writes=[bx1tm[pcc]],
                  waits=(fwaits if cc < 2 else ()))

        NCHB = LIMIT_BCHUNKS or NCH_B
        for c in range(NCHB):
            pc = c % 2
            u_ctr = [0]

            def next_unit(name, sub=None):
                ui = u_ctr[0]
                u_ctr[0] += 1
                sl = (c * NU + ui) % 3
                tok = P.dma("sync", f"dWU{sl}", wsl[sl][:, :, :],
                            unit_src(name, sub),
                            reads=[gbuf(name)], writes=[bwsl[sl]],
                            waits=(fwaits if c == 0 and ui < 3 else ()))
                return sl, tok

            def mm_block(pr, half, w_sl, m, rhs_ap, rhs_buf):
                for k in range(8):
                    P.op("tensor",
                         (lambda kk, mm_, hh, ws, dd: lambda e: e.matmul(
                             dd[:, ds(hh, 512)], wsl[ws][:, kk, ds(mm_)],
                             rhs_ap[:, kk, :], start=(kk == 0),
                             stop=(kk == 7)))(k, m, half, w_sl, PP[pr]),
                         reads=[bwsl[w_sl], rhs_buf],
                         writes=([bPP[pr][half]] if k == 0 else []),
                         inc=(k == 7))

            if c == 0:
                emit_x1_load(0)
            for ts4 in range(4):
                pr = ts4 % 2
                for j in range(8):
                    P.op("tensor",
                         (lambda tt, jj, pp, dd: lambda e: e.transpose(
                             dd[:, jj * 128:(jj + 1) * 128],
                             x1tm[pp][:, tt, ds(jj)], sbc["id32"][:]))(
                                 ts4, j, pc, PP[pr]),
                         reads=[bx1tm[pc], cb["id32"]],
                         writes=([bPP[pr][0], bPP[pr][1]] if j == 0 else []),
                         inc=(j == 7),
                         waits=(fwaits if c == 0 and ts4 == 0 and j == 0 else ()))
                P.op("vector", (lambda tt, dd: lambda e: e.tensor_copy(
                    x1T[:, :, ds(tt)],
                    dd.rearrange("p (a b) -> p a b", b=128)))(ts4, PP[pr]),
                    reads=[bPP[pr][0], bPP[pr][1]], writes=[bx1T],
                    waits=(fwaits if c == 0 and ts4 == 0 else ()))
            flushB(0)

            sl, _ = next_unit("b_lin0")
            for mp in range(4):
                pr = mp % 2
                for m2 in range(2):
                    mm_block(pr, m2, sl, 2 * mp + m2, x1T, bx1T)
                for m2 in range(2):
                    m = 2 * mp + m2
                    P.op("vector", (lambda mm_, m2_, dd: lambda e: e.tensor_scalar(
                        h0T[:, mm_, :], dd[:, ds(m2_, 512)],
                        biases["lin0_b"][:, mm_:mm_ + 1], None, op0=ALU.add))(
                            m, m2, PP[pr]),
                        reads=[bPP[pr][m2], cb["lin0_b"]], writes=[bh0])
            flushB(1)
            sl, _ = next_unit("b_s1w1")
            for mp in range(4):
                pr = mp % 2
                for m2 in range(2):
                    mm_block(pr, m2, sl, 2 * mp + m2, h0T, bh0)
                for m2 in range(2):
                    m = 2 * mp + m2
                    P.op("scalar", (lambda mm_, m2_, dd: lambda e: e.activation(
                        actT[0][:, mm_, :], dd[:, ds(m2_, 512)], AF.Silu,
                        bias=biases["s1b1"][:, mm_:mm_ + 1]))(m, m2, PP[pr]),
                        reads=[bPP[pr][m2], cb["s1b1"]], writes=[bact[0]])
            sl, _ = next_unit("b_s1w2")
            for mp in range(4):
                pr = mp % 2
                for m2 in range(2):
                    mm_block(pr, m2, sl, 2 * mp + m2, h0T, bh0)
                for m2 in range(2):
                    m = 2 * mp + m2
                    P.op("vector", (lambda mm_, m2_, dd: lambda e:
                         e.scalar_tensor_tensor(
                        h1T[:, mm_, :], dd[:, ds(m2_, 512)],
                        biases["s1b2"][:, mm_:mm_ + 1], actT[0][:, mm_, :],
                        op0=ALU.add, op1=ALU.mult))(m, m2, PP[pr]),
                        reads=[bPP[pr][m2], cb["s1b2"], bact[0]], writes=[bh1])
            flushB(2)
            for ts4 in range(4):
                for k in range(8):
                    P.op("tensor",
                         (lambda tt, kk: lambda e: e.matmul(
                             pAVX[:, tt * NE:(tt + 1) * NE], h1T[:, kk, ds(tt)],
                             ggT[:, kk, :], start=(kk == 0), stop=(kk == 7)))(ts4, k),
                         reads=[bh1, gbuf("ggT")],
                         writes=([bpAVXb2] if (ts4 == 0 and k == 0) else []),
                         inc=(k == 7))
            P.op("scalar", lambda e: e.activation(gexp[:], pAVX[:, 0:4 * NE], AF.Exp),
                 reads=[bpAVXb2], writes=[bgexp])
            P.op("vector", lambda e: e.tensor_copy(workg[0][:], gexp[:]),
                 reads=[bgexp], writes=[bworkg[0]])
            for i in range(4):
                src, bsrc = workg[i % 2], bworkg[i % 2]
                dst, bdst = workg[(i + 1) % 2], bworkg[(i + 1) % 2]
                mdst, bmdst = (t4g, bt4g) if i == 3 else (mxg, bmxg)
                P.op("vector", (lambda s_, md: lambda e: e.reduce_max(
                    md[:], gv(s_[:]), axis=AX.X))(src, mdst),
                    reads=[bsrc], writes=[bmdst])
                if i == 0:
                    P.op("vector", lambda e: e.tensor_copy(accg[:], mxg[:]),
                         reads=[bmxg], writes=[baccg])
                else:
                    P.op("vector", (lambda md: lambda e: e.tensor_tensor(
                        accg[:], accg[:], md[:], op=ALU.add))(mdst),
                        reads=[bmdst, baccg], writes=[baccg])
                if i < 3:
                    P.op("vector", (lambda s_: lambda e: e.tensor_tensor(
                        gv(selg[:]), gv(s_[:]),
                        mxg[:].unsqueeze(2).to_broadcast((128, 4, NE)),
                        op=ALU.is_ge))(src),
                        reads=[bsrc, bmxg], writes=[bselg])
                    P.op("vector", (lambda s_: lambda e: e.scalar_tensor_tensor(
                        gtmp[:], s_[:], 1.0, selg[:], op0=ALU.add,
                        op1=ALU.mult))(src),
                        reads=[bsrc, bselg], writes=[bgtmp])
                    P.op("vector", (lambda s_, d_: lambda e: e.tensor_tensor(
                        d_[:], s_[:], gtmp[:], op=ALU.subtract))(src, dst),
                        reads=[bsrc, bgtmp], writes=[bdst])
            P.op("vector", lambda e: e.reciprocal(rivg[:], accg[:]),
                 reads=[baccg], writes=[brivg])
            P.op("vector", lambda e: e.tensor_tensor(
                gv(selg[:]), gv(gexp[:]),
                t4g[:].unsqueeze(2).to_broadcast((128, 4, NE)), op=ALU.is_ge),
                reads=[bgexp, bt4g], writes=[bselg])
            P.op("vector", lambda e: e.tensor_tensor(
                gv(gexp[:]), gv(gexp[:]),
                rivg[:].unsqueeze(2).to_broadcast((128, 4, NE)), op=ALU.mult),
                reads=[brivg, bgexp], writes=[bgexp])
            P.op("vector", lambda e: e.tensor_tensor(
                combine[:].rearrange("p a b -> p (a b)"), gexp[:], selg[:],
                op=ALU.mult), reads=[bgexp, bselg], writes=[bcomb])

            def emit_combine_bcast():
                for ts4 in range(4):
                    P.op("tensor", (lambda tt: lambda e: e.transpose(
                        pTRbf[0:NE, ds(tt)], combine[:, tt, :], sbc["idb"][:]))(ts4),
                        reads=[bcomb, cb["idb"]],
                        writes=([bpTRb2] if ts4 == 0 else []), inc=(ts4 == 3))
                P.op("vector", lambda e: e.tensor_copy(cT[:], pTRbf[0:NE, 0:512]),
                     reads=[bpTRb2], writes=[bcT])
                P.dma("gpsimd", "dCC", ccs, cT[:], reads=[bcT],
                      writes=[gbuf("ccs")])
                P.dma("gpsimd", "dCC", c_bc[:].rearrange("p a b -> p (a b)"),
                      ccs.rearrange("a b -> (a b)").unsqueeze(0)
                      .to_broadcast((128, NE * 512)),
                      reads=[gbuf("ccs")], writes=[bcbc])
            flushB(3)
            if c + 1 < NCHB:
                emit_x1_load(c + 1)
            rflat = routed[:].rearrange("p a b -> p (a b)")

            def emit_gu(e_i):
                pe = e_i % 2
                aflat = actT[pe][:].rearrange("p a b -> p (a b)")
                sl1, _ = next_unit("b_ew1", e_i)
                for mp in range(4):
                    pr = mp % 2
                    for m2 in range(2):
                        mm_block(pr, m2, sl1, 2 * mp + m2, h1T, bh1)
                    P.op("scalar", (lambda mp_, fl, dd: lambda e: e.activation(
                        fl[:, mp_ * 1024:(mp_ + 1) * 1024], dd, AF.Silu))(
                            mp, aflat, PP[pr]),
                        reads=[bPP[pr][0], bPP[pr][1]], writes=[bact[pe]])
                sl3, _ = next_unit("b_ew3", e_i)
                for mp in range(4):
                    pr = mp % 2
                    for m2 in range(2):
                        mm_block(pr, m2, sl3, 2 * mp + m2, h1T, bh1)
                    P.op("vector", (lambda mp_, fl, dd: lambda e: e.tensor_tensor(
                        fl[:, mp_ * 1024:(mp_ + 1) * 1024],
                        fl[:, mp_ * 1024:(mp_ + 1) * 1024], dd,
                        op=ALU.mult))(mp, aflat, PP[pr]),
                        reads=[bPP[pr][0], bPP[pr][1], bact[pe]], writes=[bact[pe]])

            def emit_fold(e_i):
                pe = e_i % 2
                P.op("vector", (lambda e_, pp: lambda e: e.tensor_tensor(
                    actT[pp][:], actT[pp][:],
                    c_bc[:, e_, :].unsqueeze(1).to_broadcast((128, 8, 512)),
                    op=ALU.mult))(e_i, pe),
                    reads=[bcbc, bact[pe]], writes=[bact[pe]])

            def emit_y(e_i):
                pe = e_i % 2
                sl2, _ = next_unit("b_ew2", e_i)
                for mp in range(4):
                    pr = mp % 2
                    for m2 in range(2):
                        mm_block(pr, m2, sl2, 2 * mp + m2, actT[pe], bact[pe])
                    if e_i == 0:
                        P.op("vector", (lambda mp_, dd: lambda e: e.tensor_copy(
                            rflat[:, mp_ * 1024:(mp_ + 1) * 1024], dd))(
                                mp, PP[pr]),
                            reads=[bPP[pr][0], bPP[pr][1]], writes=[brt])
                    else:
                        P.op("vector", (lambda mp_, dd: lambda e: e.tensor_tensor(
                            rflat[:, mp_ * 1024:(mp_ + 1) * 1024],
                            rflat[:, mp_ * 1024:(mp_ + 1) * 1024], dd,
                            op=ALU.add))(mp, PP[pr]),
                            reads=[bPP[pr][0], bPP[pr][1], brt], writes=[brt])

            emit_gu(0)
            emit_combine_bcast()
            emit_fold(0)
            for e_i in range(1, NE):
                emit_gu(e_i)
                emit_fold(e_i)
                emit_y(e_i - 1)
            emit_y(NE - 1)
            spflat = spT[:].rearrange("p a b -> p (a b)")
            for half in range(2):
                sl, _ = next_unit("b_shw1", half)
                for mp in range(4):
                    pr = mp % 2
                    for m2 in range(2):
                        mm_block(pr, m2, sl, 2 * mp + m2, h1T, bh1)
                    P.op("scalar", (lambda hh, mp_, dd: lambda e: e.activation(
                        spflat[:, hh * 4096 + mp_ * 1024:
                               hh * 4096 + (mp_ + 1) * 1024],
                        dd, AF.Silu))(half, mp, PP[pr]),
                        reads=[bPP[pr][0], bPP[pr][1]], writes=[bsp])
            for half in range(2):
                sl, _ = next_unit("b_shw3", half)
                for mp in range(4):
                    pr = mp % 2
                    for m2 in range(2):
                        mm_block(pr, m2, sl, 2 * mp + m2, h1T, bh1)
                    P.op("vector", (lambda hh, mp_, dd: lambda e: e.tensor_tensor(
                        spflat[:, hh * 4096 + mp_ * 1024:
                               hh * 4096 + (mp_ + 1) * 1024],
                        spflat[:, hh * 4096 + mp_ * 1024:
                               hh * 4096 + (mp_ + 1) * 1024],
                        dd, op=ALU.mult))(half, mp, PP[pr]),
                        reads=[bPP[pr][0], bPP[pr][1], bsp], writes=[bsp])
            slA, _ = next_unit("b_shw2", 0)
            slB, _ = next_unit("b_shw2", 1)
            for mp in range(4):
                pr = mp % 2
                for m2 in range(2):
                    m = 2 * mp + m2
                    for k16 in range(16):
                        slx = slA if k16 < 8 else slB
                        P.op("tensor",
                             (lambda kk, mm_, hh, ws, dd: lambda e: e.matmul(
                                 dd[:, ds(hh, 512)], wsl[ws][:, kk % 8, ds(mm_)],
                                 spT[:, kk, :], start=(kk == 0),
                                 stop=(kk == 15)))(k16, m, m2, slx, PP[pr]),
                             reads=[bwsl[slx], bsp],
                             writes=([bPP[pr][m2]] if k16 == 0 else []),
                             inc=(k16 == 15))
                for m2 in range(2):
                    m = 2 * mp + m2
                    P.op("vector", (lambda mm_, m2_, dd: lambda e:
                         e.scalar_tensor_tensor(
                        h0T[:, mm_, :], dd[:, ds(m2_, 512)], 1.0,
                        routed[:, mm_, :], op0=ALU.mult, op1=ALU.add))(
                            m, m2, PP[pr]),
                        reads=[bPP[pr][m2], brt], writes=[bh0])
            sl, _ = next_unit("b_lin1")
            for mp in range(4):
                pr = mp % 2
                for m2 in range(2):
                    mm_block(pr, m2, sl, 2 * mp + m2, h0T, bh0)
                for m2 in range(2):
                    m = 2 * mp + m2
                    P.op("vector", (lambda mm_, m2_, dd: lambda e: e.tensor_scalar(
                        h1T[:, mm_, :], dd[:, ds(m2_, 512)],
                        biases["lin1_b"][:, mm_:mm_ + 1], None, op0=ALU.add))(
                            m, m2, PP[pr]),
                        reads=[bPP[pr][m2], cb["lin1_b"]], writes=[bh1])
            sl, _ = next_unit("b_s2w1")
            for mp in range(4):
                pr = mp % 2
                for m2 in range(2):
                    mm_block(pr, m2, sl, 2 * mp + m2, h1T, bh1)
                for m2 in range(2):
                    m = 2 * mp + m2
                    P.op("scalar", (lambda mm_, m2_, dd: lambda e: e.activation(
                        actT[0][:, mm_, :], dd[:, ds(m2_, 512)], AF.Silu,
                        bias=biases["s2b1"][:, mm_:mm_ + 1]))(m, m2, PP[pr]),
                        reads=[bPP[pr][m2], cb["s2b1"]], writes=[bact[0]])
            sl, _ = next_unit("b_s2w2")
            for mp in range(4):
                pr = mp % 2
                for m2 in range(2):
                    mm_block(pr, m2, sl, 2 * mp + m2, h1T, bh1)
                for m2 in range(2):
                    m = 2 * mp + m2
                    P.op("vector", (lambda mm_, m2_, dd: lambda e:
                         e.scalar_tensor_tensor(
                        routed[:, mm_, :], dd[:, ds(m2_, 512)],
                        biases["s2b2"][:, mm_:mm_ + 1], actT[0][:, mm_, :],
                        op0=ALU.add, op1=ALU.mult))(m, m2, PP[pr]),
                        reads=[bPP[pr][m2], cb["s2b2"], bact[0]], writes=[brt])
            for ts4 in range(4):
                for j in range(8):
                    P.op("tensor",
                         (lambda tt, jj: lambda e: e.transpose(
                             pS1[:, 1024 + jj * 128:1024 + (jj + 1) * 128],
                             routed[:, jj, ds(tt)], sbc["id32"][:]))(ts4, j),
                         reads=[brt, cb["id32"]],
                         writes=([bpT1] if j == 0 else []), inc=(j == 7))
                P.op("vector", (lambda tt, pp: lambda e: e.tensor_tensor(
                    x1tm[pp][:, tt, :], pS1[:, 1024:2048],
                    x1tm[pp][:, tt, :], op=ALU.add))(ts4, pc),
                    reads=[bpT1, bx1tm[pc]], writes=[bx1tm[pc]])

            def _s0(pc=pc):
                for t4 in range(4):
                    P.op("vector", (lambda tt, pp, l0: lambda e: e.reduce_sum(
                        l0[:], x1tm[pp][:, tt, :], axis=AX.X))(
                            t4, pc, lnB4[t4][0]),
                        reads=[bx1tm[pc]], writes=[blnB4[t4][0]])
                    P.op("vector", (lambda l0, l1: lambda e: e.tensor_scalar(
                        l1[:], l0[:], 1.0 / D, None, op0=ALU.mult))(
                            lnB4[t4][0], lnB4[t4][1]),
                        reads=[blnB4[t4][0]], writes=[blnB4[t4][1]])

            def _s1(pc=pc):
                for t4 in range(4):
                    P.op("vector", (lambda tt, pp, l1: lambda e: e.tensor_scalar(
                        x1tm[pp][:, tt, :], x1tm[pp][:, tt, :], l1[:], None,
                        op0=ALU.subtract))(t4, pc, lnB4[t4][1]),
                        reads=[blnB4[t4][1], bx1tm[pc]], writes=[bx1tm[pc]])

            def _s2(pc=pc):
                for t4 in range(4):
                    P.op("scalar", (lambda tt, pp, l2: lambda e: e.activation(
                        zsqB[:], x1tm[pp][:, tt, :], AF.Square,
                        accum_out=l2[:]))(t4, pc, lnB4[t4][2]),
                        reads=[bx1tm[pc]], writes=[bzsqB, blnB4[t4][2]])
                    P.op("scalar", (lambda l2, l3: lambda e: e.activation(
                        l3[:], l2[:], AF.Ln, scale=1.0 / D, bias=cEPS[:]))(
                            lnB4[t4][2], lnB4[t4][3]),
                        reads=[blnB4[t4][2], cb["eps"]], writes=[blnB4[t4][3]])
                    P.op("scalar", (lambda l3: lambda e: e.activation(
                        l3[:], l3[:], AF.Exp, scale=-0.5))(lnB4[t4][3]),
                        reads=[blnB4[t4][3]], writes=[blnB4[t4][3]])

            def _s3(c=c, pc=pc):
                for t4 in range(4):
                    P.op("vector", (lambda tt, pp, l3: lambda e: e.tensor_scalar(
                        x1tm[pp][:, tt, :], x1tm[pp][:, tt, :], l3[:], None,
                        op0=ALU.mult))(t4, pc, lnB4[t4][3]),
                        reads=[blnB4[t4][3], bx1tm[pc]], writes=[bx1tm[pc]])
                    P.dma("gpsimd", "dOUT",
                          out_d[c * 512 + t4 * 128:c * 512 + (t4 + 1) * 128, :],
                          x1tm[pc][:, t4, :], reads=[bx1tm[pc]],
                          writes=[gbuf("outd")])
            deferB.append([_s0, _s1, _s2, _s3])

        for stages in deferB:
            for f in stages:
                f()
        deferB.clear()

    fin = [Tok(s, P.count[s], None, P.gseq)
           for s in P.sems if s.startswith("d") and P.count[s]]
    P.op("gpsimd", lambda e: e.memset(neg1[0:1, :], -1.0),
         waits=fin, inc=False)

    with nc.Block() as block:
        P.emit(block)

    for cm in reversed(ctxs):
        cm.__exit__(None, None, None)
    return nc, P



def _host_prep(inputs):
    perm = np.zeros(D, dtype=np.int64)
    for h in range(NH):
        for j in range(32):
            perm[h * 64 + j] = h * 64 + 2 * j
            perm[h * 64 + 32 + j] = h * 64 + 2 * j + 1
    f16 = np.float16
    qwp = np.ascontiguousarray(np.asarray(inputs["L0_attn_qw"], f16)[:, perm])
    kwp = np.ascontiguousarray(np.asarray(inputs["L0_attn_kw"], f16)[:, perm])
    vw = np.ascontiguousarray(np.asarray(inputs["L0_attn_vw"], f16))
    ow = np.ascontiguousarray(np.asarray(inputs["L0_attn_ow"], f16))

    inv = (10000.0 ** (-(np.arange(0, HD, 2, dtype=np.float64)) / HD))
    pos = np.arange(SEQ, dtype=np.float64)
    freq = pos[:, None] * inv[None, :]

    def tab(a, scale):
        t = (a * scale).astype(np.float16)
        return np.ascontiguousarray(t.reshape(NT, 128, 32).transpose(1, 0, 2))
    cosq, sinq = tab(np.cos(freq), 1.0 / 32.0), tab(np.sin(freq), 1.0 / 32.0)
    cosk, sink = tab(np.cos(freq), 1.0), tab(np.sin(freq), 1.0)

    tri = np.zeros((128, 128), np.float16)
    for p in range(128):
        tri[p, p:] = 1.0
    dmm = np.zeros((128, 896), np.float16)
    dmm[:, 384:512] = tri
    dmm[:, 512:] = 1.0
    idb = np.eye(128).astype(ml_dtypes.bfloat16)
    idh = np.eye(128).astype(np.float16)
    id32 = np.eye(128).astype(np.float32)
    hsel = np.zeros((128, NE, NH), ml_dtypes.bfloat16)
    for pt in range(8):
        for p in range(128):
            hsel[p, pt, 2 * pt + p // 64] = 1.0

    f32 = np.float32
    return dict(
        qwp=qwp, kwp=kwp, vw=vw, ow=ow,
        cosq=cosq, sinq=sinq, cosk=cosk, sink=sink,
        tri=tri, dmm=dmm, idb=idb, idh=idh, id32=id32, hsel=hsel,
        lin0_w=np.asarray(inputs["L0_ffn_lin0_w"], f32),
        lin0_b=np.ascontiguousarray(np.asarray(inputs["L0_ffn_lin0_b"], f32).reshape(8, 128).T),
        s1w1=np.asarray(inputs["L0_ffn_swi1_w1"], f32),
        s1b1=np.ascontiguousarray(np.asarray(inputs["L0_ffn_swi1_b1"], f32).reshape(8, 128).T),
        s1w2=np.asarray(inputs["L0_ffn_swi1_w2"], f32),
        s1b2=np.ascontiguousarray(np.asarray(inputs["L0_ffn_swi1_b2"], f32).reshape(8, 128).T),
        gateT=np.ascontiguousarray(np.asarray(inputs["L0_ffn_gate_w"], f32).T),
        ew1=np.asarray(inputs["L0_ffn_exp_w1"], f32),
        ew3=np.asarray(inputs["L0_ffn_exp_w3"], f32),
        ew2=np.asarray(inputs["L0_ffn_exp_w2"], f32),
        shw1=np.asarray(inputs["L0_ffn_sh_w1"], f32),
        shw3=np.asarray(inputs["L0_ffn_sh_w3"], f32),
        shw2=np.asarray(inputs["L0_ffn_sh_w2"], f32),
        lin1_w=np.asarray(inputs["L0_ffn_lin1_w"], f32),
        lin1_b=np.ascontiguousarray(np.asarray(inputs["L0_ffn_lin1_b"], f32).reshape(8, 128).T),
        s2w1=np.asarray(inputs["L0_ffn_swi2_w1"], f32),
        s2b1=np.ascontiguousarray(np.asarray(inputs["L0_ffn_swi2_b1"], f32).reshape(8, 128).T),
        s2w2=np.asarray(inputs["L0_ffn_swi2_w2"], f32),
        s2b2=np.ascontiguousarray(np.asarray(inputs["L0_ffn_swi2_b2"], f32).reshape(8, 128).T),
    )


def kernel(**inputs):
    base = _host_prep(inputs)
    x = np.asarray(inputs["L0_attn_x"], np.float16)
    in_maps = []
    for ci in range(NCORES):
        m = dict(base)
        m["x"] = np.ascontiguousarray(x[ci * NSEQ:(ci + 1) * NSEQ])
        in_maps.append(m)
    nc, _ = build_nc("full")
    res = run_bass_kernel_spmd(nc, in_maps, core_ids=list(range(NCORES)))
    shards = [np.asarray(r["out"], np.float32).reshape(NSEQ, SEQ, D)
              for r in res.results]
    return np.concatenate(shards, axis=0)


if __name__ == "__main__":
    nc, P = build_nc("full")
    print("ops:", P.nops)


# version 27
# speedup vs baseline: 1.0741x; 1.0741x over previous
import numpy as np
import ml_dtypes
import concourse.bass as bass
import concourse.mybir as mybir
from concourse.bass_utils import run_bass_kernel_spmd

F32 = mybir.dt.float32
F16 = mybir.dt.float16
BF16 = mybir.dt.bfloat16
U8 = mybir.dt.uint8
I32 = mybir.dt.int32
AF = mybir.ActivationFunctionType
ALU = mybir.AluOpType
AX = mybir.AxisListType

NCORES = 8
BATCH, SEQ, D = 32, 2048, 1024
NH, HD = 16, 64
NSEQ = BATCH // NCORES
TPC = NSEQ * SEQ
NT = SEQ // 128
NCHK = 4
CH = 512
NCH_B = TPC // CH
LIMIT_BCHUNKS = None
LIMIT_ASEQS = None
NE = 8
EPS = 1e-5

ENGINES = ("sync", "tensor", "vector", "scalar", "gpsimd")
SEMKEY = {"sync": "sy", "tensor": "te", "vector": "ve", "scalar": "sc",
          "gpsimd": "gp"}
INORDER_SKIP = ("tensor",)


class Tok:
    __slots__ = ("sem", "v", "eng", "seq")

    def __init__(self, sem, v, eng, seq):
        self.sem, self.v, self.eng, self.seq = sem, v, eng, seq


class Buf:
    __slots__ = ("name", "w", "r")

    def __init__(self, name):
        self.name = name
        self.w = None
        self.r = []


class Prog:
    def __init__(self, nc, sems):
        self.nc = nc
        self.sems = sems
        self.count = {n: 0 for n in sems}
        self.q = {e: [] for e in ENGINES}
        self.nops = {e: 0 for e in ENGINES}
        self.gseq = 0
        self.last_real = {e: None for e in ENGINES}

    def fence(self):
        toks = []
        for e in ENGINES:
            lr = self.last_real[e]
            if lr is not None:
                toks.append(lr)
        for s in self.sems:
            if s.startswith("d") and self.count[s]:
                toks.append(Tok(s, self.count[s], None, self.gseq))
        return toks

    def op(self, eng, fn, reads=(), writes=(), waits=(), inc=True, dma=False):
        self.gseq += 1
        seq = self.gseq
        deps = list(waits)
        for b in reads:
            if b.w is not None:
                deps.append(b.w)
        for b in writes:
            if b.w is not None:
                deps.append(b.w)
            deps.extend(b.r)
        best_real = {}
        for t in deps:
            if t is not None and t.sem is not None and not t.sem.startswith("d") \
                    and t.eng is not None:
                if t.eng not in best_real or best_real[t.eng].seq < t.seq:
                    best_real[t.eng] = t
        wd = {}
        for t in deps:
            if t is None:
                continue
            if t.sem is None:
                br = best_real.get(t.eng)
                if br is not None and br.seq >= t.seq:
                    continue
                if t.eng == eng and eng in INORDER_SKIP and not dma:
                    continue
                sub = self.last_real[t.eng]
                if sub is None or sub.seq < t.seq:
                    raise RuntimeError(
                        f"unresolvable dep on {t.eng} (add inc=True there)")
                t = sub
            if (t.eng == eng and eng in INORDER_SKIP and not dma
                    and not t.sem.startswith("d")):
                continue
            if t.v > self.count[t.sem]:
                raise RuntimeError(f"forward wait {t.sem}={t.v}")
            wd[t.sem] = max(wd.get(t.sem, 0), t.v)
        if dma:
            semn = dma
            self.count[semn] += 16
            tok = Tok(semn, self.count[semn], eng, seq)
        elif inc:
            semn = SEMKEY[eng]
            self.count[semn] += 1
            tok = Tok(semn, self.count[semn], eng, seq)
        else:
            semn, tok = None, Tok(None, 0, eng, seq)
        if semn is not None:
            self.last_real[eng] = tok
        self.q[eng].append(
            (wd, fn, (self.sems[semn], 16 if dma else 1) if semn else None))
        self.nops[eng] += 1
        for b in reads:
            b.r.append(tok)
        for b in writes:
            b.w = tok
            b.r = []
        return tok

    def dma(self, eng, sem, out, in_, reads=(), writes=(), waits=(), **kw):
        return self.op(eng, lambda e: e.dma_start(out=out, in_=in_, **kw),
                       reads=reads, writes=writes, waits=waits, dma=sem)

    def emit(self, block):
        def section(engname):
            def f(eng):
                last = {}
                for wd, fn, incv in self.q[engname]:
                    for s, v in sorted(wd.items()):
                        if last.get(s, -1) < v:
                            eng.wait_ge(self.sems[s], v)
                            last[s] = v
                    inst = fn(eng)
                    if incv is not None:
                        inst.then_inc(incv[0], incv[1])
            return f
        block.sync(section("sync"))
        block.tensor(section("tensor"))
        block.vector(section("vector"))
        block.scalar(section("scalar"))
        block.gpsimd(section("gpsimd"))


class Arena:
    def __init__(self, t, nbytes):
        self.t = t
        self.nbytes = nbytes
        self.off = 0
        self.hi = 0

    def alloc(self, shape, dt, parts=128):
        esz = {F32: 4, F16: 2, BF16: 2, U8: 1, I32: 4}[dt]
        n = int(np.prod(shape)) * esz
        off = (self.off + 63) & ~63
        self.off = off + n
        self.hi = max(self.hi, self.off)
        assert self.off <= self.nbytes, f"arena overflow {self.off}"
        ap = self.t[0:parts, off:off + n].bitcast(dt)
        if len(shape) > 1:
            pat = " ".join(f"a{i}" for i in range(len(shape)))
            ap = ap.rearrange(f"p ({pat}) -> p {pat}",
                              **{f"a{i}": shape[i] for i in range(len(shape))})
        return ap

    def mark(self):
        return self.off

    def reset(self, off):
        self.off = off


def ds(i, n=128):
    return slice(i * n, (i + 1) * n)


def build_nc(mode="full"):
    nc = bass.Bass("TRN2", target_bir_lowering=False)

    x_d = nc.dram_tensor("x", [NSEQ, SEQ, D], F16, kind="ExternalInput").ap()
    att_w = {n: nc.dram_tensor(n, [D, D], F16, kind="ExternalInput").ap()
             for n in ("qwp", "kwp", "vw", "ow")}
    f32in = {}
    for n, sh in [("lin0_w", [D, D]), ("lin0_b", [128, 8]),
                  ("s1w1", [D, D]), ("s1b1", [128, 8]), ("s1w2", [D, D]),
                  ("s1b2", [128, 8]),
                  ("gateT", [D, NE]),
                  ("ew1", [NE, D, D]), ("ew3", [NE, D, D]), ("ew2", [NE, D, D]),
                  ("shw1", [D, 2 * D]), ("shw3", [D, 2 * D]), ("shw2", [2 * D, D]),
                  ("lin1_w", [D, D]), ("lin1_b", [128, 8]),
                  ("s2w1", [D, D]), ("s2b1", [128, 8]), ("s2w2", [D, D]),
                  ("s2b2", [128, 8])]:
        f32in[n] = nc.dram_tensor(n, sh, F32, kind="ExternalInput").ap()
    consts_d = {}
    for n, sh, dt in [("cosq", [128, NT, 32], F16), ("sinq", [128, NT, 32], F16),
                      ("cosk", [128, NT, 32], F16), ("sink", [128, NT, 32], F16),
                      ("tri", [128, 128], F16), ("dmm", [128, 896], F16),
                      ("idb", [128, 128], BF16),
                      ("idh", [128, 128], F16),
                      ("id32", [128, 128], F32), ("hsel", [128, NE, NH], BF16)]:
        consts_d[n] = nc.dram_tensor(n, sh, dt, kind="ExternalInput").ap()

    x1s_kind = {"full": "Internal", "A": "ExternalOutput", "B": "ExternalInput"}[mode]
    x1s = nc.dram_tensor("x1s", [TPC, D], F32, kind=x1s_kind).ap()
    out_d = nc.dram_tensor("out", [TPC, D], F32, kind="ExternalOutput").ap()

    wb = {}
    for n, sh in [("b_lin0", [D, D]), ("b_s1w1", [D, D]), ("b_s1w2", [D, D]),
                  ("b_gateT", [D, NE]),
                  ("b_ew1", [NE, D, D]), ("b_ew3", [NE, D, D]), ("b_ew2", [NE, D, D]),
                  ("b_shw1", [D, 2 * D]), ("b_shw3", [D, 2 * D]), ("b_shw2", [2 * D, D]),
                  ("b_lin1", [D, D]), ("b_s2w1", [D, D]), ("b_s2w2", [D, D])]:
        wb[n] = nc.dram_tensor(n, sh, BF16, kind="Internal").ap()
    ccs = nc.dram_tensor("ccs", [NE, CH], BF16, kind="Internal").ap()

    ctxs = []

    def enter(cm):
        ctxs.append(cm)
        return cm.__enter__()

    ARENA_BYTES = 198 * 1024
    arena_t = enter(nc.sbuf_tensor("arena", [128, ARENA_BYTES], U8))
    ar = Arena(arena_t, ARENA_BYTES)

    sbc = {}
    for n in ("cosq", "sinq", "cosk", "sink"):
        sbc[n] = enter(nc.sbuf_tensor(n + "_s", [128, NT, 32], F16))
    sbc["tri"] = enter(nc.sbuf_tensor("tri_s", [128, 128], F16))
    sbc["dmm"] = ar.alloc([896], F16)
    sbc["idb"] = enter(nc.sbuf_tensor("idb_s", [128, 128], BF16))
    sbc["idh"] = enter(nc.sbuf_tensor("idh_s", [128, 128], F16))
    sbc["id32"] = enter(nc.sbuf_tensor("id32_s", [128, 128], F32))
    sbc["hsel"] = enter(nc.sbuf_tensor("hsel_s", [128, NE, NH], BF16))
    biases = {n: enter(nc.sbuf_tensor(n + "_s", [128, 8], F32))
              for n in ("lin0_b", "s1b1", "s1b2", "lin1_b", "s2b1", "s2b2")}
    ggT = enter(nc.sbuf_tensor("ggT", [128, 8, NE], BF16))
    neg1 = enter(nc.sbuf_tensor("neg1", [128, 4 * NE], F32))
    cEPS = enter(nc.sbuf_tensor("cEPS", [128, 1], F32))
    cTINY = enter(nc.sbuf_tensor("cTINY", [128, 1], F32))
    ones128 = enter(nc.sbuf_tensor("ones128", [1, 128], F32))

    pS1 = enter(nc.psum_tensor("pS1", [128, 2048], F32))
    pS2 = enter(nc.psum_tensor("pS2", [128, 1024], F32))
    pAVX = enter(nc.psum_tensor("pAVX", [128, 512], F32))
    pTR = enter(nc.psum_tensor("pTR", [128, 1024], F16))
    pTRbf = pTR[:].bitcast(BF16)

    semnames = ("dC", "dGG", "dP0i", "dP0o", "dP1i", "dP1o", "dP2i", "dP2o",
                "dWU0", "dWU1", "dWU2", "dWA", "dWB",
                "dXI0", "dXI1", "dXR", "dXR1", "dX1S", "dX1S2", "dRDS", "dDEN",
                "dRD2",
                "dRU0", "dRU1", "dBB", "dCC", "dXT0", "dXT1", "dOUT",
                "sy", "te", "ve", "sc", "gp")
    smap = {n: enter(nc.semaphore(n)) for n in semnames}
    P = Prog(nc, smap)

    BD = {}

    def gbuf(n):
        if n not in BD:
            BD[n] = Buf(n)
        return BD[n]

    _prevc = []
    for n in ("cosq", "sinq", "cosk", "sink", "tri", "dmm", "idb",
              "idh", "id32", "hsel"):
        dst = sbc[n] if n == "dmm" else sbc[n][:]
        t = P.dma("sync", "dC", dst, consts_d[n],
                  writes=[gbuf("c_" + n)], waits=tuple(_prevc))
        _prevc = [t]
    for n in biases:
        t = P.dma("sync", "dC", biases[n][:], f32in[n],
                  writes=[gbuf("c_" + n)], waits=tuple(_prevc))
        _prevc = [t]
    P.op("vector", lambda e: e.memset(neg1[:], -1.0), writes=[gbuf("c_neg1")])
    P.op("vector", lambda e: e.memset(cEPS[:], EPS), writes=[gbuf("c_eps")])
    P.op("vector", lambda e: e.memset(cTINY[:], 1e-35), writes=[gbuf("c_tiny")])
    P.op("vector", lambda e: e.memset(ones128[:], 1.0), writes=[gbuf("c_ones")])
    cb = {n: gbuf("c_" + n) for n in
          ("cosq", "sinq", "cosk", "sink", "tri", "dmm", "idb",
           "idh", "id32", "hsel", "neg1",
           "lin0_b", "s1b1", "s1b2", "lin1_b", "s2b1", "s2b2")}
    cb["eps"] = gbuf("c_eps")
    cb["tiny"] = gbuf("c_tiny")
    cb["ones"] = gbuf("c_ones")
    bpinit = gbuf("psum_init")
    P.op("vector", lambda e: e.memset(pS1[:], 0.0), writes=[bpinit])
    P.op("vector", lambda e: e.memset(pS2[:], 0.0), reads=[bpinit])
    P.op("vector", lambda e: e.memset(pAVX[:], 0.0), writes=[bpinit])

    units0 = []
    if mode != "A":
        def conv_mat(src, dst, name):
            R, C = src.shape
            for r0 in range(0, R, 128):
                for c0 in range(0, C, 512):
                    cw = min(512, C - c0)
                    units0.append((src[r0:r0 + 128, c0:c0 + cw],
                                   dst[r0:r0 + 128, c0:c0 + cw], name, cw))

        conv_mat(f32in["lin0_w"], wb["b_lin0"], "b_lin0")
        conv_mat(f32in["s1w1"], wb["b_s1w1"], "b_s1w1")
        conv_mat(f32in["s1w2"], wb["b_s1w2"], "b_s1w2")
        conv_mat(f32in["gateT"], wb["b_gateT"], "b_gateT")
        for ei in range(NE):
            conv_mat(f32in["ew1"][ei], wb["b_ew1"][ei], "b_ew1")
            conv_mat(f32in["ew3"][ei], wb["b_ew3"][ei], "b_ew3")
            conv_mat(f32in["ew2"][ei], wb["b_ew2"][ei], "b_ew2")
        conv_mat(f32in["shw1"], wb["b_shw1"], "b_shw1")
        conv_mat(f32in["shw3"], wb["b_shw3"], "b_shw3")
        conv_mat(f32in["shw2"], wb["b_shw2"], "b_shw2")
        conv_mat(f32in["lin1_w"], wb["b_lin1"], "b_lin1")
        conv_mat(f32in["s2w1"], wb["b_s2w1"], "b_s2w1")
        conv_mat(f32in["s2w2"], wb["b_s2w2"], "b_s2w2")

    w0in = [ar.alloc([512], F32) for _ in range(3)]
    w0out = [ar.alloc([512], BF16) for _ in range(3)]
    b_w0in = [Buf(f"w0in{i}") for i in range(3)]
    b_w0out = [Buf(f"w0out{i}") for i in range(3)]
    u0 = [0]

    def drip_phase0(k):
        for _ in range(k):
            i = u0[0]
            if i >= len(units0):
                return
            u0[0] += 1
            srcap, dstap, dname, C = units0[i]
            p = i % 3
            iv = w0in[p][:, 0:C]
            ov = w0out[p][:, 0:C]
            P.dma("sync", ("dP0i", "dP1i", "dP2i")[p], iv, srcap,
                  writes=[b_w0in[p]])
            P.op("gpsimd",
                 (lambda a, bq: lambda e: e.tensor_copy(bq, a))(iv, ov),
                 reads=[b_w0in[p]], writes=[b_w0out[p]])
            P.dma("sync", ("dP0o", "dP1o", "dP2o")[p], dstap, ov,
                  reads=[b_w0out[p]], writes=[gbuf(dname)])

    markA = ar.mark()
    if mode != "B":
        xT = ar.alloc([8, 2048], F16)
        KrT = ar.alloc([8, 2048], F16)
        Vaug = ar.alloc([NT, NH * 65], F16)
        QrT = ar.alloc([8, 512], F16)
        PTb = ar.alloc([NT, 512], F16)
        mergedT = ar.alloc([8, 512], F16)

        watt = ar.alloc([8, 1024], F16)
        wattB = ar.alloc([8, 1024], F16)
        xin = [ar.alloc([1024], F16)]
        qkrope = [ar.alloc([1024], F16) for _ in range(2)]
        rtmpA = [ar.alloc([512], F16) for _ in range(2)]
        rtmpB = [ar.alloc([512], F16) for _ in range(2)]
        xres = ar.alloc([2, 1024], F16)
        zb = [ar.alloc([1024], F32), None, None, None]
        sqQT = [ar.alloc([8, 128], F16) for _ in range(2)]
        rdenS = ar.alloc([1536], F16, parts=1)
        ones64 = ar.alloc([64], F16, parts=1)
        kn2 = ar.alloc([1], F32, parts=16)
        qn2 = ar.alloc([1], F32, parts=16)
        qk2 = ar.alloc([1], F32, parts=16)
        btmp = ar.alloc([1], F32, parts=16)
        bnegrow = ar.alloc([16], F32, parts=1)
        negb = ar.alloc([16], F32)
        mx16 = ar.alloc([1], F32, parts=16)
        lnrow = [[ar.alloc([1], F32) for _ in range(4)] for _ in range(4)]

        bxT, bKrT, bVaug, bQrT, bPT, bmg = (Buf(n) for n in
            ("xT", "KrT", "Vaug", "QrT", "PT", "mergedT"))

        bwatt = Buf("watt")
        bwattB = Buf("wattB")
        bxin = [Buf("xin0")]
        bqkrope = [Buf("qkrope0"), Buf("qkrope1")]
        brtmpA = [Buf("rtmpA0"), Buf("rtmpA1")]
        brtmpB = [Buf("rtmpB0"), Buf("rtmpB1")]
        bxres = [Buf("xres0"), Buf("xres1")]
        bzb = [Buf(f"zb{i}") for i in range(4)]
        bsqQT = [Buf("sqQT0"), Buf("sqQT1")]
        brdS = [Buf("rdS0"), Buf("rdS1"), Buf("rdS2")]
        bones64 = Buf("ones64")
        bkn2, bqn2, bqk2, bbtmp = Buf("kn2"), Buf("qn2"), Buf("qk2"), Buf("btmp")
        bbnegrow, bnegb, bmx16 = Buf("bnegrow"), Buf("negb"), Buf("mx16")
        blnrow = [[Buf(f"lnrow{q}{i}") for i in range(4)] for q in range(4)]
        bqA, bqB, bqC = Buf("AqA"), Buf("AqB"), Buf("AqC")
        bpAVXb, bpTRb = Buf("ApAVX"), Buf("ApTR")
        for _bq in (bqA, bqB, bqC, bpAVXb):
            _bq.w = gbuf("psum_init").w

        pTR3 = pTR[:].rearrange("p (a b) -> p a b", b=128)
        pTRf32 = pTR[:].bitcast(F32)
        P.op("vector", lambda e: e.memset(ones64[:], 1.0), writes=[bones64])
        zb[1] = PTb[:, 2:6, :].rearrange("p a b -> p (a b)").bitcast(F32)
        zb[2] = PTb[:, 6:10, :].rearrange("p a b -> p (a b)").bitcast(F32)
        zb[3] = PTb[:, 10:14, :].rearrange("p a b -> p (a b)").bitcast(F32)

        def load_watt(name, dst=None, dbuf=None, sem="dWA"):
            return P.dma("scalar", sem, (dst if dst is not None else watt)[:, :, :],
                         att_w[name].rearrange("(k p) f -> p k f", p=128),
                         writes=[dbuf if dbuf is not None else bwatt])

        def rope6(dst, psrc, ct, st, bpsrc, bdst, btabs, par):
            E = psrc.rearrange("p (h j) -> p h j", j=64)[:, :, 0:32]
            O = psrc.rearrange("p (h j) -> p h j", j=64)[:, :, 32:64]
            dE = dst.rearrange("p (h j) -> p h j", j=64)[:, :, 0:32]
            dO = dst.rearrange("p (h j) -> p h j", j=64)[:, :, 32:64]
            ctb = ct.unsqueeze(1).to_broadcast((128, 16, 32))
            stb = st.unsqueeze(1).to_broadcast((128, 16, 32))
            tA = rtmpA[par][:].rearrange("p (h j) -> p h j", j=32)
            tB = rtmpB[par][:].rearrange("p (h j) -> p h j", j=32)
            brA, brB = brtmpA[par], brtmpB[par]
            P.op("vector", lambda e: e.tensor_tensor(tA, E, ctb, op=ALU.mult),
                 reads=[bpsrc] + btabs, writes=[brA])
            P.op("vector", lambda e: e.tensor_tensor(tB, O, stb, op=ALU.mult),
                 reads=[bpsrc], writes=[brB])
            P.op("vector", lambda e: e.tensor_tensor(dE, tA, tB, op=ALU.subtract),
                 reads=[brA, brB], writes=[bdst])
            P.op("vector", lambda e: e.tensor_tensor(tA, E, stb, op=ALU.mult),
                 reads=[bpsrc], writes=[brA])
            P.op("vector", lambda e: e.tensor_tensor(tB, O, ctb, op=ALU.mult),
                 reads=[bpsrc], writes=[brB])
            P.op("vector", lambda e: e.tensor_tensor(dO, tA, tB, op=ALU.add),
                 reads=[brA, brB], writes=[bdst])

        def transp8_evac(srcap, bsrc, dstap, bdst):
            for j in range(8):
                P.op("tensor",
                     (lambda jj: lambda e: e.transpose(
                         pTR[:, ds(jj)], srcap[:, ds(jj)], sbc["idh"][:]))(j),
                     reads=[bsrc, cb["idh"]], writes=([bpTRb] if j == 0 else []),
                     inc=(j == 7))
            P.op("scalar", lambda e: e.copy(dstap, pTR3),
                 reads=[bpTRb], writes=[bdst])

        genPP = None

        def gen_proj(t, w_ap, bw, pr):
            dst, dbuf = genPP[pr]
            for n in range(2):
                for k in range(8):
                    P.op("tensor",
                         (lambda nn, kk, tt, dd: lambda e: e.matmul(
                             dd[:, ds(nn, 512)], xT[:, kk, ds(tt)],
                             w_ap[:, kk, ds(nn, 512)],
                             start=(kk == 0), stop=(kk == 7)))(n, k, t, dst),
                         reads=[bxT, bw],
                         writes=([dbuf] if (n == 0 and k == 0) else []),
                         inc=(n == 1 and k == 7))

        def norm_tile_sq(srcT, bsrcT, t):
            P.op("scalar", (lambda tt: lambda e: e.activation(
                sqQT[tt % 2][:, 0:8, :], srcT[:, :, ds(tt)], AF.Square))(t),
                reads=[bsrcT], writes=[bsqQT[t % 2]])

        def norm_tile_mm(t, dstv, bdstv, first):
            _t = None
            for pt in range(8):
                _t = P.op("tensor", (lambda p_, tt: lambda e: e.matmul(
                    pAVX[0:16, 0:128], sbc["hsel"][:, p_, :],
                    sqQT[tt % 2][:, p_, :], start=(p_ == 0),
                    stop=(p_ == 7)))(pt, t),
                    reads=[bsqQT[t % 2], cb["hsel"]],
                    writes=([bpAVXb] if pt == 0 else []), inc=(pt == 7))
            bpAVXb.w = _t
            P.op("vector", lambda e: e.reduce_max(mx16[:], pAVX[0:16, 0:128],
                                                  axis=AX.X),
                 reads=[bpAVXb], writes=[bmx16])
            if first:
                P.op("vector", (lambda dv: lambda e: e.tensor_copy(
                    dv[:], mx16[:]))(dstv),
                    reads=[bmx16], writes=[bdstv])
            else:
                P.op("vector", (lambda dv: lambda e: e.tensor_tensor(
                    dv[:], dv[:], mx16[:], op=ALU.max))(dstv),
                    reads=[bmx16, bdstv], writes=[bdstv])

        genPP = [(pS2[:], bqA), (pS1[:, 0:1024], bqB)]
        SCB = [(pS2[:, 0:1024], bqA), (pS1[:, 0:1024], bqB),
               (pS1[:, 1024:2048], bqC)]

        deferred_ln = []

        def flush_ln():
            for f in deferred_ln:
                f()
            deferred_ln.clear()

        for b in range(LIMIT_ASEQS or NSEQ):
            load_watt("kwp")
            load_watt("ow", dst=wattB, dbuf=bwattB, sem="dWB")
            for t in range(NT + 3):
                if t >= 3:
                    tn = t - 3
                    norm_tile_mm(tn, kn2, bkn2, tn == 0)
                if 2 <= t < NT + 2:
                    tp0 = t - 2
                    transp8_evac(qkrope[tp0 % 2][:], bqkrope[tp0 % 2],
                                 KrT[:, :, ds(tp0)], bKrT)
                    norm_tile_sq(KrT, bKrT, tp0)
                if t < NT:
                    p = t % 2
                    P.dma("scalar", "dXI0", xin[0][:],
                          x_d[b, ds(t), :], writes=[bxin[0]])
                    transp8_evac(xin[0][:], bxin[0], xT[:, :, ds(t)], bxT)
                    prg = t % 2
                    gen_proj(t, watt, bwatt, prg)
                    rope6(qkrope[t % 2][:], genPP[prg][0], sbc["cosk"][:, t, :],
                          sbc["sink"][:, t, :], genPP[prg][1], bqkrope[t % 2],
                          [cb["cosk"], cb["sink"]], t % 2)
                drip_phase0(6)
            load_watt("vw")
            P.op("vector", lambda e: e.memset(
                Vaug[:, :, :].rearrange("p t (h c) -> p t h c", c=65)
                [:, :, :, 64:65].squeeze(3), 1.0), writes=[bVaug])
            for t in range(NT):
                prg = t % 2
                gen_proj(t, watt, bwatt, prg)
                P.op("vector", (lambda tt, dd: lambda e: e.tensor_copy(
                    Vaug[:, tt, :].rearrange("p (h c) -> p h c", c=65)[:, :, 0:64],
                    dd.rearrange("p (h c) -> p h c", c=64)))(t, genPP[prg][0]),
                    reads=[genPP[prg][1]], writes=[bVaug])
                drip_phase0(6)

            load_watt("qwp")
            for c in range(NCHK):
                for ts4 in range(7):
                    if ts4 >= 3:
                        tq = ts4 - 3
                        norm_tile_mm(tq, qn2, bqn2, tq == 0)
                    if 2 <= ts4 < 6:
                        tq = ts4 - 2
                        transp8_evac(qkrope[tq % 2][:], bqkrope[tq % 2],
                                     QrT[:, :, ds(tq)], bQrT)
                        norm_tile_sq(QrT, bQrT, tq)
                    if ts4 < 4:
                        t = c * 4 + ts4
                        p = ts4 % 2
                        prg = ts4 % 2
                        gen_proj(t, watt, bwatt, prg)
                        rope6(qkrope[ts4 % 2][:], genPP[prg][0],
                              sbc["cosq"][:, t, :],
                              sbc["sinq"][:, t, :], genPP[prg][1],
                              bqkrope[ts4 % 2],
                              [cb["cosq"], cb["sinq"]], ts4 % 2)
                drip_phase0(2)
                P.op("vector", lambda e: e.tensor_tensor(
                    qk2[:], qn2[:], kn2[:], op=ALU.mult),
                    reads=[bqn2, bkn2], writes=[bqk2])
                P.op("scalar", lambda e: e.activation(
                    btmp[:], qk2[:], AF.Ln, bias=cTINY[0:16, :]),
                    reads=[bqk2, cb["tiny"]], writes=[bbtmp])
                P.op("scalar", lambda e: e.activation(
                    btmp[:], btmp[:], AF.Exp, scale=0.5),
                    reads=[bbtmp], writes=[bbtmp])
                P.op("vector", lambda e: e.tensor_scalar(
                    btmp[:], btmp[:], -1.0, None, op0=ALU.mult),
                    reads=[bbtmp], writes=[bbtmp])
                P.op("tensor", lambda e: e.transpose(
                    pS2[0:1, 0:16], btmp[:], sbc["id32"][0:16, 0:16]),
                    reads=[bbtmp, cb["id32"]], writes=[bqA])
                P.op("vector", lambda e: e.tensor_copy(bnegrow[:], pS2[0:1, 0:16]),
                     reads=[bqA], writes=[bbnegrow])
                P.op("tensor", lambda e: e.matmul(
                    pAVX[:, 0:16], ones128[:], bnegrow[:], start=True, stop=True),
                    reads=[bbnegrow, cb["ones"]], writes=[bpAVXb])
                P.op("vector", lambda e: e.tensor_copy(negb[:], pAVX[:, 0:16]),
                     reads=[bpAVXb], writes=[bnegb])

                flush_ln()
                ktiles = 4 * (c + 1)
                groups = [(gi % 3, 2 * gi, 2) for gi in range(ktiles // 2)]

                def emit_fix(hh):
                    uu, rr = hh // 2, (hh % 2) * 64
                    off = (hh % 3) * 512
                    P.op("tensor", (lambda of: lambda e: e.matmul(
                        pTRf32[0:64, 0:512], ones64[0:1, 0:64],
                        rdenS[0:1, of:of + 512], start=True, stop=True))(off),
                        reads=[brdS[hh % 3], bones64], writes=[bpTRb])
                    P.op("vector", (lambda uu_, rr_: lambda e: e.tensor_tensor(
                        mergedT[rr_:rr_ + 64, uu_, :],
                        mergedT[rr_:rr_ + 64, uu_, :],
                        pTRf32[0:64, 0:512], op=ALU.mult))(uu, rr),
                        reads=[bpTRb], writes=[bmg])

                av_last = None
                avtok = {}
                for h in range(NH):
                    u, r0 = h // 2, (h % 2) * 64
                    gtoks = {}

                    def emit_scores_exp(gidx):
                        gsel, gk0, gsz = groups[gidx]
                        pbuf, pB = SCB[gsel]
                        for i in range(gsz):
                            kj = gk0 + i
                            r = kj - 4 * c
                            off = 128 * r if r >= 0 else 0
                            P.op("tensor",
                                 (lambda pb, ii, kk, of, uu, rr: lambda e: e.matmul(
                                     pb[:, ii * 512 + of:(ii + 1) * 512],
                                     KrT[rr:rr + 64, uu, ds(kk)],
                                     QrT[rr:rr + 64, uu, of:512],
                                     start=True, stop=True))(pbuf, i, kj, off, u, r0),
                                 reads=[bKrT, bQrT],
                                 writes=([pB] if i == 0 else []),
                                 inc=(i == gsz - 1))
                        zjs = sorted({(t - 2) // 4 + 1
                                      for t in range(gk0, gk0 + gsz)
                                      if 2 <= t <= 13})
                        avw = [t for t in (avtok.get(gk0 + gsz - 1),)
                               if t is not None]
                        texp = P.op("scalar",
                                    (lambda pb, kk0, sz_, hh: lambda e: e.activation(
                                        PTb[:, kk0:kk0 + sz_, :], pb[:, 0:sz_ * 512],
                                        AF.Exp, bias=negb[:, hh:hh + 1]))(
                                            pbuf, gk0, gsz, h),
                                    reads=[pB, bnegb], waits=avw,
                                    writes=[bPT] + [bzb[j] for j in zjs])
                        r0g = gk0 - 4 * c
                        if r0g >= 0:
                            tm = None
                            for ii in range(2):
                                rr_ = r0g + ii
                                tm = P.op("vector",
                                          (lambda kk, st: lambda e:
                                           e.tensor_tensor(
                                              PTb[:, kk, :], PTb[:, kk, :],
                                              sbc["dmm"][:, st:st + 512],
                                              op=ALU.mult))(
                                                  gk0 + ii, 384 - 128 * rr_),
                                          reads=[cb["dmm"]], writes=[bPT])
                                gtoks[gk0 + ii] = tm
                        else:
                            gtoks[gk0] = texp
                            gtoks[gk0 + 1] = texp

                    def emit_av_group(gidx):
                        nonlocal av_last
                        _, gk0, gsz = groups[gidx]
                        for kj in range(gk0, gk0 + gsz):
                            av_last = P.op(
                                "tensor",
                                (lambda kk, hh, kt: lambda e: e.matmul(
                                    pAVX[0:65, :],
                                    Vaug[:, kk, hh * 65:(hh + 1) * 65],
                                    PTb[:, kk, :], start=(kk == 0),
                                    stop=(kk == kt - 1)))(kj, h, ktiles),
                                reads=[bVaug], waits=[gtoks[kj]],
                                writes=([bpAVXb] if kj == 0 else []),
                                inc=True)
                            avtok[kj] = av_last

                    ng = len(groups)
                    for gi in range(ng):
                        emit_scores_exp(gi)
                    if h >= 2:
                        emit_fix(h - 2)
                    for gi in range(ng):
                        emit_av_group(gi)
                    bpAVXb.w = av_last
                    drip_phase0(1)
                    P.op("vector", (lambda uu, rr: lambda e: e.tensor_copy(
                        mergedT[rr:rr + 64, uu, :], pAVX[0:64, :]))(u, r0),
                        reads=[bpAVXb], writes=[bmg])
                    def _recip(of):
                        def g(e):
                            with nc.allow_low_precision(reason="f16 recip den"):
                                return e.reciprocal(rdenS[0:1, of:of + 512],
                                                    pAVX[64:65, 0:512])
                        return g
                    P.op("vector", _recip((h % 3) * 512),
                         reads=[bpAVXb], writes=[brdS[h % 3]])
                emit_fix(NH - 2)
                emit_fix(NH - 1)
                for ts4 in range(4):
                    pz = ts4
                    half = ts4 % 2
                    P.dma("scalar", "dXR" if ts4 % 2 == 0 else "dXR1",
                          xres[:, ts4 % 2, :],
                          x_d[b, c * 512 + ts4 * 128:c * 512 + (ts4 + 1) * 128, :],
                          writes=[bxres[ts4 % 2]])
                    for n in range(2):
                        for k in range(8):
                            P.op("tensor",
                                 (lambda hh, nn, kk, tt: lambda e: e.matmul(
                                     pS1[:, hh * 1024 + nn * 512:
                                         hh * 1024 + (nn + 1) * 512],
                                     mergedT[:, kk, ds(tt)],
                                     wattB[:, kk, ds(nn, 512)],
                                     start=(kk == 0), stop=(kk == 7)))(half, n, k, ts4),
                                 reads=[bmg, bwattB],
                                 writes=([bqB if half == 0 else bqC]
                                         if (n == 0 and k == 0) else []),
                                 inc=(n == 1 and k == 7))
                    zv, bzv = zb[pz], bzb[pz]
                    alias = [bPT] if pz > 0 else []
                    lr, blr = lnrow[pz], blnrow[pz]
                    P.op("vector", (lambda hh, zz, pzz: lambda e: e.tensor_tensor(
                        zz[:], pS1[:, hh * 1024:(hh + 1) * 1024],
                        xres[:, pzz, :], op=ALU.add))(half, zv, ts4 % 2),
                        reads=[bqB if half == 0 else bqC, bxres[ts4 % 2]],
                        writes=[bzv] + alias)
                    def _ln_tail(zv=zv, bzv=bzv, lr=lr, blr=blr, alias=alias,
                                 pz=pz, row0=b * SEQ + c * 512 + ts4 * 128):
                        P.op("vector", (lambda zz, l0: lambda e: e.reduce_sum(
                            l0[:], zz[:], axis=AX.X))(zv, lr[0]),
                            reads=[bzv], writes=[blr[0]])
                        P.op("vector", (lambda l0, l1: lambda e: e.tensor_scalar(
                            l1[:], l0[:], 1.0 / D, None, op0=ALU.mult))(
                                lr[0], lr[1]),
                            reads=[blr[0]], writes=[blr[1]])
                        P.op("vector", (lambda zz, l1: lambda e: e.tensor_scalar(
                            zz[:], zz[:], l1[:], None,
                            op0=ALU.subtract))(zv, lr[1]),
                            reads=[blr[1]], writes=[bzv])
                        P.op("scalar", (lambda zz, l2: lambda e: e.activation(
                            PTb[:, 14:16, :].rearrange("p a b -> p (a b)"), zz[:],
                            AF.Square, accum_out=l2[:]))(zv, lr[2]),
                            reads=[bzv], writes=[bPT, blr[2]])
                        P.op("scalar", (lambda l2, l3: lambda e: e.activation(
                            l3[:], l2[:], AF.Ln, scale=1.0 / D,
                            bias=cEPS[:]))(lr[2], lr[3]),
                            reads=[blr[2], cb["eps"]], writes=[blr[3]])
                        P.op("scalar", (lambda l3: lambda e: e.activation(
                            l3[:], l3[:], AF.Exp, scale=-0.5))(lr[3]),
                            reads=[blr[3]], writes=[blr[3]])
                        P.op("vector", (lambda zz, l3: lambda e: e.tensor_scalar(
                            zz[:], zz[:], l3[:], None, op0=ALU.mult))(zv, lr[3]),
                            reads=[blr[3]], writes=[bzv] + alias)
                        P.dma("scalar", "dX1S" if pz % 2 == 0 else "dX1S2",
                              x1s[row0:row0 + 128, :], zv[:], reads=[bzv],
                              writes=[gbuf("x1s0" if pz % 2 == 0 else "x1s1")])
                    deferred_ln.append(_ln_tail)
                drip_phase0(6)

    if mode != "B":
        flush_ln()
    if mode != "A":
        drip_phase0(len(units0))
        P.dma("sync", "dGG", ggT[:], wb["b_gateT"].rearrange("(k p) f -> p k f", p=128),
              reads=[gbuf("b_gateT")], writes=[gbuf("ggT")])
        fenceA = P.fence()
        ar.reset(markA)
        x1tm = [ar.alloc([4, 1024], F32) for _ in range(2)]
        x1T = ar.alloc([8, 512], BF16)
        h0T = ar.alloc([8, 512], BF16)
        h1T = ar.alloc([8, 512], BF16)
        actT = [ar.alloc([8, 512], BF16) for _ in range(2)]
        spT = ar.alloc([16, 512], BF16)
        routed = ar.alloc([8, 512], F32)
        c_bc = ar.alloc([NE, 512], BF16)
        cT = ar.alloc([512], BF16, parts=8)
        gexp = ar.alloc([4 * NE], F32)
        workg = [ar.alloc([4 * NE], F32) for _ in range(2)]
        selg = ar.alloc([4 * NE], F32)
        gtmp = ar.alloc([4 * NE], F32)
        mxg = ar.alloc([4], F32)
        t4g = ar.alloc([4], F32)
        accg = ar.alloc([4], F32)
        rivg = ar.alloc([4], F32)
        combine = ar.alloc([4, NE], BF16)
        zsqB = ar.alloc([1024], BF16)
        lnB4 = [[ar.alloc([1], F32) for _ in range(4)] for _ in range(4)]
        wsl = [ar.alloc([8, 1024], BF16) for _ in range(3)]

        bx1tm = [Buf(f"x1tm{i}") for i in range(2)]
        bx1b = None
        bx1T, bh0, bh1, bsp, brt = (Buf(n) for n in
            ("x1T", "h0T", "h1T", "spT", "routed"))
        bact = [Buf(f"actT{i}") for i in range(2)]
        bcbc, bcT, bgexp, bselg, bmxg, bt4g, baccg, brivg, bcomb = (
            Buf(n) for n in ("c_bc", "cT", "gexp", "selg", "mxg", "t4g",
                             "accg", "rivg", "comb"))
        bgtmp = Buf("gtmp")
        bworkg = [Buf(f"workg{i}") for i in range(2)]
        bzsqB = Buf("zsqB")
        blnB4 = [[Buf(f"lnB{t}{i}") for i in range(4)] for t in range(4)]
        bwsl = [Buf(f"wsl{i}") for i in range(3)]
        bPP = [[Buf("Bp0a"), Buf("Bp0b")], [Buf("Bp1a"), Buf("Bp1b")]]
        bpT1 = Buf("BpT1")
        bpAVXb2 = Buf("BpAVX")
        bpTRb2 = Buf("BpTR")
        for bb in ([bx1tm[0], bx1tm[1], bx1T, bh0, bh1, bsp, brt,
                    bcbc, bcT, bgexp, bselg, bmxg, bt4g, baccg, brivg, bcomb,
                    bzsqB, bPP[0][0], bPP[0][1], bPP[1][0], bPP[1][1], bpT1,
                    bpAVXb2, bpTRb2] + bact + bworkg + bwsl
                   + [b for bs in blnB4 for b in bs]):
            bb.r = list(fenceA)
        fwaits = fenceA

        deferB = []

        def flushB(k):
            if deferB:
                deferB[0][k]()
                if k == 3:
                    deferB.pop(0)

        NU = 36

        def unit_src(name, sub):
            t = wb[name]
            if name in ("b_ew1", "b_ew3", "b_ew2"):
                return t[sub].rearrange("(k p) f -> p k f", p=128)
            if name in ("b_shw1", "b_shw3"):
                return t[:, sub * 1024:(sub + 1) * 1024].rearrange(
                    "(k p) f -> p k f", p=128)
            if name == "b_shw2":
                return t[sub * 1024:(sub + 1) * 1024, :].rearrange(
                    "(k p) f -> p k f", p=128)
            return t.rearrange("(k p) f -> p k f", p=128)

        pTR3b = pTR[:].rearrange("p (a b) -> p a b", b=128)
        gv = lambda ap: ap.rearrange("p (a b) -> p a b", b=NE)
        PP = [pS2[:], pS1[:, 0:1024]]

        def emit_x1_load(cc):
            pcc = cc % 2
            P.dma("gpsimd", "dXT0" if pcc == 0 else "dXT1", x1tm[pcc][:, :, :],
                  x1s[cc * 512:(cc + 1) * 512, :].rearrange(
                      "(t p) f -> p t f", p=128),
                  reads=[gbuf("x1s0"), gbuf("x1s1")], writes=[bx1tm[pcc]],
                  waits=(fwaits if cc < 2 else ()))

        NCHB = LIMIT_BCHUNKS or NCH_B
        for c in range(NCHB):
            pc = c % 2
            u_ctr = [0]

            def next_unit(name, sub=None):
                ui = u_ctr[0]
                u_ctr[0] += 1
                sl = (c * NU + ui) % 3
                tok = P.dma("sync", f"dWU{sl}", wsl[sl][:, :, :],
                            unit_src(name, sub),
                            reads=[gbuf(name)], writes=[bwsl[sl]],
                            waits=(fwaits if c == 0 and ui < 3 else ()))
                return sl, tok

            def mm_block(pr, half, w_sl, m, rhs_ap, rhs_buf):
                for k in range(8):
                    P.op("tensor",
                         (lambda kk, mm_, hh, ws, dd: lambda e: e.matmul(
                             dd[:, ds(hh, 512)], wsl[ws][:, kk, ds(mm_)],
                             rhs_ap[:, kk, :], start=(kk == 0),
                             stop=(kk == 7)))(k, m, half, w_sl, PP[pr]),
                         reads=[bwsl[w_sl], rhs_buf],
                         writes=([bPP[pr][half]] if k == 0 else []),
                         inc=(k == 7))

            if c == 0:
                emit_x1_load(0)
            for ts4 in range(4):
                pr = ts4 % 2
                for j in range(8):
                    P.op("tensor",
                         (lambda tt, jj, pp, dd: lambda e: e.transpose(
                             dd[:, jj * 128:(jj + 1) * 128],
                             x1tm[pp][:, tt, ds(jj)], sbc["id32"][:]))(
                                 ts4, j, pc, PP[pr]),
                         reads=[bx1tm[pc], cb["id32"]],
                         writes=([bPP[pr][0], bPP[pr][1]] if j == 0 else []),
                         inc=(j == 7),
                         waits=(fwaits if c == 0 and ts4 == 0 and j == 0 else ()))
                P.op("vector", (lambda tt, dd: lambda e: e.tensor_copy(
                    x1T[:, :, ds(tt)],
                    dd.rearrange("p (a b) -> p a b", b=128)))(ts4, PP[pr]),
                    reads=[bPP[pr][0], bPP[pr][1]], writes=[bx1T],
                    waits=(fwaits if c == 0 and ts4 == 0 else ()))
            flushB(0)

            sl, _ = next_unit("b_lin0")
            for mp in range(4):
                pr = mp % 2
                for m2 in range(2):
                    mm_block(pr, m2, sl, 2 * mp + m2, x1T, bx1T)
                for m2 in range(2):
                    m = 2 * mp + m2
                    P.op("vector", (lambda mm_, m2_, dd: lambda e: e.tensor_scalar(
                        h0T[:, mm_, :], dd[:, ds(m2_, 512)],
                        biases["lin0_b"][:, mm_:mm_ + 1], None, op0=ALU.add))(
                            m, m2, PP[pr]),
                        reads=[bPP[pr][m2], cb["lin0_b"]], writes=[bh0])
            flushB(1)
            sl, _ = next_unit("b_s1w1")
            for mp in range(4):
                pr = mp % 2
                for m2 in range(2):
                    mm_block(pr, m2, sl, 2 * mp + m2, h0T, bh0)
                for m2 in range(2):
                    m = 2 * mp + m2
                    P.op("scalar", (lambda mm_, m2_, dd: lambda e: e.activation(
                        actT[0][:, mm_, :], dd[:, ds(m2_, 512)], AF.Silu,
                        bias=biases["s1b1"][:, mm_:mm_ + 1]))(m, m2, PP[pr]),
                        reads=[bPP[pr][m2], cb["s1b1"]], writes=[bact[0]])
            sl, _ = next_unit("b_s1w2")
            for mp in range(4):
                pr = mp % 2
                for m2 in range(2):
                    mm_block(pr, m2, sl, 2 * mp + m2, h0T, bh0)
                for m2 in range(2):
                    m = 2 * mp + m2
                    P.op("vector", (lambda mm_, m2_, dd: lambda e:
                         e.scalar_tensor_tensor(
                        h1T[:, mm_, :], dd[:, ds(m2_, 512)],
                        biases["s1b2"][:, mm_:mm_ + 1], actT[0][:, mm_, :],
                        op0=ALU.add, op1=ALU.mult))(m, m2, PP[pr]),
                        reads=[bPP[pr][m2], cb["s1b2"], bact[0]], writes=[bh1])
            flushB(2)
            for ts4 in range(4):
                for k in range(8):
                    P.op("tensor",
                         (lambda tt, kk: lambda e: e.matmul(
                             pAVX[:, tt * NE:(tt + 1) * NE], h1T[:, kk, ds(tt)],
                             ggT[:, kk, :], start=(kk == 0), stop=(kk == 7)))(ts4, k),
                         reads=[bh1, gbuf("ggT")],
                         writes=([bpAVXb2] if (ts4 == 0 and k == 0) else []),
                         inc=(k == 7))
            P.op("scalar", lambda e: e.activation(gexp[:], pAVX[:, 0:4 * NE], AF.Exp),
                 reads=[bpAVXb2], writes=[bgexp])
            P.op("vector", lambda e: e.tensor_copy(workg[0][:], gexp[:]),
                 reads=[bgexp], writes=[bworkg[0]])
            for i in range(4):
                src, bsrc = workg[i % 2], bworkg[i % 2]
                dst, bdst = workg[(i + 1) % 2], bworkg[(i + 1) % 2]
                mdst, bmdst = (t4g, bt4g) if i == 3 else (mxg, bmxg)
                P.op("vector", (lambda s_, md: lambda e: e.reduce_max(
                    md[:], gv(s_[:]), axis=AX.X))(src, mdst),
                    reads=[bsrc], writes=[bmdst])
                if i == 0:
                    P.op("vector", lambda e: e.tensor_copy(accg[:], mxg[:]),
                         reads=[bmxg], writes=[baccg])
                else:
                    P.op("vector", (lambda md: lambda e: e.tensor_tensor(
                        accg[:], accg[:], md[:], op=ALU.add))(mdst),
                        reads=[bmdst, baccg], writes=[baccg])
                if i < 3:
                    P.op("vector", (lambda s_: lambda e: e.tensor_tensor(
                        gv(selg[:]), gv(s_[:]),
                        mxg[:].unsqueeze(2).to_broadcast((128, 4, NE)),
                        op=ALU.is_ge))(src),
                        reads=[bsrc, bmxg], writes=[bselg])
                    P.op("vector", (lambda s_: lambda e: e.scalar_tensor_tensor(
                        gtmp[:], s_[:], 1.0, selg[:], op0=ALU.add,
                        op1=ALU.mult))(src),
                        reads=[bsrc, bselg], writes=[bgtmp])
                    P.op("vector", (lambda s_, d_: lambda e: e.tensor_tensor(
                        d_[:], s_[:], gtmp[:], op=ALU.subtract))(src, dst),
                        reads=[bsrc, bgtmp], writes=[bdst])
            P.op("vector", lambda e: e.reciprocal(rivg[:], accg[:]),
                 reads=[baccg], writes=[brivg])
            P.op("vector", lambda e: e.tensor_tensor(
                gv(selg[:]), gv(gexp[:]),
                t4g[:].unsqueeze(2).to_broadcast((128, 4, NE)), op=ALU.is_ge),
                reads=[bgexp, bt4g], writes=[bselg])
            P.op("vector", lambda e: e.tensor_tensor(
                gv(gexp[:]), gv(gexp[:]),
                rivg[:].unsqueeze(2).to_broadcast((128, 4, NE)), op=ALU.mult),
                reads=[brivg, bgexp], writes=[bgexp])
            P.op("vector", lambda e: e.tensor_tensor(
                combine[:].rearrange("p a b -> p (a b)"), gexp[:], selg[:],
                op=ALU.mult), reads=[bgexp, bselg], writes=[bcomb])

            def emit_combine_bcast():
                for ts4 in range(4):
                    P.op("tensor", (lambda tt: lambda e: e.transpose(
                        pTRbf[0:NE, ds(tt)], combine[:, tt, :], sbc["idb"][:]))(ts4),
                        reads=[bcomb, cb["idb"]],
                        writes=([bpTRb2] if ts4 == 0 else []), inc=(ts4 == 3))
                P.op("vector", lambda e: e.tensor_copy(cT[:], pTRbf[0:NE, 0:512]),
                     reads=[bpTRb2], writes=[bcT])
                P.dma("gpsimd", "dCC", ccs, cT[:], reads=[bcT],
                      writes=[gbuf("ccs")])
                P.dma("gpsimd", "dCC", c_bc[:].rearrange("p a b -> p (a b)"),
                      ccs.rearrange("a b -> (a b)").unsqueeze(0)
                      .to_broadcast((128, NE * 512)),
                      reads=[gbuf("ccs")], writes=[bcbc])
            flushB(3)
            if c + 1 < NCHB:
                emit_x1_load(c + 1)
            rflat = routed[:].rearrange("p a b -> p (a b)")

            def emit_gu(e_i):
                pe = e_i % 2
                aflat = actT[pe][:].rearrange("p a b -> p (a b)")
                sl1, _ = next_unit("b_ew1", e_i)
                for mp in range(4):
                    pr = mp % 2
                    for m2 in range(2):
                        mm_block(pr, m2, sl1, 2 * mp + m2, h1T, bh1)
                    P.op("scalar", (lambda mp_, fl, dd: lambda e: e.activation(
                        fl[:, mp_ * 1024:(mp_ + 1) * 1024], dd, AF.Silu))(
                            mp, aflat, PP[pr]),
                        reads=[bPP[pr][0], bPP[pr][1]], writes=[bact[pe]])
                sl3, _ = next_unit("b_ew3", e_i)
                for mp in range(4):
                    pr = mp % 2
                    for m2 in range(2):
                        mm_block(pr, m2, sl3, 2 * mp + m2, h1T, bh1)
                    P.op("vector", (lambda mp_, fl, dd: lambda e: e.tensor_tensor(
                        fl[:, mp_ * 1024:(mp_ + 1) * 1024],
                        fl[:, mp_ * 1024:(mp_ + 1) * 1024], dd,
                        op=ALU.mult))(mp, aflat, PP[pr]),
                        reads=[bPP[pr][0], bPP[pr][1], bact[pe]], writes=[bact[pe]])

            def emit_fold(e_i):
                pe = e_i % 2
                P.op("vector", (lambda e_, pp: lambda e: e.tensor_tensor(
                    actT[pp][:], actT[pp][:],
                    c_bc[:, e_, :].unsqueeze(1).to_broadcast((128, 8, 512)),
                    op=ALU.mult))(e_i, pe),
                    reads=[bcbc, bact[pe]], writes=[bact[pe]])

            def emit_y(e_i):
                pe = e_i % 2
                sl2, _ = next_unit("b_ew2", e_i)
                for mp in range(4):
                    pr = mp % 2
                    for m2 in range(2):
                        mm_block(pr, m2, sl2, 2 * mp + m2, actT[pe], bact[pe])
                    if e_i == 0:
                        P.op("vector", (lambda mp_, dd: lambda e: e.tensor_copy(
                            rflat[:, mp_ * 1024:(mp_ + 1) * 1024], dd))(
                                mp, PP[pr]),
                            reads=[bPP[pr][0], bPP[pr][1]], writes=[brt])
                    else:
                        P.op("vector", (lambda mp_, dd: lambda e: e.tensor_tensor(
                            rflat[:, mp_ * 1024:(mp_ + 1) * 1024],
                            rflat[:, mp_ * 1024:(mp_ + 1) * 1024], dd,
                            op=ALU.add))(mp, PP[pr]),
                            reads=[bPP[pr][0], bPP[pr][1], brt], writes=[brt])

            emit_gu(0)
            emit_combine_bcast()
            emit_fold(0)
            for e_i in range(1, NE):
                emit_gu(e_i)
                emit_fold(e_i)
                emit_y(e_i - 1)
            emit_y(NE - 1)
            spflat = spT[:].rearrange("p a b -> p (a b)")
            for half in range(2):
                sl, _ = next_unit("b_shw1", half)
                for mp in range(4):
                    pr = mp % 2
                    for m2 in range(2):
                        mm_block(pr, m2, sl, 2 * mp + m2, h1T, bh1)
                    P.op("scalar", (lambda hh, mp_, dd: lambda e: e.activation(
                        spflat[:, hh * 4096 + mp_ * 1024:
                               hh * 4096 + (mp_ + 1) * 1024],
                        dd, AF.Silu))(half, mp, PP[pr]),
                        reads=[bPP[pr][0], bPP[pr][1]], writes=[bsp])
            for half in range(2):
                sl, _ = next_unit("b_shw3", half)
                for mp in range(4):
                    pr = mp % 2
                    for m2 in range(2):
                        mm_block(pr, m2, sl, 2 * mp + m2, h1T, bh1)
                    P.op("vector", (lambda hh, mp_, dd: lambda e: e.tensor_tensor(
                        spflat[:, hh * 4096 + mp_ * 1024:
                               hh * 4096 + (mp_ + 1) * 1024],
                        spflat[:, hh * 4096 + mp_ * 1024:
                               hh * 4096 + (mp_ + 1) * 1024],
                        dd, op=ALU.mult))(half, mp, PP[pr]),
                        reads=[bPP[pr][0], bPP[pr][1], bsp], writes=[bsp])
            slA, _ = next_unit("b_shw2", 0)
            slB, _ = next_unit("b_shw2", 1)
            for mp in range(4):
                pr = mp % 2
                for m2 in range(2):
                    m = 2 * mp + m2
                    for k16 in range(16):
                        slx = slA if k16 < 8 else slB
                        P.op("tensor",
                             (lambda kk, mm_, hh, ws, dd: lambda e: e.matmul(
                                 dd[:, ds(hh, 512)], wsl[ws][:, kk % 8, ds(mm_)],
                                 spT[:, kk, :], start=(kk == 0),
                                 stop=(kk == 15)))(k16, m, m2, slx, PP[pr]),
                             reads=[bwsl[slx], bsp],
                             writes=([bPP[pr][m2]] if k16 == 0 else []),
                             inc=(k16 == 15))
                for m2 in range(2):
                    m = 2 * mp + m2
                    P.op("vector", (lambda mm_, m2_, dd: lambda e:
                         e.scalar_tensor_tensor(
                        h0T[:, mm_, :], dd[:, ds(m2_, 512)], 1.0,
                        routed[:, mm_, :], op0=ALU.mult, op1=ALU.add))(
                            m, m2, PP[pr]),
                        reads=[bPP[pr][m2], brt], writes=[bh0])
            sl, _ = next_unit("b_lin1")
            for mp in range(4):
                pr = mp % 2
                for m2 in range(2):
                    mm_block(pr, m2, sl, 2 * mp + m2, h0T, bh0)
                for m2 in range(2):
                    m = 2 * mp + m2
                    P.op("vector", (lambda mm_, m2_, dd: lambda e: e.tensor_scalar(
                        h1T[:, mm_, :], dd[:, ds(m2_, 512)],
                        biases["lin1_b"][:, mm_:mm_ + 1], None, op0=ALU.add))(
                            m, m2, PP[pr]),
                        reads=[bPP[pr][m2], cb["lin1_b"]], writes=[bh1])
            sl, _ = next_unit("b_s2w1")
            for mp in range(4):
                pr = mp % 2
                for m2 in range(2):
                    mm_block(pr, m2, sl, 2 * mp + m2, h1T, bh1)
                for m2 in range(2):
                    m = 2 * mp + m2
                    P.op("scalar", (lambda mm_, m2_, dd: lambda e: e.activation(
                        actT[0][:, mm_, :], dd[:, ds(m2_, 512)], AF.Silu,
                        bias=biases["s2b1"][:, mm_:mm_ + 1]))(m, m2, PP[pr]),
                        reads=[bPP[pr][m2], cb["s2b1"]], writes=[bact[0]])
            sl, _ = next_unit("b_s2w2")
            for mp in range(4):
                pr = mp % 2
                for m2 in range(2):
                    mm_block(pr, m2, sl, 2 * mp + m2, h1T, bh1)
                for m2 in range(2):
                    m = 2 * mp + m2
                    P.op("vector", (lambda mm_, m2_, dd: lambda e:
                         e.scalar_tensor_tensor(
                        routed[:, mm_, :], dd[:, ds(m2_, 512)],
                        biases["s2b2"][:, mm_:mm_ + 1], actT[0][:, mm_, :],
                        op0=ALU.add, op1=ALU.mult))(m, m2, PP[pr]),
                        reads=[bPP[pr][m2], cb["s2b2"], bact[0]], writes=[brt])
            for ts4 in range(4):
                for j in range(8):
                    P.op("tensor",
                         (lambda tt, jj: lambda e: e.transpose(
                             pS1[:, 1024 + jj * 128:1024 + (jj + 1) * 128],
                             routed[:, jj, ds(tt)], sbc["id32"][:]))(ts4, j),
                         reads=[brt, cb["id32"]],
                         writes=([bpT1] if j == 0 else []), inc=(j == 7))
                P.op("vector", (lambda tt, pp: lambda e: e.tensor_tensor(
                    x1tm[pp][:, tt, :], pS1[:, 1024:2048],
                    x1tm[pp][:, tt, :], op=ALU.add))(ts4, pc),
                    reads=[bpT1, bx1tm[pc]], writes=[bx1tm[pc]])

            def _s0(pc=pc):
                for t4 in range(4):
                    P.op("vector", (lambda tt, pp, l0: lambda e: e.reduce_sum(
                        l0[:], x1tm[pp][:, tt, :], axis=AX.X))(
                            t4, pc, lnB4[t4][0]),
                        reads=[bx1tm[pc]], writes=[blnB4[t4][0]])
                    P.op("vector", (lambda l0, l1: lambda e: e.tensor_scalar(
                        l1[:], l0[:], 1.0 / D, None, op0=ALU.mult))(
                            lnB4[t4][0], lnB4[t4][1]),
                        reads=[blnB4[t4][0]], writes=[blnB4[t4][1]])

            def _s1(pc=pc):
                for t4 in range(4):
                    P.op("vector", (lambda tt, pp, l1: lambda e: e.tensor_scalar(
                        x1tm[pp][:, tt, :], x1tm[pp][:, tt, :], l1[:], None,
                        op0=ALU.subtract))(t4, pc, lnB4[t4][1]),
                        reads=[blnB4[t4][1], bx1tm[pc]], writes=[bx1tm[pc]])

            def _s2(pc=pc):
                for t4 in range(4):
                    P.op("scalar", (lambda tt, pp, l2: lambda e: e.activation(
                        zsqB[:], x1tm[pp][:, tt, :], AF.Square,
                        accum_out=l2[:]))(t4, pc, lnB4[t4][2]),
                        reads=[bx1tm[pc]], writes=[bzsqB, blnB4[t4][2]])
                    P.op("scalar", (lambda l2, l3: lambda e: e.activation(
                        l3[:], l2[:], AF.Ln, scale=1.0 / D, bias=cEPS[:]))(
                            lnB4[t4][2], lnB4[t4][3]),
                        reads=[blnB4[t4][2], cb["eps"]], writes=[blnB4[t4][3]])
                    P.op("scalar", (lambda l3: lambda e: e.activation(
                        l3[:], l3[:], AF.Exp, scale=-0.5))(lnB4[t4][3]),
                        reads=[blnB4[t4][3]], writes=[blnB4[t4][3]])

            def _s3(c=c, pc=pc):
                for t4 in range(4):
                    P.op("vector", (lambda tt, pp, l3: lambda e: e.tensor_scalar(
                        x1tm[pp][:, tt, :], x1tm[pp][:, tt, :], l3[:], None,
                        op0=ALU.mult))(t4, pc, lnB4[t4][3]),
                        reads=[blnB4[t4][3], bx1tm[pc]], writes=[bx1tm[pc]])
                    P.dma("gpsimd", "dOUT",
                          out_d[c * 512 + t4 * 128:c * 512 + (t4 + 1) * 128, :],
                          x1tm[pc][:, t4, :], reads=[bx1tm[pc]],
                          writes=[gbuf("outd")])
            deferB.append([_s0, _s1, _s2, _s3])

        for stages in deferB:
            for f in stages:
                f()
        deferB.clear()

    fin = [Tok(s, P.count[s], None, P.gseq)
           for s in P.sems if s.startswith("d") and P.count[s]]
    P.op("gpsimd", lambda e: e.memset(neg1[0:1, :], -1.0),
         waits=fin, inc=False)

    with nc.Block() as block:
        P.emit(block)

    for cm in reversed(ctxs):
        cm.__exit__(None, None, None)
    return nc, P



def _host_prep(inputs):
    perm = np.zeros(D, dtype=np.int64)
    for h in range(NH):
        for j in range(32):
            perm[h * 64 + j] = h * 64 + 2 * j
            perm[h * 64 + 32 + j] = h * 64 + 2 * j + 1
    f16 = np.float16
    qwp = np.ascontiguousarray(np.asarray(inputs["L0_attn_qw"], f16)[:, perm])
    kwp = np.ascontiguousarray(np.asarray(inputs["L0_attn_kw"], f16)[:, perm])
    vw = np.ascontiguousarray(np.asarray(inputs["L0_attn_vw"], f16))
    ow = np.ascontiguousarray(np.asarray(inputs["L0_attn_ow"], f16))

    inv = (10000.0 ** (-(np.arange(0, HD, 2, dtype=np.float64)) / HD))
    pos = np.arange(SEQ, dtype=np.float64)
    freq = pos[:, None] * inv[None, :]

    def tab(a, scale):
        t = (a * scale).astype(np.float16)
        return np.ascontiguousarray(t.reshape(NT, 128, 32).transpose(1, 0, 2))
    cosq, sinq = tab(np.cos(freq), 1.0 / 32.0), tab(np.sin(freq), 1.0 / 32.0)
    cosk, sink = tab(np.cos(freq), 1.0), tab(np.sin(freq), 1.0)

    tri = np.zeros((128, 128), np.float16)
    for p in range(128):
        tri[p, p:] = 1.0
    dmm = np.zeros((128, 896), np.float16)
    dmm[:, 384:512] = tri
    dmm[:, 512:] = 1.0
    idb = np.eye(128).astype(ml_dtypes.bfloat16)
    idh = np.eye(128).astype(np.float16)
    id32 = np.eye(128).astype(np.float32)
    hsel = np.zeros((128, NE, NH), ml_dtypes.bfloat16)
    for pt in range(8):
        for p in range(128):
            hsel[p, pt, 2 * pt + p // 64] = 1.0

    f32 = np.float32
    return dict(
        qwp=qwp, kwp=kwp, vw=vw, ow=ow,
        cosq=cosq, sinq=sinq, cosk=cosk, sink=sink,
        tri=tri, dmm=dmm, idb=idb, idh=idh, id32=id32, hsel=hsel,
        lin0_w=np.asarray(inputs["L0_ffn_lin0_w"], f32),
        lin0_b=np.ascontiguousarray(np.asarray(inputs["L0_ffn_lin0_b"], f32).reshape(8, 128).T),
        s1w1=np.asarray(inputs["L0_ffn_swi1_w1"], f32),
        s1b1=np.ascontiguousarray(np.asarray(inputs["L0_ffn_swi1_b1"], f32).reshape(8, 128).T),
        s1w2=np.asarray(inputs["L0_ffn_swi1_w2"], f32),
        s1b2=np.ascontiguousarray(np.asarray(inputs["L0_ffn_swi1_b2"], f32).reshape(8, 128).T),
        gateT=np.ascontiguousarray(np.asarray(inputs["L0_ffn_gate_w"], f32).T),
        ew1=np.asarray(inputs["L0_ffn_exp_w1"], f32),
        ew3=np.asarray(inputs["L0_ffn_exp_w3"], f32),
        ew2=np.asarray(inputs["L0_ffn_exp_w2"], f32),
        shw1=np.asarray(inputs["L0_ffn_sh_w1"], f32),
        shw3=np.asarray(inputs["L0_ffn_sh_w3"], f32),
        shw2=np.asarray(inputs["L0_ffn_sh_w2"], f32),
        lin1_w=np.asarray(inputs["L0_ffn_lin1_w"], f32),
        lin1_b=np.ascontiguousarray(np.asarray(inputs["L0_ffn_lin1_b"], f32).reshape(8, 128).T),
        s2w1=np.asarray(inputs["L0_ffn_swi2_w1"], f32),
        s2b1=np.ascontiguousarray(np.asarray(inputs["L0_ffn_swi2_b1"], f32).reshape(8, 128).T),
        s2w2=np.asarray(inputs["L0_ffn_swi2_w2"], f32),
        s2b2=np.ascontiguousarray(np.asarray(inputs["L0_ffn_swi2_b2"], f32).reshape(8, 128).T),
    )


def kernel(**inputs):
    base = _host_prep(inputs)
    x = np.asarray(inputs["L0_attn_x"], np.float16)
    in_maps = []
    for ci in range(NCORES):
        m = dict(base)
        m["x"] = np.ascontiguousarray(x[ci * NSEQ:(ci + 1) * NSEQ])
        in_maps.append(m)
    nc, _ = build_nc("full")
    res = run_bass_kernel_spmd(nc, in_maps, core_ids=list(range(NCORES)))
    shards = [np.asarray(r["out"], np.float32).reshape(NSEQ, SEQ, D)
              for r in res.results]
    return np.concatenate(shards, axis=0)


if __name__ == "__main__":
    nc, P = build_nc("full")
    print("ops:", P.nops)


# version 28
# speedup vs baseline: 1.1466x; 1.0675x over previous
import numpy as np
import ml_dtypes
import concourse.bass as bass
import concourse.mybir as mybir
from concourse.bass_utils import run_bass_kernel_spmd

F32 = mybir.dt.float32
F16 = mybir.dt.float16
BF16 = mybir.dt.bfloat16
U8 = mybir.dt.uint8
I32 = mybir.dt.int32
AF = mybir.ActivationFunctionType
ALU = mybir.AluOpType
AX = mybir.AxisListType

NCORES = 8
BATCH, SEQ, D = 32, 2048, 1024
NH, HD = 16, 64
NSEQ = BATCH // NCORES
TPC = NSEQ * SEQ
NT = SEQ // 128
NCHK = 4
CH = 512
NCH_B = TPC // CH
LIMIT_BCHUNKS = None
LIMIT_ASEQS = None
NE = 8
EPS = 1e-5

ENGINES = ("sync", "tensor", "vector", "scalar", "gpsimd")
SEMKEY = {"sync": "sy", "tensor": "te", "vector": "ve", "scalar": "sc",
          "gpsimd": "gp"}
INORDER_SKIP = ("tensor",)


class Tok:
    __slots__ = ("sem", "v", "eng", "seq")

    def __init__(self, sem, v, eng, seq):
        self.sem, self.v, self.eng, self.seq = sem, v, eng, seq


class Buf:
    __slots__ = ("name", "w", "r")

    def __init__(self, name):
        self.name = name
        self.w = None
        self.r = []


class Prog:
    def __init__(self, nc, sems):
        self.nc = nc
        self.sems = sems
        self.count = {n: 0 for n in sems}
        self.q = {e: [] for e in ENGINES}
        self.nops = {e: 0 for e in ENGINES}
        self.gseq = 0
        self.last_real = {e: None for e in ENGINES}

    def fence(self):
        toks = []
        for e in ENGINES:
            lr = self.last_real[e]
            if lr is not None:
                toks.append(lr)
        for s in self.sems:
            if s.startswith("d") and self.count[s]:
                toks.append(Tok(s, self.count[s], None, self.gseq))
        return toks

    def op(self, eng, fn, reads=(), writes=(), waits=(), inc=True, dma=False):
        self.gseq += 1
        seq = self.gseq
        deps = list(waits)
        for b in reads:
            if b.w is not None:
                deps.append(b.w)
        for b in writes:
            if b.w is not None:
                deps.append(b.w)
            deps.extend(b.r)
        best_real = {}
        for t in deps:
            if t is not None and t.sem is not None and not t.sem.startswith("d") \
                    and t.eng is not None:
                if t.eng not in best_real or best_real[t.eng].seq < t.seq:
                    best_real[t.eng] = t
        wd = {}
        for t in deps:
            if t is None:
                continue
            if t.sem is None:
                br = best_real.get(t.eng)
                if br is not None and br.seq >= t.seq:
                    continue
                if t.eng == eng and eng in INORDER_SKIP and not dma:
                    continue
                sub = self.last_real[t.eng]
                if sub is None or sub.seq < t.seq:
                    raise RuntimeError(
                        f"unresolvable dep on {t.eng} (add inc=True there)")
                t = sub
            if (t.eng == eng and eng in INORDER_SKIP and not dma
                    and not t.sem.startswith("d")):
                continue
            if t.v > self.count[t.sem]:
                raise RuntimeError(f"forward wait {t.sem}={t.v}")
            wd[t.sem] = max(wd.get(t.sem, 0), t.v)
        if dma:
            semn = dma
            self.count[semn] += 16
            tok = Tok(semn, self.count[semn], eng, seq)
        elif inc:
            semn = SEMKEY[eng]
            self.count[semn] += 1
            tok = Tok(semn, self.count[semn], eng, seq)
        else:
            semn, tok = None, Tok(None, 0, eng, seq)
        if semn is not None:
            self.last_real[eng] = tok
        self.q[eng].append(
            (wd, fn, (self.sems[semn], 16 if dma else 1) if semn else None))
        self.nops[eng] += 1
        for b in reads:
            b.r.append(tok)
        for b in writes:
            b.w = tok
            b.r = []
        return tok

    def dma(self, eng, sem, out, in_, reads=(), writes=(), waits=(), **kw):
        return self.op(eng, lambda e: e.dma_start(out=out, in_=in_, **kw),
                       reads=reads, writes=writes, waits=waits, dma=sem)

    def emit(self, block):
        def section(engname):
            def f(eng):
                last = {}
                for wd, fn, incv in self.q[engname]:
                    for s, v in sorted(wd.items()):
                        if last.get(s, -1) < v:
                            eng.wait_ge(self.sems[s], v)
                            last[s] = v
                    inst = fn(eng)
                    if incv is not None:
                        inst.then_inc(incv[0], incv[1])
            return f
        block.sync(section("sync"))
        block.tensor(section("tensor"))
        block.vector(section("vector"))
        block.scalar(section("scalar"))
        block.gpsimd(section("gpsimd"))


class Arena:
    def __init__(self, t, nbytes):
        self.t = t
        self.nbytes = nbytes
        self.off = 0
        self.hi = 0

    def alloc(self, shape, dt, parts=128):
        esz = {F32: 4, F16: 2, BF16: 2, U8: 1, I32: 4}[dt]
        n = int(np.prod(shape)) * esz
        off = (self.off + 63) & ~63
        self.off = off + n
        self.hi = max(self.hi, self.off)
        assert self.off <= self.nbytes, f"arena overflow {self.off}"
        ap = self.t[0:parts, off:off + n].bitcast(dt)
        if len(shape) > 1:
            pat = " ".join(f"a{i}" for i in range(len(shape)))
            ap = ap.rearrange(f"p ({pat}) -> p {pat}",
                              **{f"a{i}": shape[i] for i in range(len(shape))})
        return ap

    def mark(self):
        return self.off

    def reset(self, off):
        self.off = off


def ds(i, n=128):
    return slice(i * n, (i + 1) * n)


def build_nc(mode="full"):
    nc = bass.Bass("TRN2", target_bir_lowering=False)

    x_d = nc.dram_tensor("x", [NSEQ, SEQ, D], F16, kind="ExternalInput").ap()
    att_w = {n: nc.dram_tensor(n, [D, D], F16, kind="ExternalInput").ap()
             for n in ("qwp", "kwp", "vw", "ow")}
    f32in = {}
    for n, sh in [("lin0_w", [D, D]), ("lin0_b", [128, 8]),
                  ("s1w1", [D, D]), ("s1b1", [128, 8]), ("s1w2", [D, D]),
                  ("s1b2", [128, 8]),
                  ("gateT", [D, NE]),
                  ("ew1", [NE, D, D]), ("ew3", [NE, D, D]), ("ew2", [NE, D, D]),
                  ("shw1", [D, 2 * D]), ("shw3", [D, 2 * D]), ("shw2", [2 * D, D]),
                  ("lin1_w", [D, D]), ("lin1_b", [128, 8]),
                  ("s2w1", [D, D]), ("s2b1", [128, 8]), ("s2w2", [D, D]),
                  ("s2b2", [128, 8])]:
        f32in[n] = nc.dram_tensor(n, sh, F32, kind="ExternalInput").ap()
    consts_d = {}
    for n, sh, dt in [("cosq", [128, NT, 32], F16), ("sinq", [128, NT, 32], F16),
                      ("cosk", [128, NT, 32], F16), ("sink", [128, NT, 32], F16),
                      ("tri", [128, 128], F16), ("dmm", [128, 896], F16),
                      ("idb", [128, 128], BF16),
                      ("idh", [128, 128], F16),
                      ("id32", [128, 128], F32), ("hsel", [128, NE, NH], BF16)]:
        consts_d[n] = nc.dram_tensor(n, sh, dt, kind="ExternalInput").ap()

    x1s_kind = {"full": "Internal", "A": "ExternalOutput", "B": "ExternalInput"}[mode]
    x1s = nc.dram_tensor("x1s", [TPC, D], F32, kind=x1s_kind).ap()
    out_d = nc.dram_tensor("out", [TPC, D], F32, kind="ExternalOutput").ap()

    wb = {}
    for n, sh in [("b_lin0", [D, D]), ("b_s1w1", [D, D]), ("b_s1w2", [D, D]),
                  ("b_gateT", [D, NE]),
                  ("b_ew1", [NE, D, D]), ("b_ew3", [NE, D, D]), ("b_ew2", [NE, D, D]),
                  ("b_shw1", [D, 2 * D]), ("b_shw3", [D, 2 * D]), ("b_shw2", [2 * D, D]),
                  ("b_lin1", [D, D]), ("b_s2w1", [D, D]), ("b_s2w2", [D, D])]:
        wb[n] = nc.dram_tensor(n, sh, BF16, kind="Internal").ap()
    ccs = nc.dram_tensor("ccs", [NE, CH], BF16, kind="Internal").ap()

    ctxs = []

    def enter(cm):
        ctxs.append(cm)
        return cm.__enter__()

    ARENA_BYTES = 201 * 1024
    arena_t = enter(nc.sbuf_tensor("arena", [128, ARENA_BYTES], U8))
    ar = Arena(arena_t, ARENA_BYTES)

    sbc = {}
    for n in ("cosq", "sinq", "cosk", "sink"):
        sbc[n] = enter(nc.sbuf_tensor(n + "_s", [128, NT, 32], F16))
    sbc["tri"] = enter(nc.sbuf_tensor("tri_s", [128, 128], F16))
    sbc["dmm"] = ar.alloc([896], F16)
    sbc["idb"] = enter(nc.sbuf_tensor("idb_s", [128, 128], BF16))
    sbc["idh"] = enter(nc.sbuf_tensor("idh_s", [128, 128], F16))
    sbc["id32"] = enter(nc.sbuf_tensor("id32_s", [128, 128], F32))
    sbc["hsel"] = enter(nc.sbuf_tensor("hsel_s", [128, NE, NH], BF16))
    biases = {n: enter(nc.sbuf_tensor(n + "_s", [128, 8], F32))
              for n in ("lin0_b", "s1b1", "s1b2", "lin1_b", "s2b1", "s2b2")}
    ggT = enter(nc.sbuf_tensor("ggT", [128, 8, NE], BF16))
    neg1 = enter(nc.sbuf_tensor("neg1", [128, 4 * NE], F32))
    cEPS = enter(nc.sbuf_tensor("cEPS", [128, 1], F32))
    cTINY = enter(nc.sbuf_tensor("cTINY", [128, 1], F32))
    ones128 = enter(nc.sbuf_tensor("ones128", [1, 128], F32))

    pS1 = enter(nc.psum_tensor("pS1", [128, 2048], F32))
    pS2 = enter(nc.psum_tensor("pS2", [128, 1024], F32))
    pAVX = enter(nc.psum_tensor("pAVX", [128, 512], F32))
    pTR = enter(nc.psum_tensor("pTR", [128, 1024], F16))
    pTRbf = pTR[:].bitcast(BF16)

    semnames = ("dC", "dGG", "dP0i", "dP0o", "dP1i", "dP1o", "dP2i", "dP2o",
                "dWU0", "dWU1", "dWU2", "dWA", "dWB",
                "dXI0", "dXI1", "dXR", "dXR1", "dX1S", "dX1S2", "dRDS", "dDEN",
                "dRD2",
                "dRU0", "dRU1", "dBB", "dCC", "dXT0", "dXT1", "dOUT",
                "sy", "te", "ve", "sc", "gp")
    smap = {n: enter(nc.semaphore(n)) for n in semnames}
    P = Prog(nc, smap)

    BD = {}

    def gbuf(n):
        if n not in BD:
            BD[n] = Buf(n)
        return BD[n]

    _prevc = []
    for n in ("cosq", "sinq", "cosk", "sink", "tri", "dmm", "idb",
              "idh", "id32", "hsel"):
        dst = sbc[n] if n == "dmm" else sbc[n][:]
        t = P.dma("sync", "dC", dst, consts_d[n],
                  writes=[gbuf("c_" + n)], waits=tuple(_prevc))
        _prevc = [t]
    for n in biases:
        t = P.dma("sync", "dC", biases[n][:], f32in[n],
                  writes=[gbuf("c_" + n)], waits=tuple(_prevc))
        _prevc = [t]
    P.op("vector", lambda e: e.memset(neg1[:], -1.0), writes=[gbuf("c_neg1")])
    P.op("vector", lambda e: e.memset(cEPS[:], EPS), writes=[gbuf("c_eps")])
    P.op("vector", lambda e: e.memset(cTINY[:], 1e-35), writes=[gbuf("c_tiny")])
    P.op("vector", lambda e: e.memset(ones128[:], 1.0), writes=[gbuf("c_ones")])
    cb = {n: gbuf("c_" + n) for n in
          ("cosq", "sinq", "cosk", "sink", "tri", "dmm", "idb",
           "idh", "id32", "hsel", "neg1",
           "lin0_b", "s1b1", "s1b2", "lin1_b", "s2b1", "s2b2")}
    cb["eps"] = gbuf("c_eps")
    cb["tiny"] = gbuf("c_tiny")
    cb["ones"] = gbuf("c_ones")
    bpinit = gbuf("psum_init")
    P.op("vector", lambda e: e.memset(pS1[:], 0.0), writes=[bpinit])
    P.op("vector", lambda e: e.memset(pS2[:], 0.0), reads=[bpinit])
    P.op("vector", lambda e: e.memset(pAVX[:], 0.0), writes=[bpinit])

    units0 = []
    if mode != "A":
        def conv_mat(src, dst, name):
            R, C = src.shape
            for r0 in range(0, R, 128):
                for c0 in range(0, C, 512):
                    cw = min(512, C - c0)
                    units0.append((src[r0:r0 + 128, c0:c0 + cw],
                                   dst[r0:r0 + 128, c0:c0 + cw], name, cw))

        conv_mat(f32in["lin0_w"], wb["b_lin0"], "b_lin0")
        conv_mat(f32in["s1w1"], wb["b_s1w1"], "b_s1w1")
        conv_mat(f32in["s1w2"], wb["b_s1w2"], "b_s1w2")
        conv_mat(f32in["gateT"], wb["b_gateT"], "b_gateT")
        for ei in range(NE):
            conv_mat(f32in["ew1"][ei], wb["b_ew1"][ei], "b_ew1")
            conv_mat(f32in["ew3"][ei], wb["b_ew3"][ei], "b_ew3")
            conv_mat(f32in["ew2"][ei], wb["b_ew2"][ei], "b_ew2")
        conv_mat(f32in["shw1"], wb["b_shw1"], "b_shw1")
        conv_mat(f32in["shw3"], wb["b_shw3"], "b_shw3")
        conv_mat(f32in["shw2"], wb["b_shw2"], "b_shw2")
        conv_mat(f32in["lin1_w"], wb["b_lin1"], "b_lin1")
        conv_mat(f32in["s2w1"], wb["b_s2w1"], "b_s2w1")
        conv_mat(f32in["s2w2"], wb["b_s2w2"], "b_s2w2")

    w0in = [ar.alloc([512], F32) for _ in range(3)]
    w0out = [ar.alloc([512], BF16) for _ in range(3)]
    b_w0in = [Buf(f"w0in{i}") for i in range(3)]
    b_w0out = [Buf(f"w0out{i}") for i in range(3)]
    u0 = [0]

    def drip_phase0(k):
        for _ in range(k):
            i = u0[0]
            if i >= len(units0):
                return
            u0[0] += 1
            srcap, dstap, dname, C = units0[i]
            p = i % 3
            iv = w0in[p][:, 0:C]
            ov = w0out[p][:, 0:C]
            P.dma("sync", ("dP0i", "dP1i", "dP2i")[p], iv, srcap,
                  writes=[b_w0in[p]])
            P.op("gpsimd",
                 (lambda a, bq: lambda e: e.tensor_copy(bq, a))(iv, ov),
                 reads=[b_w0in[p]], writes=[b_w0out[p]])
            P.dma("sync", ("dP0o", "dP1o", "dP2o")[p], dstap, ov,
                  reads=[b_w0out[p]], writes=[gbuf(dname)])

    markA = ar.mark()
    if mode != "B":
        xT = ar.alloc([8, 2048], F16)
        KrT = ar.alloc([8, 2048], F16)
        Vaug = ar.alloc([NT, NH * 65], F16)
        QrT = ar.alloc([8, 512], F16)
        PTb = ar.alloc([NT, 512], F16)
        mergedT = ar.alloc([8, 512], F16)

        watt = ar.alloc([8, 1024], F16)
        wattB = ar.alloc([8, 1024], F16)
        xin = [ar.alloc([1024], F16) for _ in range(2)]
        qkrope = [ar.alloc([1024], F16) for _ in range(2)]
        rtmpA = [ar.alloc([512], F16) for _ in range(2)]
        rtmpB = [ar.alloc([512], F16) for _ in range(2)]
        xres = ar.alloc([2, 1024], F16)
        zb = [ar.alloc([1024], F32), None, None, None]
        sqQT = [ar.alloc([8, 128], F16) for _ in range(2)]
        rdenS = ar.alloc([1536], F16, parts=1)
        ones64 = ar.alloc([64], F16, parts=1)
        kn2 = ar.alloc([1], F32, parts=16)
        qn2 = ar.alloc([1], F32, parts=16)
        qk2 = ar.alloc([1], F32, parts=16)
        btmp = ar.alloc([1], F32, parts=16)
        bnegrow = ar.alloc([16], F32, parts=1)
        negb = ar.alloc([16], F32)
        mx16 = ar.alloc([1], F32, parts=16)
        lnrow = [[ar.alloc([1], F32) for _ in range(4)] for _ in range(4)]

        bxT, bKrT, bVaug, bQrT, bPT, bmg = (Buf(n) for n in
            ("xT", "KrT", "Vaug", "QrT", "PT", "mergedT"))

        bwatt = Buf("watt")
        bwattB = Buf("wattB")
        bxin = [Buf("xin0"), Buf("xin1")]
        bqkrope = [Buf("qkrope0"), Buf("qkrope1")]
        brtmpA = [Buf("rtmpA0"), Buf("rtmpA1")]
        brtmpB = [Buf("rtmpB0"), Buf("rtmpB1")]
        bxres = [Buf("xres0"), Buf("xres1")]
        bzb = [Buf(f"zb{i}") for i in range(4)]
        bsqQT = [Buf("sqQT0"), Buf("sqQT1")]
        brdS = [Buf("rdS0"), Buf("rdS1"), Buf("rdS2")]
        bones64 = Buf("ones64")
        bkn2, bqn2, bqk2, bbtmp = Buf("kn2"), Buf("qn2"), Buf("qk2"), Buf("btmp")
        bbnegrow, bnegb, bmx16 = Buf("bnegrow"), Buf("negb"), Buf("mx16")
        blnrow = [[Buf(f"lnrow{q}{i}") for i in range(4)] for q in range(4)]
        bqA, bqB, bqC = Buf("AqA"), Buf("AqB"), Buf("AqC")
        bpAVXb, bpTRb = Buf("ApAVX"), Buf("ApTR")
        for _bq in (bqA, bqB, bqC, bpAVXb):
            _bq.w = gbuf("psum_init").w

        pTR3 = pTR[:].rearrange("p (a b) -> p a b", b=128)
        pTRf32 = pTR[:].bitcast(F32)
        P.op("vector", lambda e: e.memset(ones64[:], 1.0), writes=[bones64])
        zb[1] = PTb[:, 2:6, :].rearrange("p a b -> p (a b)").bitcast(F32)
        zb[2] = PTb[:, 6:10, :].rearrange("p a b -> p (a b)").bitcast(F32)
        zb[3] = PTb[:, 10:14, :].rearrange("p a b -> p (a b)").bitcast(F32)

        def load_watt(name, dst=None, dbuf=None, sem="dWA"):
            return P.dma("scalar", sem, (dst if dst is not None else watt)[:, :, :],
                         att_w[name].rearrange("(k p) f -> p k f", p=128),
                         writes=[dbuf if dbuf is not None else bwatt])

        def rope6(dst, psrc, ct, st, bpsrc, bdst, btabs, par):
            E = psrc.rearrange("p (h j) -> p h j", j=64)[:, :, 0:32]
            O = psrc.rearrange("p (h j) -> p h j", j=64)[:, :, 32:64]
            dE = dst.rearrange("p (h j) -> p h j", j=64)[:, :, 0:32]
            dO = dst.rearrange("p (h j) -> p h j", j=64)[:, :, 32:64]
            ctb = ct.unsqueeze(1).to_broadcast((128, 16, 32))
            stb = st.unsqueeze(1).to_broadcast((128, 16, 32))
            tA = rtmpA[par][:].rearrange("p (h j) -> p h j", j=32)
            tB = rtmpB[par][:].rearrange("p (h j) -> p h j", j=32)
            brA, brB = brtmpA[par], brtmpB[par]
            P.op("vector", lambda e: e.tensor_tensor(tA, E, ctb, op=ALU.mult),
                 reads=[bpsrc] + btabs, writes=[brA])
            P.op("vector", lambda e: e.tensor_tensor(tB, O, stb, op=ALU.mult),
                 reads=[bpsrc], writes=[brB])
            P.op("vector", lambda e: e.tensor_tensor(dE, tA, tB, op=ALU.subtract),
                 reads=[brA, brB], writes=[bdst])
            P.op("vector", lambda e: e.tensor_tensor(tA, E, stb, op=ALU.mult),
                 reads=[bpsrc], writes=[brA])
            P.op("vector", lambda e: e.tensor_tensor(tB, O, ctb, op=ALU.mult),
                 reads=[bpsrc], writes=[brB])
            P.op("vector", lambda e: e.tensor_tensor(dO, tA, tB, op=ALU.add),
                 reads=[brA, brB], writes=[bdst])

        def transp8_evac(srcap, bsrc, dstap, bdst):
            for j in range(8):
                P.op("tensor",
                     (lambda jj: lambda e: e.transpose(
                         pTR[:, ds(jj)], srcap[:, ds(jj)], sbc["idh"][:]))(j),
                     reads=[bsrc, cb["idh"]], writes=([bpTRb] if j == 0 else []),
                     inc=(j == 7))
            P.op("scalar", lambda e: e.copy(dstap, pTR3),
                 reads=[bpTRb], writes=[bdst])

        genPP = None

        def gen_proj(t, w_ap, bw, pr):
            dst, dbuf = genPP[pr]
            for n in range(2):
                for k in range(8):
                    P.op("tensor",
                         (lambda nn, kk, tt, dd: lambda e: e.matmul(
                             dd[:, ds(nn, 512)], xT[:, kk, ds(tt)],
                             w_ap[:, kk, ds(nn, 512)],
                             start=(kk == 0), stop=(kk == 7)))(n, k, t, dst),
                         reads=[bxT, bw],
                         writes=([dbuf] if (n == 0 and k == 0) else []),
                         inc=(n == 1 and k == 7))

        def norm_tile_sq(srcT, bsrcT, t):
            P.op("scalar", (lambda tt: lambda e: e.activation(
                sqQT[tt % 2][:, 0:8, :], srcT[:, :, ds(tt)], AF.Square))(t),
                reads=[bsrcT], writes=[bsqQT[t % 2]])

        def norm_tile_mm(t, dstv, bdstv, first):
            _t = None
            for pt in range(8):
                _t = P.op("tensor", (lambda p_, tt: lambda e: e.matmul(
                    pAVX[0:16, 0:128], sbc["hsel"][:, p_, :],
                    sqQT[tt % 2][:, p_, :], start=(p_ == 0),
                    stop=(p_ == 7)))(pt, t),
                    reads=[bsqQT[t % 2], cb["hsel"]],
                    writes=([bpAVXb] if pt == 0 else []), inc=(pt == 7))
            bpAVXb.w = _t
            P.op("vector", lambda e: e.reduce_max(mx16[:], pAVX[0:16, 0:128],
                                                  axis=AX.X),
                 reads=[bpAVXb], writes=[bmx16])
            if first:
                P.op("vector", (lambda dv: lambda e: e.tensor_copy(
                    dv[:], mx16[:]))(dstv),
                    reads=[bmx16], writes=[bdstv])
            else:
                P.op("vector", (lambda dv: lambda e: e.tensor_tensor(
                    dv[:], dv[:], mx16[:], op=ALU.max))(dstv),
                    reads=[bmx16, bdstv], writes=[bdstv])

        genPP = [(pS2[:], bqA), (pS1[:, 0:1024], bqB)]
        SCB = [(pS2[:, 0:1024], bqA), (pS1[:, 0:1024], bqB),
               (pS1[:, 1024:2048], bqC)]

        deferred_ln = []

        def flush_ln():
            for f in deferred_ln:
                f()
            deferred_ln.clear()

        for b in range(LIMIT_ASEQS or NSEQ):
            load_watt("kwp")
            load_watt("ow", dst=wattB, dbuf=bwattB, sem="dWB")
            for t in range(NT + 3):
                if t >= 3:
                    tn = t - 3
                    norm_tile_mm(tn, kn2, bkn2, tn == 0)
                if 2 <= t < NT + 2:
                    tp0 = t - 2
                    transp8_evac(qkrope[tp0 % 2][:], bqkrope[tp0 % 2],
                                 KrT[:, :, ds(tp0)], bKrT)
                    norm_tile_sq(KrT, bKrT, tp0)
                if t < NT:
                    p = t % 2
                    if t == 0:
                        P.dma("scalar", "dXI0", xin[0][:],
                              x_d[b, ds(0), :], writes=[bxin[0]])
                    transp8_evac(xin[p][:], bxin[p], xT[:, :, ds(t)], bxT)
                    if t + 1 < NT:
                        P.dma("scalar", "dXI0" if t % 2 else "dXI1",
                              xin[(t + 1) % 2][:],
                              x_d[b, ds(t + 1), :], writes=[bxin[(t + 1) % 2]])
                    prg = t % 2
                    gen_proj(t, watt, bwatt, prg)
                    rope6(qkrope[t % 2][:], genPP[prg][0], sbc["cosk"][:, t, :],
                          sbc["sink"][:, t, :], genPP[prg][1], bqkrope[t % 2],
                          [cb["cosk"], cb["sink"]], t % 2)
                drip_phase0(6)
            load_watt("vw")
            P.op("vector", lambda e: e.memset(
                Vaug[:, :, :].rearrange("p t (h c) -> p t h c", c=65)
                [:, :, :, 64:65].squeeze(3), 1.0), writes=[bVaug])
            for t in range(NT):
                prg = t % 2
                gen_proj(t, watt, bwatt, prg)
                P.op("vector", (lambda tt, dd: lambda e: e.tensor_copy(
                    Vaug[:, tt, :].rearrange("p (h c) -> p h c", c=65)[:, :, 0:64],
                    dd.rearrange("p (h c) -> p h c", c=64)))(t, genPP[prg][0]),
                    reads=[genPP[prg][1]], writes=[bVaug])
                drip_phase0(6)

            load_watt("qwp")
            for c in range(NCHK):
                for ts4 in range(7):
                    if ts4 >= 3:
                        tq = ts4 - 3
                        norm_tile_mm(tq, qn2, bqn2, tq == 0)
                    if 2 <= ts4 < 6:
                        tq = ts4 - 2
                        transp8_evac(qkrope[tq % 2][:], bqkrope[tq % 2],
                                     QrT[:, :, ds(tq)], bQrT)
                        norm_tile_sq(QrT, bQrT, tq)
                    if ts4 < 4:
                        t = c * 4 + ts4
                        p = ts4 % 2
                        prg = ts4 % 2
                        gen_proj(t, watt, bwatt, prg)
                        rope6(qkrope[ts4 % 2][:], genPP[prg][0],
                              sbc["cosq"][:, t, :],
                              sbc["sinq"][:, t, :], genPP[prg][1],
                              bqkrope[ts4 % 2],
                              [cb["cosq"], cb["sinq"]], ts4 % 2)
                drip_phase0(2)
                P.op("vector", lambda e: e.tensor_tensor(
                    qk2[:], qn2[:], kn2[:], op=ALU.mult),
                    reads=[bqn2, bkn2], writes=[bqk2])
                P.op("scalar", lambda e: e.activation(
                    btmp[:], qk2[:], AF.Ln, bias=cTINY[0:16, :]),
                    reads=[bqk2, cb["tiny"]], writes=[bbtmp])
                P.op("scalar", lambda e: e.activation(
                    btmp[:], btmp[:], AF.Exp, scale=0.5),
                    reads=[bbtmp], writes=[bbtmp])
                P.op("vector", lambda e: e.tensor_scalar(
                    btmp[:], btmp[:], -1.0, None, op0=ALU.mult),
                    reads=[bbtmp], writes=[bbtmp])
                P.op("tensor", lambda e: e.transpose(
                    pS2[0:1, 0:16], btmp[:], sbc["id32"][0:16, 0:16]),
                    reads=[bbtmp, cb["id32"]], writes=[bqA])
                P.op("vector", lambda e: e.tensor_copy(bnegrow[:], pS2[0:1, 0:16]),
                     reads=[bqA], writes=[bbnegrow])
                P.op("tensor", lambda e: e.matmul(
                    pAVX[:, 0:16], ones128[:], bnegrow[:], start=True, stop=True),
                    reads=[bbnegrow, cb["ones"]], writes=[bpAVXb])
                P.op("vector", lambda e: e.tensor_copy(negb[:], pAVX[:, 0:16]),
                     reads=[bpAVXb], writes=[bnegb])

                flush_ln()
                ktiles = 4 * (c + 1)
                groups = [(gi % 3, 2 * gi, 2) for gi in range(ktiles // 2)]

                def emit_fix(hh):
                    uu, rr = hh // 2, (hh % 2) * 64
                    off = (hh % 3) * 512
                    P.op("tensor", (lambda of: lambda e: e.matmul(
                        pTRf32[0:64, 0:512], ones64[0:1, 0:64],
                        rdenS[0:1, of:of + 512], start=True, stop=True))(off),
                        reads=[brdS[hh % 3], bones64], writes=[bpTRb])
                    P.op("vector", (lambda uu_, rr_: lambda e: e.tensor_tensor(
                        mergedT[rr_:rr_ + 64, uu_, :],
                        mergedT[rr_:rr_ + 64, uu_, :],
                        pTRf32[0:64, 0:512], op=ALU.mult))(uu, rr),
                        reads=[bpTRb], writes=[bmg])

                av_last = None
                avtok = {}
                for h in range(NH):
                    u, r0 = h // 2, (h % 2) * 64
                    gtoks = {}

                    def emit_scores_exp(gidx):
                        gsel, gk0, gsz = groups[gidx]
                        pbuf, pB = SCB[gsel]
                        for i in range(gsz):
                            kj = gk0 + i
                            r = kj - 4 * c
                            off = 128 * r if r >= 0 else 0
                            P.op("tensor",
                                 (lambda pb, ii, kk, of, uu, rr: lambda e: e.matmul(
                                     pb[:, ii * 512 + of:(ii + 1) * 512],
                                     KrT[rr:rr + 64, uu, ds(kk)],
                                     QrT[rr:rr + 64, uu, of:512],
                                     start=True, stop=True))(pbuf, i, kj, off, u, r0),
                                 reads=[bKrT, bQrT],
                                 writes=([pB] if i == 0 else []),
                                 inc=(i == gsz - 1))
                        zjs = sorted({(t - 2) // 4 + 1
                                      for t in range(gk0, gk0 + gsz)
                                      if 2 <= t <= 13})
                        avw = [t for t in (avtok.get(gk0 + gsz - 1),)
                               if t is not None]
                        texp = P.op("scalar",
                                    (lambda pb, kk0, sz_, hh: lambda e: e.activation(
                                        PTb[:, kk0:kk0 + sz_, :], pb[:, 0:sz_ * 512],
                                        AF.Exp, bias=negb[:, hh:hh + 1]))(
                                            pbuf, gk0, gsz, h),
                                    reads=[pB, bnegb], waits=avw,
                                    writes=[bPT] + [bzb[j] for j in zjs])
                        r0g = gk0 - 4 * c
                        if r0g >= 0:
                            tm = None
                            for ii in range(2):
                                rr_ = r0g + ii
                                tm = P.op("vector",
                                          (lambda kk, st: lambda e:
                                           e.tensor_tensor(
                                              PTb[:, kk, :], PTb[:, kk, :],
                                              sbc["dmm"][:, st:st + 512],
                                              op=ALU.mult))(
                                                  gk0 + ii, 384 - 128 * rr_),
                                          reads=[cb["dmm"]], writes=[bPT])
                                gtoks[gk0 + ii] = tm
                        else:
                            gtoks[gk0] = texp
                            gtoks[gk0 + 1] = texp

                    def emit_av_group(gidx):
                        nonlocal av_last
                        _, gk0, gsz = groups[gidx]
                        for kj in range(gk0, gk0 + gsz):
                            av_last = P.op(
                                "tensor",
                                (lambda kk, hh, kt: lambda e: e.matmul(
                                    pAVX[0:65, :],
                                    Vaug[:, kk, hh * 65:(hh + 1) * 65],
                                    PTb[:, kk, :], start=(kk == 0),
                                    stop=(kk == kt - 1)))(kj, h, ktiles),
                                reads=[bVaug], waits=[gtoks[kj]],
                                writes=([bpAVXb] if kj == 0 else []),
                                inc=True)
                            avtok[kj] = av_last

                    ng = len(groups)
                    for gi in range(ng):
                        emit_scores_exp(gi)
                    if h >= 2:
                        emit_fix(h - 2)
                    for gi in range(ng):
                        emit_av_group(gi)
                    bpAVXb.w = av_last
                    drip_phase0(1)
                    P.op("vector", (lambda uu, rr: lambda e: e.tensor_copy(
                        mergedT[rr:rr + 64, uu, :], pAVX[0:64, :]))(u, r0),
                        reads=[bpAVXb], writes=[bmg])
                    def _recip(of):
                        def g(e):
                            with nc.allow_low_precision(reason="f16 recip den"):
                                return e.reciprocal(rdenS[0:1, of:of + 512],
                                                    pAVX[64:65, 0:512])
                        return g
                    P.op("vector", _recip((h % 3) * 512),
                         reads=[bpAVXb], writes=[brdS[h % 3]])
                emit_fix(NH - 2)
                emit_fix(NH - 1)
                for ts4 in range(4):
                    pz = ts4
                    half = ts4 % 2
                    P.dma("scalar", "dXR" if ts4 % 2 == 0 else "dXR1",
                          xres[:, ts4 % 2, :],
                          x_d[b, c * 512 + ts4 * 128:c * 512 + (ts4 + 1) * 128, :],
                          writes=[bxres[ts4 % 2]])
                    for n in range(2):
                        for k in range(8):
                            P.op("tensor",
                                 (lambda hh, nn, kk, tt: lambda e: e.matmul(
                                     pS1[:, hh * 1024 + nn * 512:
                                         hh * 1024 + (nn + 1) * 512],
                                     mergedT[:, kk, ds(tt)],
                                     wattB[:, kk, ds(nn, 512)],
                                     start=(kk == 0), stop=(kk == 7)))(half, n, k, ts4),
                                 reads=[bmg, bwattB],
                                 writes=([bqB if half == 0 else bqC]
                                         if (n == 0 and k == 0) else []),
                                 inc=(n == 1 and k == 7))
                    zv, bzv = zb[pz], bzb[pz]
                    alias = [bPT] if pz > 0 else []
                    lr, blr = lnrow[pz], blnrow[pz]
                    P.op("vector", (lambda hh, zz, pzz: lambda e: e.tensor_tensor(
                        zz[:], pS1[:, hh * 1024:(hh + 1) * 1024],
                        xres[:, pzz, :], op=ALU.add))(half, zv, ts4 % 2),
                        reads=[bqB if half == 0 else bqC, bxres[ts4 % 2]],
                        writes=[bzv] + alias)
                    def _ln_tail(zv=zv, bzv=bzv, lr=lr, blr=blr, alias=alias,
                                 pz=pz, row0=b * SEQ + c * 512 + ts4 * 128):
                        P.op("vector", (lambda zz, l0: lambda e: e.reduce_sum(
                            l0[:], zz[:], axis=AX.X))(zv, lr[0]),
                            reads=[bzv], writes=[blr[0]])
                        P.op("vector", (lambda l0, l1: lambda e: e.tensor_scalar(
                            l1[:], l0[:], 1.0 / D, None, op0=ALU.mult))(
                                lr[0], lr[1]),
                            reads=[blr[0]], writes=[blr[1]])
                        P.op("vector", (lambda zz, l1: lambda e: e.tensor_scalar(
                            zz[:], zz[:], l1[:], None,
                            op0=ALU.subtract))(zv, lr[1]),
                            reads=[blr[1]], writes=[bzv])
                        P.op("scalar", (lambda zz, l2: lambda e: e.activation(
                            PTb[:, 14:16, :].rearrange("p a b -> p (a b)"), zz[:],
                            AF.Square, accum_out=l2[:]))(zv, lr[2]),
                            reads=[bzv], writes=[bPT, blr[2]])
                        P.op("scalar", (lambda l2, l3: lambda e: e.activation(
                            l3[:], l2[:], AF.Ln, scale=1.0 / D,
                            bias=cEPS[:]))(lr[2], lr[3]),
                            reads=[blr[2], cb["eps"]], writes=[blr[3]])
                        P.op("scalar", (lambda l3: lambda e: e.activation(
                            l3[:], l3[:], AF.Exp, scale=-0.5))(lr[3]),
                            reads=[blr[3]], writes=[blr[3]])
                        P.op("vector", (lambda zz, l3: lambda e: e.tensor_scalar(
                            zz[:], zz[:], l3[:], None, op0=ALU.mult))(zv, lr[3]),
                            reads=[blr[3]], writes=[bzv] + alias)
                        P.dma("scalar", "dX1S" if pz % 2 == 0 else "dX1S2",
                              x1s[row0:row0 + 128, :], zv[:], reads=[bzv],
                              writes=[gbuf("x1s0" if pz % 2 == 0 else "x1s1")])
                    deferred_ln.append(_ln_tail)
                drip_phase0(6)

    if mode != "B":
        flush_ln()
    if mode != "A":
        drip_phase0(len(units0))
        P.dma("sync", "dGG", ggT[:], wb["b_gateT"].rearrange("(k p) f -> p k f", p=128),
              reads=[gbuf("b_gateT")], writes=[gbuf("ggT")])
        fenceA = P.fence()
        ar.reset(markA)
        x1tm = [ar.alloc([4, 1024], F32) for _ in range(2)]
        x1T = ar.alloc([8, 512], BF16)
        h0T = ar.alloc([8, 512], BF16)
        h1T = ar.alloc([8, 512], BF16)
        actT = [ar.alloc([8, 512], BF16) for _ in range(2)]
        spT = ar.alloc([16, 512], BF16)
        routed = ar.alloc([8, 512], F32)
        c_bc = ar.alloc([NE, 512], BF16)
        cT = ar.alloc([512], BF16, parts=8)
        gexp = ar.alloc([4 * NE], F32)
        workg = [ar.alloc([4 * NE], F32) for _ in range(2)]
        selg = ar.alloc([4 * NE], F32)
        gtmp = ar.alloc([4 * NE], F32)
        mxg = ar.alloc([4], F32)
        t4g = ar.alloc([4], F32)
        accg = ar.alloc([4], F32)
        rivg = ar.alloc([4], F32)
        combine = ar.alloc([4, NE], BF16)
        zsqB = ar.alloc([1024], BF16)
        lnB4 = [[ar.alloc([1], F32) for _ in range(4)] for _ in range(4)]
        wsl = [ar.alloc([8, 1024], BF16) for _ in range(3)]

        bx1tm = [Buf(f"x1tm{i}") for i in range(2)]
        bx1b = None
        bx1T, bh0, bh1, bsp, brt = (Buf(n) for n in
            ("x1T", "h0T", "h1T", "spT", "routed"))
        bact = [Buf(f"actT{i}") for i in range(2)]
        bcbc, bcT, bgexp, bselg, bmxg, bt4g, baccg, brivg, bcomb = (
            Buf(n) for n in ("c_bc", "cT", "gexp", "selg", "mxg", "t4g",
                             "accg", "rivg", "comb"))
        bgtmp = Buf("gtmp")
        bworkg = [Buf(f"workg{i}") for i in range(2)]
        bzsqB = Buf("zsqB")
        blnB4 = [[Buf(f"lnB{t}{i}") for i in range(4)] for t in range(4)]
        bwsl = [Buf(f"wsl{i}") for i in range(3)]
        bPP = [[Buf("Bp0a"), Buf("Bp0b")], [Buf("Bp1a"), Buf("Bp1b")]]
        bpT1 = Buf("BpT1")
        bpAVXb2 = Buf("BpAVX")
        bpTRb2 = Buf("BpTR")
        for bb in ([bx1tm[0], bx1tm[1], bx1T, bh0, bh1, bsp, brt,
                    bcbc, bcT, bgexp, bselg, bmxg, bt4g, baccg, brivg, bcomb,
                    bzsqB, bPP[0][0], bPP[0][1], bPP[1][0], bPP[1][1], bpT1,
                    bpAVXb2, bpTRb2] + bact + bworkg + bwsl
                   + [b for bs in blnB4 for b in bs]):
            bb.r = list(fenceA)
        fwaits = fenceA

        deferB = []

        def flushB(k):
            if deferB:
                deferB[0][k]()
                if k == 3:
                    deferB.pop(0)

        NU = 36

        def unit_src(name, sub):
            t = wb[name]
            if name in ("b_ew1", "b_ew3", "b_ew2"):
                return t[sub].rearrange("(k p) f -> p k f", p=128)
            if name in ("b_shw1", "b_shw3"):
                return t[:, sub * 1024:(sub + 1) * 1024].rearrange(
                    "(k p) f -> p k f", p=128)
            if name == "b_shw2":
                return t[sub * 1024:(sub + 1) * 1024, :].rearrange(
                    "(k p) f -> p k f", p=128)
            return t.rearrange("(k p) f -> p k f", p=128)

        pTR3b = pTR[:].rearrange("p (a b) -> p a b", b=128)
        gv = lambda ap: ap.rearrange("p (a b) -> p a b", b=NE)
        PP = [pS2[:], pS1[:, 0:1024]]

        def emit_x1_load(cc):
            pcc = cc % 2
            P.dma("gpsimd", "dXT0" if pcc == 0 else "dXT1", x1tm[pcc][:, :, :],
                  x1s[cc * 512:(cc + 1) * 512, :].rearrange(
                      "(t p) f -> p t f", p=128),
                  reads=[gbuf("x1s0"), gbuf("x1s1")], writes=[bx1tm[pcc]],
                  waits=(fwaits if cc < 2 else ()))

        NCHB = LIMIT_BCHUNKS or NCH_B
        for c in range(NCHB):
            pc = c % 2
            u_ctr = [0]

            def next_unit(name, sub=None):
                ui = u_ctr[0]
                u_ctr[0] += 1
                sl = (c * NU + ui) % 3
                tok = P.dma("sync", f"dWU{sl}", wsl[sl][:, :, :],
                            unit_src(name, sub),
                            reads=[gbuf(name)], writes=[bwsl[sl]],
                            waits=(fwaits if c == 0 and ui < 3 else ()))
                return sl, tok

            def mm_block(pr, half, w_sl, m, rhs_ap, rhs_buf):
                for k in range(8):
                    P.op("tensor",
                         (lambda kk, mm_, hh, ws, dd: lambda e: e.matmul(
                             dd[:, ds(hh, 512)], wsl[ws][:, kk, ds(mm_)],
                             rhs_ap[:, kk, :], start=(kk == 0),
                             stop=(kk == 7)))(k, m, half, w_sl, PP[pr]),
                         reads=[bwsl[w_sl], rhs_buf],
                         writes=([bPP[pr][half]] if k == 0 else []),
                         inc=(k == 7))

            if c == 0:
                emit_x1_load(0)
            for ts4 in range(4):
                pr = ts4 % 2
                for j in range(8):
                    P.op("tensor",
                         (lambda tt, jj, pp, dd: lambda e: e.transpose(
                             dd[:, jj * 128:(jj + 1) * 128],
                             x1tm[pp][:, tt, ds(jj)], sbc["id32"][:]))(
                                 ts4, j, pc, PP[pr]),
                         reads=[bx1tm[pc], cb["id32"]],
                         writes=([bPP[pr][0], bPP[pr][1]] if j == 0 else []),
                         inc=(j == 7),
                         waits=(fwaits if c == 0 and ts4 == 0 and j == 0 else ()))
                P.op("vector", (lambda tt, dd: lambda e: e.tensor_copy(
                    x1T[:, :, ds(tt)],
                    dd.rearrange("p (a b) -> p a b", b=128)))(ts4, PP[pr]),
                    reads=[bPP[pr][0], bPP[pr][1]], writes=[bx1T],
                    waits=(fwaits if c == 0 and ts4 == 0 else ()))
            flushB(0)

            sl, _ = next_unit("b_lin0")
            for mp in range(4):
                pr = mp % 2
                for m2 in range(2):
                    mm_block(pr, m2, sl, 2 * mp + m2, x1T, bx1T)
                for m2 in range(2):
                    m = 2 * mp + m2
                    P.op("vector", (lambda mm_, m2_, dd: lambda e: e.tensor_scalar(
                        h0T[:, mm_, :], dd[:, ds(m2_, 512)],
                        biases["lin0_b"][:, mm_:mm_ + 1], None, op0=ALU.add))(
                            m, m2, PP[pr]),
                        reads=[bPP[pr][m2], cb["lin0_b"]], writes=[bh0])
            flushB(1)
            sl, _ = next_unit("b_s1w1")
            for mp in range(4):
                pr = mp % 2
                for m2 in range(2):
                    mm_block(pr, m2, sl, 2 * mp + m2, h0T, bh0)
                for m2 in range(2):
                    m = 2 * mp + m2
                    P.op("scalar", (lambda mm_, m2_, dd: lambda e: e.activation(
                        actT[0][:, mm_, :], dd[:, ds(m2_, 512)], AF.Silu,
                        bias=biases["s1b1"][:, mm_:mm_ + 1]))(m, m2, PP[pr]),
                        reads=[bPP[pr][m2], cb["s1b1"]], writes=[bact[0]])
            sl, _ = next_unit("b_s1w2")
            for mp in range(4):
                pr = mp % 2
                for m2 in range(2):
                    mm_block(pr, m2, sl, 2 * mp + m2, h0T, bh0)
                for m2 in range(2):
                    m = 2 * mp + m2
                    P.op("vector", (lambda mm_, m2_, dd: lambda e:
                         e.scalar_tensor_tensor(
                        h1T[:, mm_, :], dd[:, ds(m2_, 512)],
                        biases["s1b2"][:, mm_:mm_ + 1], actT[0][:, mm_, :],
                        op0=ALU.add, op1=ALU.mult))(m, m2, PP[pr]),
                        reads=[bPP[pr][m2], cb["s1b2"], bact[0]], writes=[bh1])
            flushB(2)
            for ts4 in range(4):
                for k in range(8):
                    P.op("tensor",
                         (lambda tt, kk: lambda e: e.matmul(
                             pAVX[:, tt * NE:(tt + 1) * NE], h1T[:, kk, ds(tt)],
                             ggT[:, kk, :], start=(kk == 0), stop=(kk == 7)))(ts4, k),
                         reads=[bh1, gbuf("ggT")],
                         writes=([bpAVXb2] if (ts4 == 0 and k == 0) else []),
                         inc=(k == 7))
            P.op("scalar", lambda e: e.activation(gexp[:], pAVX[:, 0:4 * NE], AF.Exp),
                 reads=[bpAVXb2], writes=[bgexp])
            P.op("vector", lambda e: e.tensor_copy(workg[0][:], gexp[:]),
                 reads=[bgexp], writes=[bworkg[0]])
            for i in range(4):
                src, bsrc = workg[i % 2], bworkg[i % 2]
                dst, bdst = workg[(i + 1) % 2], bworkg[(i + 1) % 2]
                mdst, bmdst = (t4g, bt4g) if i == 3 else (mxg, bmxg)
                P.op("vector", (lambda s_, md: lambda e: e.reduce_max(
                    md[:], gv(s_[:]), axis=AX.X))(src, mdst),
                    reads=[bsrc], writes=[bmdst])
                if i == 0:
                    P.op("vector", lambda e: e.tensor_copy(accg[:], mxg[:]),
                         reads=[bmxg], writes=[baccg])
                else:
                    P.op("vector", (lambda md: lambda e: e.tensor_tensor(
                        accg[:], accg[:], md[:], op=ALU.add))(mdst),
                        reads=[bmdst, baccg], writes=[baccg])
                if i < 3:
                    P.op("vector", (lambda s_: lambda e: e.tensor_tensor(
                        gv(selg[:]), gv(s_[:]),
                        mxg[:].unsqueeze(2).to_broadcast((128, 4, NE)),
                        op=ALU.is_ge))(src),
                        reads=[bsrc, bmxg], writes=[bselg])
                    P.op("vector", (lambda s_: lambda e: e.scalar_tensor_tensor(
                        gtmp[:], s_[:], 1.0, selg[:], op0=ALU.add,
                        op1=ALU.mult))(src),
                        reads=[bsrc, bselg], writes=[bgtmp])
                    P.op("vector", (lambda s_, d_: lambda e: e.tensor_tensor(
                        d_[:], s_[:], gtmp[:], op=ALU.subtract))(src, dst),
                        reads=[bsrc, bgtmp], writes=[bdst])
            P.op("vector", lambda e: e.reciprocal(rivg[:], accg[:]),
                 reads=[baccg], writes=[brivg])
            P.op("vector", lambda e: e.tensor_tensor(
                gv(selg[:]), gv(gexp[:]),
                t4g[:].unsqueeze(2).to_broadcast((128, 4, NE)), op=ALU.is_ge),
                reads=[bgexp, bt4g], writes=[bselg])
            P.op("vector", lambda e: e.tensor_tensor(
                gv(gexp[:]), gv(gexp[:]),
                rivg[:].unsqueeze(2).to_broadcast((128, 4, NE)), op=ALU.mult),
                reads=[brivg, bgexp], writes=[bgexp])
            P.op("vector", lambda e: e.tensor_tensor(
                combine[:].rearrange("p a b -> p (a b)"), gexp[:], selg[:],
                op=ALU.mult), reads=[bgexp, bselg], writes=[bcomb])

            def emit_combine_bcast():
                for ts4 in range(4):
                    P.op("tensor", (lambda tt: lambda e: e.transpose(
                        pTRbf[0:NE, ds(tt)], combine[:, tt, :], sbc["idb"][:]))(ts4),
                        reads=[bcomb, cb["idb"]],
                        writes=([bpTRb2] if ts4 == 0 else []), inc=(ts4 == 3))
                P.op("vector", lambda e: e.tensor_copy(cT[:], pTRbf[0:NE, 0:512]),
                     reads=[bpTRb2], writes=[bcT])
                P.dma("gpsimd", "dCC", ccs, cT[:], reads=[bcT],
                      writes=[gbuf("ccs")])
                P.dma("gpsimd", "dCC", c_bc[:].rearrange("p a b -> p (a b)"),
                      ccs.rearrange("a b -> (a b)").unsqueeze(0)
                      .to_broadcast((128, NE * 512)),
                      reads=[gbuf("ccs")], writes=[bcbc])
            flushB(3)
            if c + 1 < NCHB:
                emit_x1_load(c + 1)
            rflat = routed[:].rearrange("p a b -> p (a b)")

            def emit_gu(e_i):
                pe = e_i % 2
                aflat = actT[pe][:].rearrange("p a b -> p (a b)")
                sl1, _ = next_unit("b_ew1", e_i)
                for mp in range(4):
                    pr = mp % 2
                    for m2 in range(2):
                        mm_block(pr, m2, sl1, 2 * mp + m2, h1T, bh1)
                    P.op("scalar", (lambda mp_, fl, dd: lambda e: e.activation(
                        fl[:, mp_ * 1024:(mp_ + 1) * 1024], dd, AF.Silu))(
                            mp, aflat, PP[pr]),
                        reads=[bPP[pr][0], bPP[pr][1]], writes=[bact[pe]])
                sl3, _ = next_unit("b_ew3", e_i)
                for mp in range(4):
                    pr = mp % 2
                    for m2 in range(2):
                        mm_block(pr, m2, sl3, 2 * mp + m2, h1T, bh1)
                    P.op("vector", (lambda mp_, fl, dd: lambda e: e.tensor_tensor(
                        fl[:, mp_ * 1024:(mp_ + 1) * 1024],
                        fl[:, mp_ * 1024:(mp_ + 1) * 1024], dd,
                        op=ALU.mult))(mp, aflat, PP[pr]),
                        reads=[bPP[pr][0], bPP[pr][1], bact[pe]], writes=[bact[pe]])

            def emit_fold(e_i):
                pe = e_i % 2
                P.op("vector", (lambda e_, pp: lambda e: e.tensor_tensor(
                    actT[pp][:], actT[pp][:],
                    c_bc[:, e_, :].unsqueeze(1).to_broadcast((128, 8, 512)),
                    op=ALU.mult))(e_i, pe),
                    reads=[bcbc, bact[pe]], writes=[bact[pe]])

            def emit_y(e_i):
                pe = e_i % 2
                sl2, _ = next_unit("b_ew2", e_i)
                for mp in range(4):
                    pr = mp % 2
                    for m2 in range(2):
                        mm_block(pr, m2, sl2, 2 * mp + m2, actT[pe], bact[pe])
                    if e_i == 0:
                        P.op("vector", (lambda mp_, dd: lambda e: e.tensor_copy(
                            rflat[:, mp_ * 1024:(mp_ + 1) * 1024], dd))(
                                mp, PP[pr]),
                            reads=[bPP[pr][0], bPP[pr][1]], writes=[brt])
                    else:
                        P.op("vector", (lambda mp_, dd: lambda e: e.tensor_tensor(
                            rflat[:, mp_ * 1024:(mp_ + 1) * 1024],
                            rflat[:, mp_ * 1024:(mp_ + 1) * 1024], dd,
                            op=ALU.add))(mp, PP[pr]),
                            reads=[bPP[pr][0], bPP[pr][1], brt], writes=[brt])

            emit_gu(0)
            emit_combine_bcast()
            emit_fold(0)
            for e_i in range(1, NE):
                emit_gu(e_i)
                emit_fold(e_i)
                emit_y(e_i - 1)
            emit_y(NE - 1)
            spflat = spT[:].rearrange("p a b -> p (a b)")
            for half in range(2):
                sl, _ = next_unit("b_shw1", half)
                for mp in range(4):
                    pr = mp % 2
                    for m2 in range(2):
                        mm_block(pr, m2, sl, 2 * mp + m2, h1T, bh1)
                    P.op("scalar", (lambda hh, mp_, dd: lambda e: e.activation(
                        spflat[:, hh * 4096 + mp_ * 1024:
                               hh * 4096 + (mp_ + 1) * 1024],
                        dd, AF.Silu))(half, mp, PP[pr]),
                        reads=[bPP[pr][0], bPP[pr][1]], writes=[bsp])
            for half in range(2):
                sl, _ = next_unit("b_shw3", half)
                for mp in range(4):
                    pr = mp % 2
                    for m2 in range(2):
                        mm_block(pr, m2, sl, 2 * mp + m2, h1T, bh1)
                    P.op("vector", (lambda hh, mp_, dd: lambda e: e.tensor_tensor(
                        spflat[:, hh * 4096 + mp_ * 1024:
                               hh * 4096 + (mp_ + 1) * 1024],
                        spflat[:, hh * 4096 + mp_ * 1024:
                               hh * 4096 + (mp_ + 1) * 1024],
                        dd, op=ALU.mult))(half, mp, PP[pr]),
                        reads=[bPP[pr][0], bPP[pr][1], bsp], writes=[bsp])
            slA, _ = next_unit("b_shw2", 0)
            slB, _ = next_unit("b_shw2", 1)
            for mp in range(4):
                pr = mp % 2
                for m2 in range(2):
                    m = 2 * mp + m2
                    for k16 in range(16):
                        slx = slA if k16 < 8 else slB
                        P.op("tensor",
                             (lambda kk, mm_, hh, ws, dd: lambda e: e.matmul(
                                 dd[:, ds(hh, 512)], wsl[ws][:, kk % 8, ds(mm_)],
                                 spT[:, kk, :], start=(kk == 0),
                                 stop=(kk == 15)))(k16, m, m2, slx, PP[pr]),
                             reads=[bwsl[slx], bsp],
                             writes=([bPP[pr][m2]] if k16 == 0 else []),
                             inc=(k16 == 15))
                for m2 in range(2):
                    m = 2 * mp + m2
                    P.op("vector", (lambda mm_, m2_, dd: lambda e:
                         e.scalar_tensor_tensor(
                        h0T[:, mm_, :], dd[:, ds(m2_, 512)], 1.0,
                        routed[:, mm_, :], op0=ALU.mult, op1=ALU.add))(
                            m, m2, PP[pr]),
                        reads=[bPP[pr][m2], brt], writes=[bh0])
            sl, _ = next_unit("b_lin1")
            for mp in range(4):
                pr = mp % 2
                for m2 in range(2):
                    mm_block(pr, m2, sl, 2 * mp + m2, h0T, bh0)
                for m2 in range(2):
                    m = 2 * mp + m2
                    P.op("vector", (lambda mm_, m2_, dd: lambda e: e.tensor_scalar(
                        h1T[:, mm_, :], dd[:, ds(m2_, 512)],
                        biases["lin1_b"][:, mm_:mm_ + 1], None, op0=ALU.add))(
                            m, m2, PP[pr]),
                        reads=[bPP[pr][m2], cb["lin1_b"]], writes=[bh1])
            sl, _ = next_unit("b_s2w1")
            for mp in range(4):
                pr = mp % 2
                for m2 in range(2):
                    mm_block(pr, m2, sl, 2 * mp + m2, h1T, bh1)
                for m2 in range(2):
                    m = 2 * mp + m2
                    P.op("scalar", (lambda mm_, m2_, dd: lambda e: e.activation(
                        actT[0][:, mm_, :], dd[:, ds(m2_, 512)], AF.Silu,
                        bias=biases["s2b1"][:, mm_:mm_ + 1]))(m, m2, PP[pr]),
                        reads=[bPP[pr][m2], cb["s2b1"]], writes=[bact[0]])
            sl, _ = next_unit("b_s2w2")
            for mp in range(4):
                pr = mp % 2
                for m2 in range(2):
                    mm_block(pr, m2, sl, 2 * mp + m2, h1T, bh1)
                for m2 in range(2):
                    m = 2 * mp + m2
                    P.op("vector", (lambda mm_, m2_, dd: lambda e:
                         e.scalar_tensor_tensor(
                        routed[:, mm_, :], dd[:, ds(m2_, 512)],
                        biases["s2b2"][:, mm_:mm_ + 1], actT[0][:, mm_, :],
                        op0=ALU.add, op1=ALU.mult))(m, m2, PP[pr]),
                        reads=[bPP[pr][m2], cb["s2b2"], bact[0]], writes=[brt])
            for ts4 in range(4):
                for j in range(8):
                    P.op("tensor",
                         (lambda tt, jj: lambda e: e.transpose(
                             pS1[:, 1024 + jj * 128:1024 + (jj + 1) * 128],
                             routed[:, jj, ds(tt)], sbc["id32"][:]))(ts4, j),
                         reads=[brt, cb["id32"]],
                         writes=([bpT1] if j == 0 else []), inc=(j == 7))
                P.op("vector", (lambda tt, pp: lambda e: e.tensor_tensor(
                    x1tm[pp][:, tt, :], pS1[:, 1024:2048],
                    x1tm[pp][:, tt, :], op=ALU.add))(ts4, pc),
                    reads=[bpT1, bx1tm[pc]], writes=[bx1tm[pc]])

            def _s0(pc=pc):
                for t4 in range(4):
                    P.op("vector", (lambda tt, pp, l0: lambda e: e.reduce_sum(
                        l0[:], x1tm[pp][:, tt, :], axis=AX.X))(
                            t4, pc, lnB4[t4][0]),
                        reads=[bx1tm[pc]], writes=[blnB4[t4][0]])
                    P.op("vector", (lambda l0, l1: lambda e: e.tensor_scalar(
                        l1[:], l0[:], 1.0 / D, None, op0=ALU.mult))(
                            lnB4[t4][0], lnB4[t4][1]),
                        reads=[blnB4[t4][0]], writes=[blnB4[t4][1]])

            def _s1(pc=pc):
                for t4 in range(4):
                    P.op("vector", (lambda tt, pp, l1: lambda e: e.tensor_scalar(
                        x1tm[pp][:, tt, :], x1tm[pp][:, tt, :], l1[:], None,
                        op0=ALU.subtract))(t4, pc, lnB4[t4][1]),
                        reads=[blnB4[t4][1], bx1tm[pc]], writes=[bx1tm[pc]])

            def _s2(pc=pc):
                for t4 in range(4):
                    P.op("scalar", (lambda tt, pp, l2: lambda e: e.activation(
                        zsqB[:], x1tm[pp][:, tt, :], AF.Square,
                        accum_out=l2[:]))(t4, pc, lnB4[t4][2]),
                        reads=[bx1tm[pc]], writes=[bzsqB, blnB4[t4][2]])
                    P.op("scalar", (lambda l2, l3: lambda e: e.activation(
                        l3[:], l2[:], AF.Ln, scale=1.0 / D, bias=cEPS[:]))(
                            lnB4[t4][2], lnB4[t4][3]),
                        reads=[blnB4[t4][2], cb["eps"]], writes=[blnB4[t4][3]])
                    P.op("scalar", (lambda l3: lambda e: e.activation(
                        l3[:], l3[:], AF.Exp, scale=-0.5))(lnB4[t4][3]),
                        reads=[blnB4[t4][3]], writes=[blnB4[t4][3]])

            def _s3(c=c, pc=pc):
                for t4 in range(4):
                    P.op("vector", (lambda tt, pp, l3: lambda e: e.tensor_scalar(
                        x1tm[pp][:, tt, :], x1tm[pp][:, tt, :], l3[:], None,
                        op0=ALU.mult))(t4, pc, lnB4[t4][3]),
                        reads=[blnB4[t4][3], bx1tm[pc]], writes=[bx1tm[pc]])
                    P.dma("gpsimd", "dOUT",
                          out_d[c * 512 + t4 * 128:c * 512 + (t4 + 1) * 128, :],
                          x1tm[pc][:, t4, :], reads=[bx1tm[pc]],
                          writes=[gbuf("outd")])
            deferB.append([_s0, _s1, _s2, _s3])

        for stages in deferB:
            for f in stages:
                f()
        deferB.clear()

    fin = [Tok(s, P.count[s], None, P.gseq)
           for s in P.sems if s.startswith("d") and P.count[s]]
    P.op("gpsimd", lambda e: e.memset(neg1[0:1, :], -1.0),
         waits=fin, inc=False)

    with nc.Block() as block:
        P.emit(block)

    for cm in reversed(ctxs):
        cm.__exit__(None, None, None)
    return nc, P



def _host_prep(inputs):
    perm = np.zeros(D, dtype=np.int64)
    for h in range(NH):
        for j in range(32):
            perm[h * 64 + j] = h * 64 + 2 * j
            perm[h * 64 + 32 + j] = h * 64 + 2 * j + 1
    f16 = np.float16
    qwp = np.ascontiguousarray(np.asarray(inputs["L0_attn_qw"], f16)[:, perm])
    kwp = np.ascontiguousarray(np.asarray(inputs["L0_attn_kw"], f16)[:, perm])
    vw = np.ascontiguousarray(np.asarray(inputs["L0_attn_vw"], f16))
    ow = np.ascontiguousarray(np.asarray(inputs["L0_attn_ow"], f16))

    inv = (10000.0 ** (-(np.arange(0, HD, 2, dtype=np.float64)) / HD))
    pos = np.arange(SEQ, dtype=np.float64)
    freq = pos[:, None] * inv[None, :]

    def tab(a, scale):
        t = (a * scale).astype(np.float16)
        return np.ascontiguousarray(t.reshape(NT, 128, 32).transpose(1, 0, 2))
    cosq, sinq = tab(np.cos(freq), 1.0 / 32.0), tab(np.sin(freq), 1.0 / 32.0)
    cosk, sink = tab(np.cos(freq), 1.0), tab(np.sin(freq), 1.0)

    tri = np.zeros((128, 128), np.float16)
    for p in range(128):
        tri[p, p:] = 1.0
    dmm = np.zeros((128, 896), np.float16)
    dmm[:, 384:512] = tri
    dmm[:, 512:] = 1.0
    idb = np.eye(128).astype(ml_dtypes.bfloat16)
    idh = np.eye(128).astype(np.float16)
    id32 = np.eye(128).astype(np.float32)
    hsel = np.zeros((128, NE, NH), ml_dtypes.bfloat16)
    for pt in range(8):
        for p in range(128):
            hsel[p, pt, 2 * pt + p // 64] = 1.0

    f32 = np.float32
    return dict(
        qwp=qwp, kwp=kwp, vw=vw, ow=ow,
        cosq=cosq, sinq=sinq, cosk=cosk, sink=sink,
        tri=tri, dmm=dmm, idb=idb, idh=idh, id32=id32, hsel=hsel,
        lin0_w=np.asarray(inputs["L0_ffn_lin0_w"], f32),
        lin0_b=np.ascontiguousarray(np.asarray(inputs["L0_ffn_lin0_b"], f32).reshape(8, 128).T),
        s1w1=np.asarray(inputs["L0_ffn_swi1_w1"], f32),
        s1b1=np.ascontiguousarray(np.asarray(inputs["L0_ffn_swi1_b1"], f32).reshape(8, 128).T),
        s1w2=np.asarray(inputs["L0_ffn_swi1_w2"], f32),
        s1b2=np.ascontiguousarray(np.asarray(inputs["L0_ffn_swi1_b2"], f32).reshape(8, 128).T),
        gateT=np.ascontiguousarray(np.asarray(inputs["L0_ffn_gate_w"], f32).T),
        ew1=np.asarray(inputs["L0_ffn_exp_w1"], f32),
        ew3=np.asarray(inputs["L0_ffn_exp_w3"], f32),
        ew2=np.asarray(inputs["L0_ffn_exp_w2"], f32),
        shw1=np.asarray(inputs["L0_ffn_sh_w1"], f32),
        shw3=np.asarray(inputs["L0_ffn_sh_w3"], f32),
        shw2=np.asarray(inputs["L0_ffn_sh_w2"], f32),
        lin1_w=np.asarray(inputs["L0_ffn_lin1_w"], f32),
        lin1_b=np.ascontiguousarray(np.asarray(inputs["L0_ffn_lin1_b"], f32).reshape(8, 128).T),
        s2w1=np.asarray(inputs["L0_ffn_swi2_w1"], f32),
        s2b1=np.ascontiguousarray(np.asarray(inputs["L0_ffn_swi2_b1"], f32).reshape(8, 128).T),
        s2w2=np.asarray(inputs["L0_ffn_swi2_w2"], f32),
        s2b2=np.ascontiguousarray(np.asarray(inputs["L0_ffn_swi2_b2"], f32).reshape(8, 128).T),
    )


def kernel(**inputs):
    base = _host_prep(inputs)
    x = np.asarray(inputs["L0_attn_x"], np.float16)
    in_maps = []
    for ci in range(NCORES):
        m = dict(base)
        m["x"] = np.ascontiguousarray(x[ci * NSEQ:(ci + 1) * NSEQ])
        in_maps.append(m)
    nc, _ = build_nc("full")
    res = run_bass_kernel_spmd(nc, in_maps, core_ids=list(range(NCORES)))
    shards = [np.asarray(r["out"], np.float32).reshape(NSEQ, SEQ, D)
              for r in res.results]
    return np.concatenate(shards, axis=0)


if __name__ == "__main__":
    nc, P = build_nc("full")
    print("ops:", P.nops)


# version 29
# speedup vs baseline: 1.2181x; 1.0624x over previous
import numpy as np
import ml_dtypes
import concourse.bass as bass
import concourse.mybir as mybir
from concourse.bass_utils import run_bass_kernel_spmd

F32 = mybir.dt.float32
F16 = mybir.dt.float16
BF16 = mybir.dt.bfloat16
U8 = mybir.dt.uint8
I32 = mybir.dt.int32
AF = mybir.ActivationFunctionType
ALU = mybir.AluOpType
AX = mybir.AxisListType

NCORES = 8
BATCH, SEQ, D = 32, 2048, 1024
NH, HD = 16, 64
NSEQ = BATCH // NCORES
TPC = NSEQ * SEQ
NT = SEQ // 128
NCHK = 4
CH = 512
NCH_B = TPC // CH
LIMIT_BCHUNKS = None
LIMIT_ASEQS = None
NE = 8
EPS = 1e-5

ENGINES = ("sync", "tensor", "vector", "scalar", "gpsimd")
SEMKEY = {"sync": "sy", "tensor": "te", "vector": "ve", "scalar": "sc",
          "gpsimd": "gp"}
INORDER_SKIP = ("tensor",)


class Tok:
    __slots__ = ("sem", "v", "eng", "seq")

    def __init__(self, sem, v, eng, seq):
        self.sem, self.v, self.eng, self.seq = sem, v, eng, seq


class Buf:
    __slots__ = ("name", "w", "r")

    def __init__(self, name):
        self.name = name
        self.w = None
        self.r = []


class Prog:
    def __init__(self, nc, sems):
        self.nc = nc
        self.sems = sems
        self.count = {n: 0 for n in sems}
        self.q = {e: [] for e in ENGINES}
        self.nops = {e: 0 for e in ENGINES}
        self.gseq = 0
        self.last_real = {e: None for e in ENGINES}

    def fence(self):
        toks = []
        for e in ENGINES:
            lr = self.last_real[e]
            if lr is not None:
                toks.append(lr)
        for s in self.sems:
            if s.startswith("d") and self.count[s]:
                toks.append(Tok(s, self.count[s], None, self.gseq))
        return toks

    def op(self, eng, fn, reads=(), writes=(), waits=(), inc=True, dma=False):
        self.gseq += 1
        seq = self.gseq
        deps = list(waits)
        for b in reads:
            if b.w is not None:
                deps.append(b.w)
        for b in writes:
            if b.w is not None:
                deps.append(b.w)
            deps.extend(b.r)
        best_real = {}
        for t in deps:
            if t is not None and t.sem is not None and not t.sem.startswith("d") \
                    and t.eng is not None:
                if t.eng not in best_real or best_real[t.eng].seq < t.seq:
                    best_real[t.eng] = t
        wd = {}
        for t in deps:
            if t is None:
                continue
            if t.sem is None:
                br = best_real.get(t.eng)
                if br is not None and br.seq >= t.seq:
                    continue
                if t.eng == eng and eng in INORDER_SKIP and not dma:
                    continue
                sub = self.last_real[t.eng]
                if sub is None or sub.seq < t.seq:
                    raise RuntimeError(
                        f"unresolvable dep on {t.eng} (add inc=True there)")
                t = sub
            if (t.eng == eng and eng in INORDER_SKIP and not dma
                    and not t.sem.startswith("d")):
                continue
            if t.v > self.count[t.sem]:
                raise RuntimeError(f"forward wait {t.sem}={t.v}")
            wd[t.sem] = max(wd.get(t.sem, 0), t.v)
        if dma:
            semn = dma
            self.count[semn] += 16
            tok = Tok(semn, self.count[semn], eng, seq)
        elif inc:
            semn = SEMKEY[eng]
            self.count[semn] += 1
            tok = Tok(semn, self.count[semn], eng, seq)
        else:
            semn, tok = None, Tok(None, 0, eng, seq)
        if semn is not None:
            self.last_real[eng] = tok
        self.q[eng].append(
            (wd, fn, (self.sems[semn], 16 if dma else 1) if semn else None))
        self.nops[eng] += 1
        for b in reads:
            b.r.append(tok)
        for b in writes:
            b.w = tok
            b.r = []
        return tok

    def dma(self, eng, sem, out, in_, reads=(), writes=(), waits=(), **kw):
        return self.op(eng, lambda e: e.dma_start(out=out, in_=in_, **kw),
                       reads=reads, writes=writes, waits=waits, dma=sem)

    def emit(self, block):
        def section(engname):
            def f(eng):
                last = {}
                for wd, fn, incv in self.q[engname]:
                    for s, v in sorted(wd.items()):
                        if last.get(s, -1) < v:
                            eng.wait_ge(self.sems[s], v)
                            last[s] = v
                    inst = fn(eng)
                    if incv is not None:
                        inst.then_inc(incv[0], incv[1])
            return f
        block.sync(section("sync"))
        block.tensor(section("tensor"))
        block.vector(section("vector"))
        block.scalar(section("scalar"))
        block.gpsimd(section("gpsimd"))


class Arena:
    def __init__(self, t, nbytes):
        self.t = t
        self.nbytes = nbytes
        self.off = 0
        self.hi = 0

    def alloc(self, shape, dt, parts=128):
        esz = {F32: 4, F16: 2, BF16: 2, U8: 1, I32: 4}[dt]
        n = int(np.prod(shape)) * esz
        off = (self.off + 63) & ~63
        self.off = off + n
        self.hi = max(self.hi, self.off)
        assert self.off <= self.nbytes, f"arena overflow {self.off}"
        ap = self.t[0:parts, off:off + n].bitcast(dt)
        if len(shape) > 1:
            pat = " ".join(f"a{i}" for i in range(len(shape)))
            ap = ap.rearrange(f"p ({pat}) -> p {pat}",
                              **{f"a{i}": shape[i] for i in range(len(shape))})
        return ap

    def mark(self):
        return self.off

    def reset(self, off):
        self.off = off


def ds(i, n=128):
    return slice(i * n, (i + 1) * n)


def build_nc(mode="full"):
    nc = bass.Bass("TRN2", target_bir_lowering=False)

    x_d = nc.dram_tensor("x", [NSEQ, SEQ, D], F16, kind="ExternalInput").ap()
    att_w = {n: nc.dram_tensor(n, [D, D], F16, kind="ExternalInput").ap()
             for n in ("qwp", "kwp", "vw", "ow")}
    f32in = {}
    for n, sh in [("lin0_w", [D, D]), ("lin0_b", [128, 8]),
                  ("s1w1", [D, D]), ("s1b1", [128, 8]), ("s1w2", [D, D]),
                  ("s1b2", [128, 8]),
                  ("gateT", [D, NE]),
                  ("ew1", [NE, D, D]), ("ew3", [NE, D, D]), ("ew2", [NE, D, D]),
                  ("shw1", [D, 2 * D]), ("shw3", [D, 2 * D]), ("shw2", [2 * D, D]),
                  ("lin1_w", [D, D]), ("lin1_b", [128, 8]),
                  ("s2w1", [D, D]), ("s2b1", [128, 8]), ("s2w2", [D, D]),
                  ("s2b2", [128, 8])]:
        f32in[n] = nc.dram_tensor(n, sh, F32, kind="ExternalInput").ap()
    consts_d = {}
    for n, sh, dt in [("cosq", [128, NT, 32], F16), ("sinq", [128, NT, 32], F16),
                      ("cosk", [128, NT, 32], F16), ("sink", [128, NT, 32], F16),
                      ("tri", [128, 128], F16), ("dmm", [128, 896], F16),
                      ("idb", [128, 128], BF16),
                      ("idh", [128, 128], F16),
                      ("id32", [128, 128], F32), ("hsel", [128, NE, NH], BF16)]:
        consts_d[n] = nc.dram_tensor(n, sh, dt, kind="ExternalInput").ap()

    x1s_kind = {"full": "Internal", "A": "ExternalOutput", "B": "ExternalInput"}[mode]
    x1s = nc.dram_tensor("x1s", [TPC, D], F32, kind=x1s_kind).ap()
    out_d = nc.dram_tensor("out", [TPC, D], F32, kind="ExternalOutput").ap()

    wb = {}
    for n, sh in [("b_lin0", [D, D]), ("b_s1w1", [D, D]), ("b_s1w2", [D, D]),
                  ("b_gateT", [D, NE]),
                  ("b_ew1", [NE, D, D]), ("b_ew3", [NE, D, D]), ("b_ew2", [NE, D, D]),
                  ("b_shw1", [D, 2 * D]), ("b_shw3", [D, 2 * D]), ("b_shw2", [2 * D, D]),
                  ("b_lin1", [D, D]), ("b_s2w1", [D, D]), ("b_s2w2", [D, D])]:
        wb[n] = nc.dram_tensor(n, sh, BF16, kind="Internal").ap()
    ccs = nc.dram_tensor("ccs", [NE, CH], BF16, kind="Internal").ap()

    ctxs = []

    def enter(cm):
        ctxs.append(cm)
        return cm.__enter__()

    ARENA_BYTES = 201 * 1024
    arena_t = enter(nc.sbuf_tensor("arena", [128, ARENA_BYTES], U8))
    ar = Arena(arena_t, ARENA_BYTES)

    sbc = {}
    for n in ("cosq", "sinq", "cosk", "sink"):
        sbc[n] = enter(nc.sbuf_tensor(n + "_s", [128, NT, 32], F16))
    sbc["tri"] = enter(nc.sbuf_tensor("tri_s", [128, 128], F16))
    sbc["dmm"] = ar.alloc([896], F16)
    sbc["idb"] = enter(nc.sbuf_tensor("idb_s", [128, 128], BF16))
    sbc["idh"] = enter(nc.sbuf_tensor("idh_s", [128, 128], F16))
    sbc["id32"] = enter(nc.sbuf_tensor("id32_s", [128, 128], F32))
    sbc["hsel"] = enter(nc.sbuf_tensor("hsel_s", [128, NE, NH], BF16))
    biases = {n: enter(nc.sbuf_tensor(n + "_s", [128, 8], F32))
              for n in ("lin0_b", "s1b1", "s1b2", "lin1_b", "s2b1", "s2b2")}
    ggT = enter(nc.sbuf_tensor("ggT", [128, 8, NE], BF16))
    neg1 = enter(nc.sbuf_tensor("neg1", [128, 4 * NE], F32))
    cEPS = enter(nc.sbuf_tensor("cEPS", [128, 1], F32))
    cTINY = enter(nc.sbuf_tensor("cTINY", [128, 1], F32))
    ones128 = enter(nc.sbuf_tensor("ones128", [1, 128], F32))

    pS1 = enter(nc.psum_tensor("pS1", [128, 2048], F32))
    pS2 = enter(nc.psum_tensor("pS2", [128, 1024], F32))
    pAVX = enter(nc.psum_tensor("pAVX", [128, 512], F32))
    pTR = enter(nc.psum_tensor("pTR", [128, 1024], F16))
    pTRbf = pTR[:].bitcast(BF16)

    semnames = ("dC", "dGG", "dP0i", "dP0o", "dP1i", "dP1o", "dP2i", "dP2o",
                "dWU0", "dWU1", "dWU2", "dWA", "dWB",
                "dXI0", "dXI1", "dXR", "dXR1", "dX1S", "dX1S2", "dRDS", "dDEN",
                "dRD2",
                "dRU0", "dRU1", "dBB", "dCC", "dXT0", "dXT1", "dOUT",
                "sy", "te", "ve", "sc", "gp")
    smap = {n: enter(nc.semaphore(n)) for n in semnames}
    P = Prog(nc, smap)

    BD = {}

    def gbuf(n):
        if n not in BD:
            BD[n] = Buf(n)
        return BD[n]

    _prevc = []
    for n in ("cosq", "sinq", "cosk", "sink", "tri", "dmm", "idb",
              "idh", "id32", "hsel"):
        dst = sbc[n] if n == "dmm" else sbc[n][:]
        t = P.dma("sync", "dC", dst, consts_d[n],
                  writes=[gbuf("c_" + n)], waits=tuple(_prevc))
        _prevc = [t]
    for n in biases:
        t = P.dma("sync", "dC", biases[n][:], f32in[n],
                  writes=[gbuf("c_" + n)], waits=tuple(_prevc))
        _prevc = [t]
    P.op("vector", lambda e: e.memset(neg1[:], -1.0), writes=[gbuf("c_neg1")])
    P.op("vector", lambda e: e.memset(cEPS[:], EPS), writes=[gbuf("c_eps")])
    P.op("vector", lambda e: e.memset(cTINY[:], 1e-35), writes=[gbuf("c_tiny")])
    P.op("vector", lambda e: e.memset(ones128[:], 1.0), writes=[gbuf("c_ones")])
    cb = {n: gbuf("c_" + n) for n in
          ("cosq", "sinq", "cosk", "sink", "tri", "dmm", "idb",
           "idh", "id32", "hsel", "neg1",
           "lin0_b", "s1b1", "s1b2", "lin1_b", "s2b1", "s2b2")}
    cb["eps"] = gbuf("c_eps")
    cb["tiny"] = gbuf("c_tiny")
    cb["ones"] = gbuf("c_ones")
    bpinit = gbuf("psum_init")
    P.op("vector", lambda e: e.memset(pS1[:], 0.0), writes=[bpinit])
    P.op("vector", lambda e: e.memset(pS2[:], 0.0), reads=[bpinit])
    P.op("vector", lambda e: e.memset(pAVX[:], 0.0), writes=[bpinit])

    units0 = []
    if mode != "A":
        def conv_mat(src, dst, name):
            R, C = src.shape
            for r0 in range(0, R, 128):
                for c0 in range(0, C, 512):
                    cw = min(512, C - c0)
                    units0.append((src[r0:r0 + 128, c0:c0 + cw],
                                   dst[r0:r0 + 128, c0:c0 + cw], name, cw))

        conv_mat(f32in["lin0_w"], wb["b_lin0"], "b_lin0")
        conv_mat(f32in["s1w1"], wb["b_s1w1"], "b_s1w1")
        conv_mat(f32in["s1w2"], wb["b_s1w2"], "b_s1w2")
        conv_mat(f32in["gateT"], wb["b_gateT"], "b_gateT")
        for ei in range(NE):
            conv_mat(f32in["ew1"][ei], wb["b_ew1"][ei], "b_ew1")
            conv_mat(f32in["ew3"][ei], wb["b_ew3"][ei], "b_ew3")
            conv_mat(f32in["ew2"][ei], wb["b_ew2"][ei], "b_ew2")
        conv_mat(f32in["shw1"], wb["b_shw1"], "b_shw1")
        conv_mat(f32in["shw3"], wb["b_shw3"], "b_shw3")
        conv_mat(f32in["shw2"], wb["b_shw2"], "b_shw2")
        conv_mat(f32in["lin1_w"], wb["b_lin1"], "b_lin1")
        conv_mat(f32in["s2w1"], wb["b_s2w1"], "b_s2w1")
        conv_mat(f32in["s2w2"], wb["b_s2w2"], "b_s2w2")

    w0in = [ar.alloc([512], F32) for _ in range(3)]
    w0out = [ar.alloc([512], BF16) for _ in range(3)]
    b_w0in = [Buf(f"w0in{i}") for i in range(3)]
    b_w0out = [Buf(f"w0out{i}") for i in range(3)]
    u0 = [0]

    def drip_phase0(k):
        for _ in range(k):
            i = u0[0]
            if i >= len(units0):
                return
            u0[0] += 1
            srcap, dstap, dname, C = units0[i]
            p = i % 3
            iv = w0in[p][:, 0:C]
            ov = w0out[p][:, 0:C]
            P.dma("sync", ("dP0i", "dP1i", "dP2i")[p], iv, srcap,
                  writes=[b_w0in[p]])
            P.op("gpsimd",
                 (lambda a, bq: lambda e: e.tensor_copy(bq, a))(iv, ov),
                 reads=[b_w0in[p]], writes=[b_w0out[p]])
            P.dma("sync", ("dP0o", "dP1o", "dP2o")[p], dstap, ov,
                  reads=[b_w0out[p]], writes=[gbuf(dname)])

    markA = ar.mark()
    if mode != "B":
        xT = ar.alloc([8, 2048], F16)
        KrT = ar.alloc([8, 2048], F16)
        Vaug = ar.alloc([NT, NH * 65], F16)
        QrT = ar.alloc([8, 512], F16)
        PTb = ar.alloc([NT, 512], F16)
        mergedT = ar.alloc([8, 512], F16)

        watt = ar.alloc([8, 1024], F16)
        wattB = ar.alloc([8, 1024], F16)
        xin = [ar.alloc([1024], F16) for _ in range(2)]
        qkrope = [ar.alloc([1024], F16) for _ in range(2)]
        rtmpA = [ar.alloc([512], F16) for _ in range(2)]
        rtmpB = [ar.alloc([512], F16) for _ in range(2)]
        xres = ar.alloc([2, 1024], F16)
        zb = [ar.alloc([1024], F32), None, None, None]
        sqQT = [ar.alloc([8, 128], F16) for _ in range(2)]
        rdenS = ar.alloc([1536], F16, parts=1)
        ones64 = ar.alloc([64], F16, parts=1)
        kn2 = ar.alloc([1], F32, parts=16)
        qn2 = ar.alloc([1], F32, parts=16)
        qk2 = ar.alloc([1], F32, parts=16)
        btmp = ar.alloc([1], F32, parts=16)
        bnegrow = ar.alloc([16], F32, parts=1)
        negb = ar.alloc([16], F32)
        mx16 = ar.alloc([1], F32, parts=16)
        lnrow = [[ar.alloc([1], F32) for _ in range(4)] for _ in range(4)]

        bxT, bKrT, bVaug, bQrT, bPT, bmg = (Buf(n) for n in
            ("xT", "KrT", "Vaug", "QrT", "PT", "mergedT"))

        bwatt = Buf("watt")
        bwattB = Buf("wattB")
        bxin = [Buf("xin0"), Buf("xin1")]
        bqkrope = [Buf("qkrope0"), Buf("qkrope1")]
        brtmpA = [Buf("rtmpA0"), Buf("rtmpA1")]
        brtmpB = [Buf("rtmpB0"), Buf("rtmpB1")]
        bxres = [Buf("xres0"), Buf("xres1")]
        bzb = [Buf(f"zb{i}") for i in range(4)]
        bsqQT = [Buf("sqQT0"), Buf("sqQT1")]
        brdS = [Buf("rdS0"), Buf("rdS1"), Buf("rdS2")]
        bones64 = Buf("ones64")
        bkn2, bqn2, bqk2, bbtmp = Buf("kn2"), Buf("qn2"), Buf("qk2"), Buf("btmp")
        bbnegrow, bnegb, bmx16 = Buf("bnegrow"), Buf("negb"), Buf("mx16")
        blnrow = [[Buf(f"lnrow{q}{i}") for i in range(4)] for q in range(4)]
        bqA, bqB, bqC = Buf("AqA"), Buf("AqB"), Buf("AqC")
        bpAVXb, bpTRb = Buf("ApAVX"), Buf("ApTR")
        for _bq in (bqA, bqB, bqC, bpAVXb):
            _bq.w = gbuf("psum_init").w

        pTR3 = pTR[:].rearrange("p (a b) -> p a b", b=128)
        pTRf32 = pTR[:].bitcast(F32)
        P.op("vector", lambda e: e.memset(ones64[:], 1.0), writes=[bones64])
        zb[1] = PTb[:, 2:6, :].rearrange("p a b -> p (a b)").bitcast(F32)
        zb[2] = PTb[:, 6:10, :].rearrange("p a b -> p (a b)").bitcast(F32)
        zb[3] = PTb[:, 10:14, :].rearrange("p a b -> p (a b)").bitcast(F32)

        def load_watt(name, dst=None, dbuf=None, sem="dWA"):
            return P.dma("scalar", sem, (dst if dst is not None else watt)[:, :, :],
                         att_w[name].rearrange("(k p) f -> p k f", p=128),
                         writes=[dbuf if dbuf is not None else bwatt])

        def rope6(dst, psrc, ct, st, bpsrc, bdst, btabs, par):
            E = psrc.rearrange("p (h j) -> p h j", j=64)[:, :, 0:32]
            O = psrc.rearrange("p (h j) -> p h j", j=64)[:, :, 32:64]
            dE = dst.rearrange("p (h j) -> p h j", j=64)[:, :, 0:32]
            dO = dst.rearrange("p (h j) -> p h j", j=64)[:, :, 32:64]
            ctb = ct.unsqueeze(1).to_broadcast((128, 16, 32))
            stb = st.unsqueeze(1).to_broadcast((128, 16, 32))
            tA = rtmpA[par][:].rearrange("p (h j) -> p h j", j=32)
            tB = rtmpB[par][:].rearrange("p (h j) -> p h j", j=32)
            brA, brB = brtmpA[par], brtmpB[par]
            P.op("vector", lambda e: e.tensor_tensor(tA, E, ctb, op=ALU.mult),
                 reads=[bpsrc] + btabs, writes=[brA])
            P.op("vector", lambda e: e.tensor_tensor(tB, O, stb, op=ALU.mult),
                 reads=[bpsrc], writes=[brB])
            P.op("vector", lambda e: e.tensor_tensor(dE, tA, tB, op=ALU.subtract),
                 reads=[brA, brB], writes=[bdst])
            P.op("vector", lambda e: e.tensor_tensor(tA, E, stb, op=ALU.mult),
                 reads=[bpsrc], writes=[brA])
            P.op("vector", lambda e: e.tensor_tensor(tB, O, ctb, op=ALU.mult),
                 reads=[bpsrc], writes=[brB])
            P.op("vector", lambda e: e.tensor_tensor(dO, tA, tB, op=ALU.add),
                 reads=[brA, brB], writes=[bdst])

        def transp8_evac(srcap, bsrc, dstap, bdst):
            for j in range(8):
                P.op("tensor",
                     (lambda jj: lambda e: e.transpose(
                         pTR[:, ds(jj)], srcap[:, ds(jj)], sbc["idh"][:]))(j),
                     reads=[bsrc, cb["idh"]], writes=([bpTRb] if j == 0 else []),
                     inc=(j == 7))
            P.op("scalar", lambda e: e.copy(dstap, pTR3),
                 reads=[bpTRb], writes=[bdst])

        genPP = None

        def gen_proj(t, w_ap, bw, pr):
            dst, dbuf = genPP[pr]
            for n in range(2):
                for k in range(8):
                    P.op("tensor",
                         (lambda nn, kk, tt, dd: lambda e: e.matmul(
                             dd[:, ds(nn, 512)], xT[:, kk, ds(tt)],
                             w_ap[:, kk, ds(nn, 512)],
                             start=(kk == 0), stop=(kk == 7)))(n, k, t, dst),
                         reads=[bxT, bw],
                         writes=([dbuf] if (n == 0 and k == 0) else []),
                         inc=(n == 1 and k == 7))

        def norm_tile_sq(srcT, bsrcT, t):
            P.op("scalar", (lambda tt: lambda e: e.activation(
                sqQT[tt % 2][:, 0:8, :], srcT[:, :, ds(tt)], AF.Square))(t),
                reads=[bsrcT], writes=[bsqQT[t % 2]])

        def norm_tile_mm(t, dstv, bdstv, first):
            _t = None
            for pt in range(8):
                _t = P.op("tensor", (lambda p_, tt: lambda e: e.matmul(
                    pAVX[0:16, 0:128], sbc["hsel"][:, p_, :],
                    sqQT[tt % 2][:, p_, :], start=(p_ == 0),
                    stop=(p_ == 7)))(pt, t),
                    reads=[bsqQT[t % 2], cb["hsel"]],
                    writes=([bpAVXb] if pt == 0 else []), inc=(pt == 7))
            bpAVXb.w = _t
            P.op("vector", lambda e: e.reduce_max(mx16[:], pAVX[0:16, 0:128],
                                                  axis=AX.X),
                 reads=[bpAVXb], writes=[bmx16])
            if first:
                P.op("vector", (lambda dv: lambda e: e.tensor_copy(
                    dv[:], mx16[:]))(dstv),
                    reads=[bmx16], writes=[bdstv])
            else:
                P.op("vector", (lambda dv: lambda e: e.tensor_tensor(
                    dv[:], dv[:], mx16[:], op=ALU.max))(dstv),
                    reads=[bmx16, bdstv], writes=[bdstv])

        genPP = [(pS2[:], bqA), (pS1[:, 0:1024], bqB)]
        SCB = [(pS2[:, 0:1024], bqA), (pS1[:, 0:1024], bqB),
               (pS1[:, 1024:2048], bqC)]

        deferred_ln = []

        def flush_ln():
            for f in deferred_ln:
                f()
            deferred_ln.clear()

        for b in range(LIMIT_ASEQS or NSEQ):
            load_watt("kwp")
            load_watt("ow", dst=wattB, dbuf=bwattB, sem="dWB")
            for t in range(NT + 3):
                if t >= 3:
                    tn = t - 3
                    norm_tile_mm(tn, kn2, bkn2, tn == 0)
                if 2 <= t < NT + 2:
                    tp0 = t - 2
                    transp8_evac(qkrope[tp0 % 2][:], bqkrope[tp0 % 2],
                                 KrT[:, :, ds(tp0)], bKrT)
                    norm_tile_sq(KrT, bKrT, tp0)
                if t < NT:
                    p = t % 2
                    if t == 0:
                        P.dma("scalar", "dXI0", xin[0][:],
                              x_d[b, ds(0), :], writes=[bxin[0]])
                    transp8_evac(xin[p][:], bxin[p], xT[:, :, ds(t)], bxT)
                    if t + 1 < NT:
                        P.dma("scalar", "dXI0" if t % 2 else "dXI1",
                              xin[(t + 1) % 2][:],
                              x_d[b, ds(t + 1), :], writes=[bxin[(t + 1) % 2]])
                    prg = t % 2
                    gen_proj(t, watt, bwatt, prg)
                    rope6(qkrope[t % 2][:], genPP[prg][0], sbc["cosk"][:, t, :],
                          sbc["sink"][:, t, :], genPP[prg][1], bqkrope[t % 2],
                          [cb["cosk"], cb["sink"]], t % 2)
                drip_phase0(6)
            load_watt("vw")
            P.op("vector", lambda e: e.memset(
                Vaug[:, :, :].rearrange("p t (h c) -> p t h c", c=65)
                [:, :, :, 64:65].squeeze(3), 1.0), writes=[bVaug])
            for t in range(NT):
                prg = t % 2
                gen_proj(t, watt, bwatt, prg)
                P.op("vector", (lambda tt, dd: lambda e: e.tensor_copy(
                    Vaug[:, tt, :].rearrange("p (h c) -> p h c", c=65)[:, :, 0:64],
                    dd.rearrange("p (h c) -> p h c", c=64)))(t, genPP[prg][0]),
                    reads=[genPP[prg][1]], writes=[bVaug])
                drip_phase0(6)

            load_watt("qwp")
            for c in range(NCHK):
                for ts4 in range(7):
                    if ts4 >= 3:
                        tq = ts4 - 3
                        norm_tile_mm(tq, qn2, bqn2, tq == 0)
                    if 2 <= ts4 < 6:
                        tq = ts4 - 2
                        transp8_evac(qkrope[tq % 2][:], bqkrope[tq % 2],
                                     QrT[:, :, ds(tq)], bQrT)
                        norm_tile_sq(QrT, bQrT, tq)
                    if ts4 < 4:
                        t = c * 4 + ts4
                        p = ts4 % 2
                        prg = ts4 % 2
                        gen_proj(t, watt, bwatt, prg)
                        rope6(qkrope[ts4 % 2][:], genPP[prg][0],
                              sbc["cosq"][:, t, :],
                              sbc["sinq"][:, t, :], genPP[prg][1],
                              bqkrope[ts4 % 2],
                              [cb["cosq"], cb["sinq"]], ts4 % 2)
                drip_phase0(2)
                P.op("vector", lambda e: e.tensor_tensor(
                    qk2[:], qn2[:], kn2[:], op=ALU.mult),
                    reads=[bqn2, bkn2], writes=[bqk2])
                P.op("scalar", lambda e: e.activation(
                    btmp[:], qk2[:], AF.Ln, bias=cTINY[0:16, :]),
                    reads=[bqk2, cb["tiny"]], writes=[bbtmp])
                P.op("scalar", lambda e: e.activation(
                    btmp[:], btmp[:], AF.Exp, scale=0.5),
                    reads=[bbtmp], writes=[bbtmp])
                P.op("vector", lambda e: e.tensor_scalar(
                    btmp[:], btmp[:], -1.0, None, op0=ALU.mult),
                    reads=[bbtmp], writes=[bbtmp])
                P.op("tensor", lambda e: e.transpose(
                    pS2[0:1, 0:16], btmp[:], sbc["id32"][0:16, 0:16]),
                    reads=[bbtmp, cb["id32"]], writes=[bqA])
                P.op("vector", lambda e: e.tensor_copy(bnegrow[:], pS2[0:1, 0:16]),
                     reads=[bqA], writes=[bbnegrow])
                P.op("tensor", lambda e: e.matmul(
                    pAVX[:, 0:16], ones128[:], bnegrow[:], start=True, stop=True),
                    reads=[bbnegrow, cb["ones"]], writes=[bpAVXb])
                P.op("vector", lambda e: e.tensor_copy(negb[:], pAVX[:, 0:16]),
                     reads=[bpAVXb], writes=[bnegb])

                flush_ln()
                ktiles = 4 * (c + 1)
                groups = [(gi % 3, 2 * gi, 2) for gi in range(ktiles // 2)]

                def emit_fix(hh):
                    uu, rr = hh // 2, (hh % 2) * 64
                    off = (hh % 3) * 512
                    P.op("tensor", (lambda of: lambda e: e.matmul(
                        pTRf32[0:64, 0:512], ones64[0:1, 0:64],
                        rdenS[0:1, of:of + 512], start=True, stop=True))(off),
                        reads=[brdS[hh % 3], bones64], writes=[bpTRb])
                    P.op("vector", (lambda uu_, rr_: lambda e: e.tensor_tensor(
                        mergedT[rr_:rr_ + 64, uu_, :],
                        mergedT[rr_:rr_ + 64, uu_, :],
                        pTRf32[0:64, 0:512], op=ALU.mult))(uu, rr),
                        reads=[bpTRb], writes=[bmg])

                av_last = None
                avtok = {}
                for h in range(NH):
                    u, r0 = h // 2, (h % 2) * 64
                    gtoks = {}

                    def emit_scores_exp(gidx):
                        gsel, gk0, gsz = groups[gidx]
                        pbuf, pB = SCB[gsel]
                        for i in range(gsz):
                            kj = gk0 + i
                            r = kj - 4 * c
                            off = 128 * r if r >= 0 else 0
                            P.op("tensor",
                                 (lambda pb, ii, kk, of, uu, rr: lambda e: e.matmul(
                                     pb[:, ii * 512 + of:(ii + 1) * 512],
                                     KrT[rr:rr + 64, uu, ds(kk)],
                                     QrT[rr:rr + 64, uu, of:512],
                                     start=True, stop=True))(pbuf, i, kj, off, u, r0),
                                 reads=[bKrT, bQrT],
                                 writes=([pB] if i == 0 else []),
                                 inc=(i == gsz - 1))
                        zjs = sorted({(t - 2) // 4 + 1
                                      for t in range(gk0, gk0 + gsz)
                                      if 2 <= t <= 13})
                        avw = [t for t in (avtok.get(gk0 + gsz - 1),)
                               if t is not None]
                        texp = P.op("scalar",
                                    (lambda pb, kk0, sz_, hh: lambda e: e.activation(
                                        PTb[:, kk0:kk0 + sz_, :], pb[:, 0:sz_ * 512],
                                        AF.Exp, bias=negb[:, hh:hh + 1]))(
                                            pbuf, gk0, gsz, h),
                                    reads=[pB, bnegb], waits=avw,
                                    writes=[bPT] + [bzb[j] for j in zjs])
                        r0g = gk0 - 4 * c
                        if r0g >= 0:
                            tm = None
                            for ii in range(2):
                                rr_ = r0g + ii
                                tm = P.op("vector",
                                          (lambda kk, st: lambda e:
                                           e.tensor_tensor(
                                              PTb[:, kk, :], PTb[:, kk, :],
                                              sbc["dmm"][:, st:st + 512],
                                              op=ALU.mult))(
                                                  gk0 + ii, 384 - 128 * rr_),
                                          reads=[cb["dmm"]], writes=[bPT])
                                gtoks[gk0 + ii] = tm
                        else:
                            gtoks[gk0] = texp
                            gtoks[gk0 + 1] = texp

                    def emit_av_group(gidx):
                        nonlocal av_last
                        _, gk0, gsz = groups[gidx]
                        for kj in range(gk0, gk0 + gsz):
                            av_last = P.op(
                                "tensor",
                                (lambda kk, hh, kt: lambda e: e.matmul(
                                    pAVX[0:65, :],
                                    Vaug[:, kk, hh * 65:(hh + 1) * 65],
                                    PTb[:, kk, :], start=(kk == 0),
                                    stop=(kk == kt - 1)))(kj, h, ktiles),
                                reads=[bVaug], waits=[gtoks[kj]],
                                writes=([bpAVXb] if kj == 0 else []),
                                inc=True)
                            avtok[kj] = av_last

                    ng = len(groups)
                    for gi in range(ng):
                        emit_scores_exp(gi)
                    if h >= 2:
                        emit_fix(h - 2)
                    for gi in range(ng):
                        emit_av_group(gi)
                    bpAVXb.w = av_last
                    drip_phase0(1)
                    P.op("vector", (lambda uu, rr: lambda e: e.tensor_copy(
                        mergedT[rr:rr + 64, uu, :], pAVX[0:64, :]))(u, r0),
                        reads=[bpAVXb], writes=[bmg])
                    def _recip(of):
                        def g(e):
                            with nc.allow_low_precision(reason="f16 recip den"):
                                return e.reciprocal(rdenS[0:1, of:of + 512],
                                                    pAVX[64:65, 0:512])
                        return g
                    P.op("vector", _recip((h % 3) * 512),
                         reads=[bpAVXb], writes=[brdS[h % 3]])
                emit_fix(NH - 2)
                emit_fix(NH - 1)
                for ts4 in range(4):
                    pz = ts4
                    half = ts4 % 2
                    P.dma("scalar", "dXR" if ts4 % 2 == 0 else "dXR1",
                          xres[:, ts4 % 2, :],
                          x_d[b, c * 512 + ts4 * 128:c * 512 + (ts4 + 1) * 128, :],
                          writes=[bxres[ts4 % 2]])
                    for n in range(2):
                        for k in range(8):
                            P.op("tensor",
                                 (lambda hh, nn, kk, tt: lambda e: e.matmul(
                                     pS1[:, hh * 1024 + nn * 512:
                                         hh * 1024 + (nn + 1) * 512],
                                     mergedT[:, kk, ds(tt)],
                                     wattB[:, kk, ds(nn, 512)],
                                     start=(kk == 0), stop=(kk == 7)))(half, n, k, ts4),
                                 reads=[bmg, bwattB],
                                 writes=([bqB if half == 0 else bqC]
                                         if (n == 0 and k == 0) else []),
                                 inc=(n == 1 and k == 7))
                    zv, bzv = zb[pz], bzb[pz]
                    alias = [bPT] if pz > 0 else []
                    lr, blr = lnrow[pz], blnrow[pz]
                    P.op("vector", (lambda hh, zz, pzz: lambda e: e.tensor_tensor(
                        zz[:], pS1[:, hh * 1024:(hh + 1) * 1024],
                        xres[:, pzz, :], op=ALU.add))(half, zv, ts4 % 2),
                        reads=[bqB if half == 0 else bqC, bxres[ts4 % 2]],
                        writes=[bzv] + alias)
                    def _ln_tail(zv=zv, bzv=bzv, lr=lr, blr=blr, alias=alias,
                                 pz=pz, row0=b * SEQ + c * 512 + ts4 * 128):
                        P.op("vector", (lambda zz, l0: lambda e: e.reduce_sum(
                            l0[:], zz[:], axis=AX.X))(zv, lr[0]),
                            reads=[bzv], writes=[blr[0]])
                        P.op("vector", (lambda l0, l1: lambda e: e.tensor_scalar(
                            l1[:], l0[:], 1.0 / D, None, op0=ALU.mult))(
                                lr[0], lr[1]),
                            reads=[blr[0]], writes=[blr[1]])
                        P.op("vector", (lambda zz, l1: lambda e: e.tensor_scalar(
                            zz[:], zz[:], l1[:], None,
                            op0=ALU.subtract))(zv, lr[1]),
                            reads=[blr[1]], writes=[bzv])
                        P.op("scalar", (lambda zz, l2: lambda e: e.activation(
                            PTb[:, 14:16, :].rearrange("p a b -> p (a b)"), zz[:],
                            AF.Square, accum_out=l2[:]))(zv, lr[2]),
                            reads=[bzv], writes=[bPT, blr[2]])
                        P.op("scalar", (lambda l2, l3: lambda e: e.activation(
                            l3[:], l2[:], AF.Ln, scale=1.0 / D,
                            bias=cEPS[:]))(lr[2], lr[3]),
                            reads=[blr[2], cb["eps"]], writes=[blr[3]])
                        P.op("scalar", (lambda l3: lambda e: e.activation(
                            l3[:], l3[:], AF.Exp, scale=-0.5))(lr[3]),
                            reads=[blr[3]], writes=[blr[3]])
                        P.op("vector", (lambda zz, l3: lambda e: e.tensor_scalar(
                            zz[:], zz[:], l3[:], None, op0=ALU.mult))(zv, lr[3]),
                            reads=[blr[3]], writes=[bzv] + alias)
                        P.dma("scalar", "dX1S" if pz % 2 == 0 else "dX1S2",
                              x1s[row0:row0 + 128, :], zv[:], reads=[bzv],
                              writes=[gbuf("x1s0" if pz % 2 == 0 else "x1s1")])
                    deferred_ln.append(_ln_tail)
                drip_phase0(6)

    if mode != "B":
        flush_ln()
    if mode != "A":
        drip_phase0(len(units0))
        P.dma("sync", "dGG", ggT[:], wb["b_gateT"].rearrange("(k p) f -> p k f", p=128),
              reads=[gbuf("b_gateT")], writes=[gbuf("ggT")])
        fenceA = P.fence()
        ar.reset(markA)
        x1tm = [ar.alloc([4, 1024], F32) for _ in range(2)]
        x1T = ar.alloc([8, 512], BF16)
        h0T = ar.alloc([8, 512], BF16)
        h1T = ar.alloc([8, 512], BF16)
        actT = [ar.alloc([8, 512], BF16) for _ in range(2)]
        spT = ar.alloc([16, 512], BF16)
        routed = ar.alloc([8, 512], F32)
        c_bc = ar.alloc([NE, 512], BF16)
        cT = ar.alloc([512], BF16, parts=8)
        gexp = ar.alloc([4 * NE], F32)
        workg = [ar.alloc([4 * NE], F32) for _ in range(2)]
        selg = ar.alloc([4 * NE], F32)
        gtmp = ar.alloc([4 * NE], F32)
        mxg = ar.alloc([4], F32)
        t4g = ar.alloc([4], F32)
        accg = ar.alloc([4], F32)
        rivg = ar.alloc([4], F32)
        combine = ar.alloc([4, NE], BF16)
        zsqB = ar.alloc([1024], BF16)
        lnB4 = [[ar.alloc([1], F32) for _ in range(4)] for _ in range(4)]
        wsl = [ar.alloc([8, 1024], BF16) for _ in range(3)]

        bx1tm = [Buf(f"x1tm{i}") for i in range(2)]
        bx1b = None
        bx1T = Buf("x1T")
        bh0 = [Buf(f"h0T{i}") for i in range(8)]
        bh1 = [Buf(f"h1T{i}") for i in range(8)]
        bsp = [Buf(f"spT{i}") for i in range(16)]
        brt = [Buf(f"rt{i}") for i in range(8)]
        bact = [[Buf(f"actT{p}_{i}") for i in range(8)] for p in range(2)]
        bcbc, bcT, bgexp, bselg, bmxg, bt4g, baccg, brivg, bcomb = (
            Buf(n) for n in ("c_bc", "cT", "gexp", "selg", "mxg", "t4g",
                             "accg", "rivg", "comb"))
        bgtmp = Buf("gtmp")
        bworkg = [Buf(f"workg{i}") for i in range(2)]
        bzsqB = Buf("zsqB")
        blnB4 = [[Buf(f"lnB{t}{i}") for i in range(4)] for t in range(4)]
        bwsl = [Buf(f"wsl{i}") for i in range(3)]
        bPP = [[Buf("Bp0a"), Buf("Bp0b")], [Buf("Bp1a"), Buf("Bp1b")]]
        bpT1 = Buf("BpT1")
        bpAVXb2 = Buf("BpAVX")
        bpTRb2 = Buf("BpTR")
        for bb in ([bx1tm[0], bx1tm[1], bx1T,
                    bcbc, bcT, bgexp, bselg, bmxg, bt4g, baccg, brivg, bcomb,
                    bzsqB, bPP[0][0], bPP[0][1], bPP[1][0], bPP[1][1], bpT1,
                    bpAVXb2, bpTRb2] + bh0 + bh1 + bsp + brt
                   + bact[0] + bact[1] + bworkg + bwsl
                   + [b for bs in blnB4 for b in bs]):
            bb.r = list(fenceA)
        fwaits = fenceA

        deferB = []

        def flushB(k):
            if deferB:
                deferB[0][k]()
                if k == 3:
                    deferB.pop(0)

        NU = 36

        def unit_src(name, sub):
            t = wb[name]
            if name in ("b_ew1", "b_ew3", "b_ew2"):
                return t[sub].rearrange("(k p) f -> p k f", p=128)
            if name in ("b_shw1", "b_shw3"):
                return t[:, sub * 1024:(sub + 1) * 1024].rearrange(
                    "(k p) f -> p k f", p=128)
            if name == "b_shw2":
                return t[sub * 1024:(sub + 1) * 1024, :].rearrange(
                    "(k p) f -> p k f", p=128)
            return t.rearrange("(k p) f -> p k f", p=128)

        pTR3b = pTR[:].rearrange("p (a b) -> p a b", b=128)
        gv = lambda ap: ap.rearrange("p (a b) -> p a b", b=NE)
        PP = [pS2[:], pS1[:, 0:1024]]

        def emit_x1_load(cc):
            pcc = cc % 2
            P.dma("gpsimd", "dXT0" if pcc == 0 else "dXT1", x1tm[pcc][:, :, :],
                  x1s[cc * 512:(cc + 1) * 512, :].rearrange(
                      "(t p) f -> p t f", p=128),
                  reads=[gbuf("x1s0"), gbuf("x1s1")], writes=[bx1tm[pcc]],
                  waits=(fwaits if cc < 2 else ()))

        NCHB = LIMIT_BCHUNKS or NCH_B
        for c in range(NCHB):
            pc = c % 2
            u_ctr = [0]

            def next_unit(name, sub=None):
                ui = u_ctr[0]
                u_ctr[0] += 1
                sl = (c * NU + ui) % 3
                tok = P.dma("sync", f"dWU{sl}", wsl[sl][:, :, :],
                            unit_src(name, sub),
                            reads=[gbuf(name)], writes=[bwsl[sl]],
                            waits=(fwaits if c == 0 and ui < 3 else ()))
                return sl, tok

            def mm_block(pr, half, w_sl, m, rhs_ap, rhs_bufs):
                for k in range(8):
                    P.op("tensor",
                         (lambda kk, mm_, hh, ws, dd: lambda e: e.matmul(
                             dd[:, ds(hh, 512)], wsl[ws][:, kk, ds(mm_)],
                             rhs_ap[:, kk, :], start=(kk == 0),
                             stop=(kk == 7)))(k, m, half, w_sl, PP[pr]),
                         reads=[bwsl[w_sl], rhs_bufs[k]],
                         writes=([bPP[pr][half]] if k == 0 else []),
                         inc=(k == 7))

            if c == 0:
                emit_x1_load(0)
            for ts4 in range(4):
                pr = ts4 % 2
                for j in range(8):
                    P.op("tensor",
                         (lambda tt, jj, pp, dd: lambda e: e.transpose(
                             dd[:, jj * 128:(jj + 1) * 128],
                             x1tm[pp][:, tt, ds(jj)], sbc["id32"][:]))(
                                 ts4, j, pc, PP[pr]),
                         reads=[bx1tm[pc], cb["id32"]],
                         writes=([bPP[pr][0], bPP[pr][1]] if j == 0 else []),
                         inc=(j == 7),
                         waits=(fwaits if c == 0 and ts4 == 0 and j == 0 else ()))
                P.op("vector", (lambda tt, dd: lambda e: e.tensor_copy(
                    x1T[:, :, ds(tt)],
                    dd.rearrange("p (a b) -> p a b", b=128)))(ts4, PP[pr]),
                    reads=[bPP[pr][0], bPP[pr][1]], writes=[bx1T],
                    waits=(fwaits if c == 0 and ts4 == 0 else ()))
            flushB(0)

            sl, _ = next_unit("b_lin0")
            for mp in range(4):
                pr = mp % 2
                for m2 in range(2):
                    mm_block(pr, m2, sl, 2 * mp + m2, x1T, [bx1T] * 8)
                for m2 in range(2):
                    m = 2 * mp + m2
                    P.op("vector", (lambda mm_, m2_, dd: lambda e: e.tensor_scalar(
                        h0T[:, mm_, :], dd[:, ds(m2_, 512)],
                        biases["lin0_b"][:, mm_:mm_ + 1], None, op0=ALU.add))(
                            m, m2, PP[pr]),
                        reads=[bPP[pr][m2], cb["lin0_b"]], writes=[bh0[m]])
            flushB(1)
            sl, _ = next_unit("b_s1w1")
            for mp in range(4):
                pr = mp % 2
                for m2 in range(2):
                    mm_block(pr, m2, sl, 2 * mp + m2, h0T, bh0)
                for m2 in range(2):
                    m = 2 * mp + m2
                    P.op("scalar", (lambda mm_, m2_, dd: lambda e: e.activation(
                        actT[0][:, mm_, :], dd[:, ds(m2_, 512)], AF.Silu,
                        bias=biases["s1b1"][:, mm_:mm_ + 1]))(m, m2, PP[pr]),
                        reads=[bPP[pr][m2], cb["s1b1"]], writes=[bact[0][m]])
            sl, _ = next_unit("b_s1w2")
            for mp in range(4):
                pr = mp % 2
                for m2 in range(2):
                    mm_block(pr, m2, sl, 2 * mp + m2, h0T, bh0)
                for m2 in range(2):
                    m = 2 * mp + m2
                    P.op("vector", (lambda mm_, m2_, dd: lambda e:
                         e.scalar_tensor_tensor(
                        h1T[:, mm_, :], dd[:, ds(m2_, 512)],
                        biases["s1b2"][:, mm_:mm_ + 1], actT[0][:, mm_, :],
                        op0=ALU.add, op1=ALU.mult))(m, m2, PP[pr]),
                        reads=[bPP[pr][m2], cb["s1b2"], bact[0][m]],
                        writes=[bh1[m]])
            flushB(2)
            for ts4 in range(4):
                for k in range(8):
                    P.op("tensor",
                         (lambda tt, kk: lambda e: e.matmul(
                             pAVX[:, tt * NE:(tt + 1) * NE], h1T[:, kk, ds(tt)],
                             ggT[:, kk, :], start=(kk == 0), stop=(kk == 7)))(ts4, k),
                         reads=[bh1[k], gbuf("ggT")],
                         writes=([bpAVXb2] if (ts4 == 0 and k == 0) else []),
                         inc=(k == 7))
            P.op("scalar", lambda e: e.activation(gexp[:], pAVX[:, 0:4 * NE], AF.Exp),
                 reads=[bpAVXb2], writes=[bgexp])
            P.op("vector", lambda e: e.tensor_copy(workg[0][:], gexp[:]),
                 reads=[bgexp], writes=[bworkg[0]])
            for i in range(4):
                src, bsrc = workg[i % 2], bworkg[i % 2]
                dst, bdst = workg[(i + 1) % 2], bworkg[(i + 1) % 2]
                mdst, bmdst = (t4g, bt4g) if i == 3 else (mxg, bmxg)
                P.op("vector", (lambda s_, md: lambda e: e.reduce_max(
                    md[:], gv(s_[:]), axis=AX.X))(src, mdst),
                    reads=[bsrc], writes=[bmdst])
                if i == 0:
                    P.op("vector", lambda e: e.tensor_copy(accg[:], mxg[:]),
                         reads=[bmxg], writes=[baccg])
                else:
                    P.op("vector", (lambda md: lambda e: e.tensor_tensor(
                        accg[:], accg[:], md[:], op=ALU.add))(mdst),
                        reads=[bmdst, baccg], writes=[baccg])
                if i < 3:
                    P.op("vector", (lambda s_: lambda e: e.tensor_tensor(
                        gv(selg[:]), gv(s_[:]),
                        mxg[:].unsqueeze(2).to_broadcast((128, 4, NE)),
                        op=ALU.is_ge))(src),
                        reads=[bsrc, bmxg], writes=[bselg])
                    P.op("vector", (lambda s_: lambda e: e.scalar_tensor_tensor(
                        gtmp[:], s_[:], 1.0, selg[:], op0=ALU.add,
                        op1=ALU.mult))(src),
                        reads=[bsrc, bselg], writes=[bgtmp])
                    P.op("vector", (lambda s_, d_: lambda e: e.tensor_tensor(
                        d_[:], s_[:], gtmp[:], op=ALU.subtract))(src, dst),
                        reads=[bsrc, bgtmp], writes=[bdst])
            P.op("vector", lambda e: e.reciprocal(rivg[:], accg[:]),
                 reads=[baccg], writes=[brivg])
            P.op("vector", lambda e: e.tensor_tensor(
                gv(selg[:]), gv(gexp[:]),
                t4g[:].unsqueeze(2).to_broadcast((128, 4, NE)), op=ALU.is_ge),
                reads=[bgexp, bt4g], writes=[bselg])
            P.op("vector", lambda e: e.tensor_tensor(
                gv(gexp[:]), gv(gexp[:]),
                rivg[:].unsqueeze(2).to_broadcast((128, 4, NE)), op=ALU.mult),
                reads=[brivg, bgexp], writes=[bgexp])
            P.op("vector", lambda e: e.tensor_tensor(
                combine[:].rearrange("p a b -> p (a b)"), gexp[:], selg[:],
                op=ALU.mult), reads=[bgexp, bselg], writes=[bcomb])

            def emit_combine_bcast():
                for ts4 in range(4):
                    P.op("tensor", (lambda tt: lambda e: e.transpose(
                        pTRbf[0:NE, ds(tt)], combine[:, tt, :], sbc["idb"][:]))(ts4),
                        reads=[bcomb, cb["idb"]],
                        writes=([bpTRb2] if ts4 == 0 else []), inc=(ts4 == 3))
                P.op("vector", lambda e: e.tensor_copy(cT[:], pTRbf[0:NE, 0:512]),
                     reads=[bpTRb2], writes=[bcT])
                P.dma("gpsimd", "dCC", ccs, cT[:], reads=[bcT],
                      writes=[gbuf("ccs")])
                P.dma("gpsimd", "dCC", c_bc[:].rearrange("p a b -> p (a b)"),
                      ccs.rearrange("a b -> (a b)").unsqueeze(0)
                      .to_broadcast((128, NE * 512)),
                      reads=[gbuf("ccs")], writes=[bcbc])
            flushB(3)
            if c + 1 < NCHB:
                emit_x1_load(c + 1)
            rflat = routed[:].rearrange("p a b -> p (a b)")

            def emit_gu(e_i):
                pe = e_i % 2
                aflat = actT[pe][:].rearrange("p a b -> p (a b)")
                sl1, _ = next_unit("b_ew1", e_i)
                for mp in range(4):
                    pr = mp % 2
                    for m2 in range(2):
                        mm_block(pr, m2, sl1, 2 * mp + m2, h1T, bh1)
                    P.op("scalar", (lambda mp_, fl, dd: lambda e: e.activation(
                        fl[:, mp_ * 1024:(mp_ + 1) * 1024], dd, AF.Silu))(
                            mp, aflat, PP[pr]),
                        reads=[bPP[pr][0], bPP[pr][1]],
                        writes=[bact[pe][2 * mp], bact[pe][2 * mp + 1]])
                sl3, _ = next_unit("b_ew3", e_i)
                for mp in range(4):
                    pr = mp % 2
                    for m2 in range(2):
                        mm_block(pr, m2, sl3, 2 * mp + m2, h1T, bh1)
                    P.op("vector", (lambda mp_, fl, dd: lambda e: e.tensor_tensor(
                        fl[:, mp_ * 1024:(mp_ + 1) * 1024],
                        fl[:, mp_ * 1024:(mp_ + 1) * 1024], dd,
                        op=ALU.mult))(mp, aflat, PP[pr]),
                        reads=[bPP[pr][0], bPP[pr][1],
                               bact[pe][2 * mp], bact[pe][2 * mp + 1]],
                        writes=[bact[pe][2 * mp], bact[pe][2 * mp + 1]])

            def emit_fold(e_i):
                pe = e_i % 2
                P.op("vector", (lambda e_, pp: lambda e: e.tensor_tensor(
                    actT[pp][:], actT[pp][:],
                    c_bc[:, e_, :].unsqueeze(1).to_broadcast((128, 8, 512)),
                    op=ALU.mult))(e_i, pe),
                    reads=[bcbc] + bact[pe], writes=bact[pe])

            def emit_y(e_i):
                pe = e_i % 2
                sl2, _ = next_unit("b_ew2", e_i)
                for mp in range(4):
                    pr = mp % 2
                    for m2 in range(2):
                        mm_block(pr, m2, sl2, 2 * mp + m2, actT[pe], bact[pe])
                    if e_i == 0:
                        P.op("vector", (lambda mp_, dd: lambda e: e.tensor_copy(
                            rflat[:, mp_ * 1024:(mp_ + 1) * 1024], dd))(
                                mp, PP[pr]),
                            reads=[bPP[pr][0], bPP[pr][1]],
                            writes=[brt[2 * mp], brt[2 * mp + 1]])
                    else:
                        P.op("vector", (lambda mp_, dd: lambda e: e.tensor_tensor(
                            rflat[:, mp_ * 1024:(mp_ + 1) * 1024],
                            rflat[:, mp_ * 1024:(mp_ + 1) * 1024], dd,
                            op=ALU.add))(mp, PP[pr]),
                            reads=[bPP[pr][0], bPP[pr][1],
                                   brt[2 * mp], brt[2 * mp + 1]],
                            writes=[brt[2 * mp], brt[2 * mp + 1]])

            emit_gu(0)
            emit_combine_bcast()
            emit_fold(0)
            for e_i in range(1, NE):
                emit_gu(e_i)
                emit_fold(e_i)
                emit_y(e_i - 1)
            emit_y(NE - 1)
            spflat = spT[:].rearrange("p a b -> p (a b)")
            for half in range(2):
                sl, _ = next_unit("b_shw1", half)
                for mp in range(4):
                    pr = mp % 2
                    for m2 in range(2):
                        mm_block(pr, m2, sl, 2 * mp + m2, h1T, bh1)
                    P.op("scalar", (lambda hh, mp_, dd: lambda e: e.activation(
                        spflat[:, hh * 4096 + mp_ * 1024:
                               hh * 4096 + (mp_ + 1) * 1024],
                        dd, AF.Silu))(half, mp, PP[pr]),
                        reads=[bPP[pr][0], bPP[pr][1]],
                        writes=[bsp[half * 8 + 2 * mp], bsp[half * 8 + 2 * mp + 1]])
            for half in range(2):
                sl, _ = next_unit("b_shw3", half)
                for mp in range(4):
                    pr = mp % 2
                    for m2 in range(2):
                        mm_block(pr, m2, sl, 2 * mp + m2, h1T, bh1)
                    P.op("vector", (lambda hh, mp_, dd: lambda e: e.tensor_tensor(
                        spflat[:, hh * 4096 + mp_ * 1024:
                               hh * 4096 + (mp_ + 1) * 1024],
                        spflat[:, hh * 4096 + mp_ * 1024:
                               hh * 4096 + (mp_ + 1) * 1024],
                        dd, op=ALU.mult))(half, mp, PP[pr]),
                        reads=[bPP[pr][0], bPP[pr][1],
                               bsp[half * 8 + 2 * mp], bsp[half * 8 + 2 * mp + 1]],
                        writes=[bsp[half * 8 + 2 * mp], bsp[half * 8 + 2 * mp + 1]])
            slA, _ = next_unit("b_shw2", 0)
            slB, _ = next_unit("b_shw2", 1)
            for mp in range(4):
                pr = mp % 2
                for m2 in range(2):
                    m = 2 * mp + m2
                    for k16 in range(16):
                        slx = slA if k16 < 8 else slB
                        P.op("tensor",
                             (lambda kk, mm_, hh, ws, dd: lambda e: e.matmul(
                                 dd[:, ds(hh, 512)], wsl[ws][:, kk % 8, ds(mm_)],
                                 spT[:, kk, :], start=(kk == 0),
                                 stop=(kk == 15)))(k16, m, m2, slx, PP[pr]),
                             reads=[bwsl[slx], bsp[k16]],
                             writes=([bPP[pr][m2]] if k16 == 0 else []),
                             inc=(k16 == 15))
                for m2 in range(2):
                    m = 2 * mp + m2
                    P.op("vector", (lambda mm_, m2_, dd: lambda e:
                         e.scalar_tensor_tensor(
                        h0T[:, mm_, :], dd[:, ds(m2_, 512)], 1.0,
                        routed[:, mm_, :], op0=ALU.mult, op1=ALU.add))(
                            m, m2, PP[pr]),
                        reads=[bPP[pr][m2], brt[m]], writes=[bh0[m]])
            sl, _ = next_unit("b_lin1")
            for mp in range(4):
                pr = mp % 2
                for m2 in range(2):
                    mm_block(pr, m2, sl, 2 * mp + m2, h0T, bh0)
                for m2 in range(2):
                    m = 2 * mp + m2
                    P.op("vector", (lambda mm_, m2_, dd: lambda e: e.tensor_scalar(
                        h1T[:, mm_, :], dd[:, ds(m2_, 512)],
                        biases["lin1_b"][:, mm_:mm_ + 1], None, op0=ALU.add))(
                            m, m2, PP[pr]),
                        reads=[bPP[pr][m2], cb["lin1_b"]], writes=[bh1[m]])
            sl, _ = next_unit("b_s2w1")
            for mp in range(4):
                pr = mp % 2
                for m2 in range(2):
                    mm_block(pr, m2, sl, 2 * mp + m2, h1T, bh1)
                for m2 in range(2):
                    m = 2 * mp + m2
                    P.op("scalar", (lambda mm_, m2_, dd: lambda e: e.activation(
                        actT[0][:, mm_, :], dd[:, ds(m2_, 512)], AF.Silu,
                        bias=biases["s2b1"][:, mm_:mm_ + 1]))(m, m2, PP[pr]),
                        reads=[bPP[pr][m2], cb["s2b1"]], writes=[bact[0][m]])
            sl, _ = next_unit("b_s2w2")
            for mp in range(4):
                pr = mp % 2
                for m2 in range(2):
                    mm_block(pr, m2, sl, 2 * mp + m2, h1T, bh1)
                for m2 in range(2):
                    m = 2 * mp + m2
                    P.op("vector", (lambda mm_, m2_, dd: lambda e:
                         e.scalar_tensor_tensor(
                        routed[:, mm_, :], dd[:, ds(m2_, 512)],
                        biases["s2b2"][:, mm_:mm_ + 1], actT[0][:, mm_, :],
                        op0=ALU.add, op1=ALU.mult))(m, m2, PP[pr]),
                        reads=[bPP[pr][m2], cb["s2b2"], bact[0][m]],
                        writes=[brt[m]])
            for ts4 in range(4):
                for j in range(8):
                    P.op("tensor",
                         (lambda tt, jj: lambda e: e.transpose(
                             pS1[:, 1024 + jj * 128:1024 + (jj + 1) * 128],
                             routed[:, jj, ds(tt)], sbc["id32"][:]))(ts4, j),
                         reads=[brt[j], cb["id32"]],
                         writes=([bpT1] if j == 0 else []), inc=(j == 7))
                P.op("vector", (lambda tt, pp: lambda e: e.tensor_tensor(
                    x1tm[pp][:, tt, :], pS1[:, 1024:2048],
                    x1tm[pp][:, tt, :], op=ALU.add))(ts4, pc),
                    reads=[bpT1, bx1tm[pc]], writes=[bx1tm[pc]])

            def _s0(pc=pc):
                for t4 in range(4):
                    P.op("vector", (lambda tt, pp, l0: lambda e: e.reduce_sum(
                        l0[:], x1tm[pp][:, tt, :], axis=AX.X))(
                            t4, pc, lnB4[t4][0]),
                        reads=[bx1tm[pc]], writes=[blnB4[t4][0]])
                    P.op("vector", (lambda l0, l1: lambda e: e.tensor_scalar(
                        l1[:], l0[:], 1.0 / D, None, op0=ALU.mult))(
                            lnB4[t4][0], lnB4[t4][1]),
                        reads=[blnB4[t4][0]], writes=[blnB4[t4][1]])

            def _s1(pc=pc):
                for t4 in range(4):
                    P.op("vector", (lambda tt, pp, l1: lambda e: e.tensor_scalar(
                        x1tm[pp][:, tt, :], x1tm[pp][:, tt, :], l1[:], None,
                        op0=ALU.subtract))(t4, pc, lnB4[t4][1]),
                        reads=[blnB4[t4][1], bx1tm[pc]], writes=[bx1tm[pc]])

            def _s2(pc=pc):
                for t4 in range(4):
                    P.op("scalar", (lambda tt, pp, l2: lambda e: e.activation(
                        zsqB[:], x1tm[pp][:, tt, :], AF.Square,
                        accum_out=l2[:]))(t4, pc, lnB4[t4][2]),
                        reads=[bx1tm[pc]], writes=[bzsqB, blnB4[t4][2]])
                    P.op("scalar", (lambda l2, l3: lambda e: e.activation(
                        l3[:], l2[:], AF.Ln, scale=1.0 / D, bias=cEPS[:]))(
                            lnB4[t4][2], lnB4[t4][3]),
                        reads=[blnB4[t4][2], cb["eps"]], writes=[blnB4[t4][3]])
                    P.op("scalar", (lambda l3: lambda e: e.activation(
                        l3[:], l3[:], AF.Exp, scale=-0.5))(lnB4[t4][3]),
                        reads=[blnB4[t4][3]], writes=[blnB4[t4][3]])

            def _s3(c=c, pc=pc):
                for t4 in range(4):
                    P.op("vector", (lambda tt, pp, l3: lambda e: e.tensor_scalar(
                        x1tm[pp][:, tt, :], x1tm[pp][:, tt, :], l3[:], None,
                        op0=ALU.mult))(t4, pc, lnB4[t4][3]),
                        reads=[blnB4[t4][3], bx1tm[pc]], writes=[bx1tm[pc]])
                    P.dma("gpsimd", "dOUT",
                          out_d[c * 512 + t4 * 128:c * 512 + (t4 + 1) * 128, :],
                          x1tm[pc][:, t4, :], reads=[bx1tm[pc]],
                          writes=[gbuf("outd")])
            deferB.append([_s0, _s1, _s2, _s3])

        for stages in deferB:
            for f in stages:
                f()
        deferB.clear()

    fin = [Tok(s, P.count[s], None, P.gseq)
           for s in P.sems if s.startswith("d") and P.count[s]]
    P.op("gpsimd", lambda e: e.memset(neg1[0:1, :], -1.0),
         waits=fin, inc=False)

    with nc.Block() as block:
        P.emit(block)

    for cm in reversed(ctxs):
        cm.__exit__(None, None, None)
    return nc, P



def _host_prep(inputs):
    perm = np.zeros(D, dtype=np.int64)
    for h in range(NH):
        for j in range(32):
            perm[h * 64 + j] = h * 64 + 2 * j
            perm[h * 64 + 32 + j] = h * 64 + 2 * j + 1
    f16 = np.float16
    qwp = np.ascontiguousarray(np.asarray(inputs["L0_attn_qw"], f16)[:, perm])
    kwp = np.ascontiguousarray(np.asarray(inputs["L0_attn_kw"], f16)[:, perm])
    vw = np.ascontiguousarray(np.asarray(inputs["L0_attn_vw"], f16))
    ow = np.ascontiguousarray(np.asarray(inputs["L0_attn_ow"], f16))

    inv = (10000.0 ** (-(np.arange(0, HD, 2, dtype=np.float64)) / HD))
    pos = np.arange(SEQ, dtype=np.float64)
    freq = pos[:, None] * inv[None, :]

    def tab(a, scale):
        t = (a * scale).astype(np.float16)
        return np.ascontiguousarray(t.reshape(NT, 128, 32).transpose(1, 0, 2))
    cosq, sinq = tab(np.cos(freq), 1.0 / 32.0), tab(np.sin(freq), 1.0 / 32.0)
    cosk, sink = tab(np.cos(freq), 1.0), tab(np.sin(freq), 1.0)

    tri = np.zeros((128, 128), np.float16)
    for p in range(128):
        tri[p, p:] = 1.0
    dmm = np.zeros((128, 896), np.float16)
    dmm[:, 384:512] = tri
    dmm[:, 512:] = 1.0
    idb = np.eye(128).astype(ml_dtypes.bfloat16)
    idh = np.eye(128).astype(np.float16)
    id32 = np.eye(128).astype(np.float32)
    hsel = np.zeros((128, NE, NH), ml_dtypes.bfloat16)
    for pt in range(8):
        for p in range(128):
            hsel[p, pt, 2 * pt + p // 64] = 1.0

    f32 = np.float32
    return dict(
        qwp=qwp, kwp=kwp, vw=vw, ow=ow,
        cosq=cosq, sinq=sinq, cosk=cosk, sink=sink,
        tri=tri, dmm=dmm, idb=idb, idh=idh, id32=id32, hsel=hsel,
        lin0_w=np.asarray(inputs["L0_ffn_lin0_w"], f32),
        lin0_b=np.ascontiguousarray(np.asarray(inputs["L0_ffn_lin0_b"], f32).reshape(8, 128).T),
        s1w1=np.asarray(inputs["L0_ffn_swi1_w1"], f32),
        s1b1=np.ascontiguousarray(np.asarray(inputs["L0_ffn_swi1_b1"], f32).reshape(8, 128).T),
        s1w2=np.asarray(inputs["L0_ffn_swi1_w2"], f32),
        s1b2=np.ascontiguousarray(np.asarray(inputs["L0_ffn_swi1_b2"], f32).reshape(8, 128).T),
        gateT=np.ascontiguousarray(np.asarray(inputs["L0_ffn_gate_w"], f32).T),
        ew1=np.asarray(inputs["L0_ffn_exp_w1"], f32),
        ew3=np.asarray(inputs["L0_ffn_exp_w3"], f32),
        ew2=np.asarray(inputs["L0_ffn_exp_w2"], f32),
        shw1=np.asarray(inputs["L0_ffn_sh_w1"], f32),
        shw3=np.asarray(inputs["L0_ffn_sh_w3"], f32),
        shw2=np.asarray(inputs["L0_ffn_sh_w2"], f32),
        lin1_w=np.asarray(inputs["L0_ffn_lin1_w"], f32),
        lin1_b=np.ascontiguousarray(np.asarray(inputs["L0_ffn_lin1_b"], f32).reshape(8, 128).T),
        s2w1=np.asarray(inputs["L0_ffn_swi2_w1"], f32),
        s2b1=np.ascontiguousarray(np.asarray(inputs["L0_ffn_swi2_b1"], f32).reshape(8, 128).T),
        s2w2=np.asarray(inputs["L0_ffn_swi2_w2"], f32),
        s2b2=np.ascontiguousarray(np.asarray(inputs["L0_ffn_swi2_b2"], f32).reshape(8, 128).T),
    )


def kernel(**inputs):
    base = _host_prep(inputs)
    x = np.asarray(inputs["L0_attn_x"], np.float16)
    in_maps = []
    for ci in range(NCORES):
        m = dict(base)
        m["x"] = np.ascontiguousarray(x[ci * NSEQ:(ci + 1) * NSEQ])
        in_maps.append(m)
    nc, _ = build_nc("full")
    res = run_bass_kernel_spmd(nc, in_maps, core_ids=list(range(NCORES)))
    shards = [np.asarray(r["out"], np.float32).reshape(NSEQ, SEQ, D)
              for r in res.results]
    return np.concatenate(shards, axis=0)


if __name__ == "__main__":
    nc, P = build_nc("full")
    print("ops:", P.nops)
